# version 05
# baseline (speedup 1.0000x reference)
import numpy as np
import ml_dtypes
import concourse.bass as bass
import concourse.mybir as mybir
from concourse.bass_utils import run_bass_kernel_spmd

F32 = mybir.dt.float32
BF16 = mybir.dt.bfloat16
AF = mybir.ActivationFunctionType
ALU = mybir.AluOpType
AX = mybir.AxisListType
BFNP = ml_dtypes.bfloat16

NCORES = 8
DMODEL = 4096
CC = 32
TP = 1024
NP2 = 2
SEQ = 8192
POS_PER_CORE = 1024
LN_EPS = 1e-5
L0_SCALE = 1.0 / 32.0
L1_SCALE = 1.0 / (128.0 ** 0.5)

ENGINES = ("tensor", "scalar", "vector", "gpsimd", "sync")


class Op:
    __slots__ = ("eng", "fn", "deps", "is_dma", "dma_key", "consumed",
                 "event", "vc", "waits", "inc", "idx", "tag")

    def __init__(self):
        self.deps = ()
        self.consumed = False
        self.event = None
        self.vc = None
        self.waits = None
        self.inc = None


class Plan:
    def __init__(self, nc):
        self.nc = nc
        self.ops = []
        self.by_eng = {e: [] for e in ENGINES}
        self.res = {}
        self.n_sems = 0

    def emit(self, eng, fn, reads=(), writes=(), dma=None, tag=""):
        deps = set()
        for k in reads:
            e = self.res.get(k)
            if e is not None and e[0] is not None:
                deps.add(e[0])
        for k in writes:
            e = self.res.get(k)
            if e is not None:
                if e[0] is not None:
                    deps.add(e[0])
                deps.update(e[1])
        op = Op()
        op.eng = eng
        op.fn = fn
        op.is_dma = dma is not None
        op.dma_key = dma
        op.tag = tag
        op.idx = len(self.ops)
        kept = []
        for d in deps:
            if eng == "tensor" and d.eng == "tensor" and not d.is_dma and not op.is_dma:
                continue
            kept.append(d)
        op.deps = tuple(kept)
        self.ops.append(op)
        self.by_eng[eng].append(op)
        for k in reads:
            ent = self.res.get(k)
            if ent is None:
                self.res[k] = [None, [op]]
            else:
                ent[1].append(op)
        for k in writes:
            self.res[k] = [op, []]
        return op

    def build(self):
        nc = self.nc
        for op in self.ops:
            for d in op.deps:
                d.consumed = True
        eng_sem = {e: nc.alloc_semaphore(name=f"sq_{e}") for e in ENGINES}
        self.n_sems = len(ENGINES)
        dma_sems = {}
        fam_counts = {}
        fam_gen_pending = {}

        def get_dma_sem(semkey):
            ent = dma_sems.get(semkey)
            if ent is None:
                h = nc.alloc_semaphore(name=f"dm{len(dma_sems)}")
                self.n_sems += 1
                ent = dma_sems[semkey] = [h, 0]
            return ent

        cnt = {e: 0 for e in ENGINES}
        vc = {e: {} for e in ENGINES}

        for op in self.ops:
            myvc = vc[op.eng]
            needed = {}
            for d in op.deps:
                assert d.event is not None, f"dep without event: {d.tag}"
                s, v = d.event
                if d.is_dma and d.dma_key[1] is None:
                    semkey = ("F", d.dma_key[0])
                    v = dma_sems[semkey][1]
                if myvc.get(s, 0) < v and needed.get(s, 0) < v:
                    needed[s] = v
            for d in op.deps:
                for s2, v2 in d.vc.items():
                    if myvc.get(s2, 0) < v2:
                        myvc[s2] = v2
            for s, v in needed.items():
                if myvc.get(s, 0) < v:
                    myvc[s] = v
            if op.is_dma:
                fam, slot = op.dma_key[0], op.dma_key[1]
                gen = op.dma_key[2] if len(op.dma_key) > 2 else None
                if slot is None:
                    semkey = ("F", fam)
                else:
                    semkey = ("S", fam, slot)
                ent = get_dma_sem(semkey)
                h, prev = ent[0], ent[1]
                same_gen = gen is not None and len(ent) > 2 and ent[2] == gen
                if prev > 0 and not same_gen and myvc.get(id(h), 0) < prev:
                    if needed.get(id(h), 0) < prev:
                        needed[id(h)] = prev
                    myvc[id(h)] = prev
                ent[1] = prev + 16
                if len(ent) > 2:
                    ent[2] = gen
                else:
                    ent.append(gen)
                op.event = (id(h), ent[1])
                op.inc = (h, 16)
            elif op.consumed:
                cnt[op.eng] += 1
                op.event = (id(eng_sem[op.eng]), cnt[op.eng])
                op.inc = (eng_sem[op.eng], 1)
            else:
                op.event = (id(eng_sem[op.eng]), cnt[op.eng])
                op.inc = None
            op.waits = needed
            snap = dict(myvc)
            if op.inc is not None:
                snap[op.event[0]] = op.event[1]
            op.vc = snap

        id2h = {}
        for e in ENGINES:
            id2h[id(eng_sem[e])] = eng_sem[e]
        for ent in dma_sems.values():
            id2h[id(ent[0])] = ent[0]

        tail = [(eng_sem[e], cnt[e]) for e in ENGINES if cnt[e] > 0]
        tail += [(ent[0], ent[1]) for ent in dma_sems.values()]

        with nc.Block() as block:
            def mk(eng_ops, is_sync):
                def body(eng):
                    for op in eng_ops:
                        for s, v in sorted(op.waits.items(), key=lambda kv: kv[1]):
                            eng.wait_ge(id2h[s], v)
                        inst = op.fn(eng)
                        if op.inc is not None:
                            inst.then_inc(op.inc[0], op.inc[1])
                    if is_sync:
                        for h, v in tail:
                            eng.wait_ge(h, v)
                        for h, _ in tail:
                            eng.sem_clear(h)
                return body
            for e in ENGINES:
                if self.by_eng[e] or e == "sync":
                    getattr(block, e)(mk(self.by_eng[e], e == "sync"))
        return eng_sem, dma_sems


class Arena:
    QUANT = 512

    def __init__(self, nc, name, dtype, cols):
        self.name = name
        self.dtype = dtype
        self.cols = cols
        self.t = nc.alloc_sbuf_tensor(name, [128, cols], dtype).ap()
        self.free_list = [(0, cols)]
        self.peak = 0
        self.used = 0

    def alloc(self, cols, tag=""):
        cols = ((cols + self.QUANT - 1) // self.QUANT) * self.QUANT
        for i, (off, size) in enumerate(self.free_list):
            if size >= cols:
                if size == cols:
                    self.free_list.pop(i)
                else:
                    self.free_list[i] = (off + cols, size - cols)
                self.used += cols
                self.peak = max(self.peak, self.used)
                return Tile(self, off, cols)
        raise MemoryError(f"arena {self.name} out of space for {cols} ({tag}); used={self.used}/{self.cols}")

    def free(self, tile):
        self.used -= tile.cols
        self.free_list.append((tile.off, tile.cols))
        self.free_list.sort()
        merged = []
        for off, size in self.free_list:
            if merged and merged[-1][0] + merged[-1][1] == off:
                merged[-1] = (merged[-1][0], merged[-1][1] + size)
            else:
                merged.append((off, size))
        self.free_list = [tuple(x) for x in merged]


class Tile:
    __slots__ = ("arena", "off", "cols")

    def __init__(self, arena, off, cols):
        self.arena = arena
        self.off = off
        self.cols = cols

    def ap(self, c0=0, c1=None, p0=0, p1=128):
        c1 = self.cols if c1 is None else c1
        return self.arena.t[p0:p1, self.off + c0:self.off + c1]

    def keys(self, c0=0, c1=None):
        c1 = self.cols if c1 is None else c1
        q = Arena.QUANT
        lo = (self.off + c0) // q
        hi = (self.off + c1 + q - 1) // q
        return [(self.arena.name, i) for i in range(lo, hi)]

    def free(self):
        self.arena.free(self)


def _ceil(a, b):
    return (a + b - 1) // b


class PsumPool:

    def __init__(self, nc):
        self.f32 = [(f"ps{i}", nc.alloc_psum_tensor(f"ps{i}", [128, 512], F32).ap())
                    for i in range(8)]
        self.bf16 = [(k, ap.bitcast(BF16)) for k, ap in self.f32[6:8]]
        self.i_f = 0
        self.i_b = 0

    def f(self):
        k, ap = self.f32[self.i_f % 6]
        self.i_f += 1
        return k, ap

    def fgroup(self, n):
        assert n <= 6
        return [self.f() for _ in range(n)]

    def r(self):
        k, ap = self.f32[6 + self.i_b % 2]
        self.i_b += 1
        return k, ap

    def b(self):
        k, ap = self.bf16[self.i_b % 2]
        self.i_b += 1
        return k, ap


class WStream:
    def __init__(self, mk, name, w_ap):
        self.mk = mk
        self.name = name
        self.w = w_ap
        self.K = w_ap.shape[0]
        self.F = w_ap.shape[1]

    def stream_cols(self, c0, c1):
        width = c1 - c0
        mk = self.mk
        ics_per_piece = max(1, mk.wring_cols // width)
        n_ic = self.K // 128
        pieces = {}
        emitted = {}

        def ensure(ic0):
            if ic0 in emitted:
                return
            emitted[ic0] = True
            ic1 = min(ic0 + ics_per_piece, n_ic)
            sl = mk.wring_slot % len(mk.wring)
            t = mk.wring[sl]
            mk.wring_slot += 1
            nic = ic1 - ic0
            src = self.w[ic0 * 128:ic1 * 128, c0:c1].rearrange(
                "(ic p) f -> p ic f", p=128)
            dst = t.ap(0, nic * width).rearrange("p (ic f) -> p ic f", f=width)
            mk.plan.emit(
                "gpsimd",
                (lambda e, dst=dst, src=src: e.dma_start(out=dst, in_=src)),
                reads=[], writes=t.keys(0, nic * width),
                dma=("wg", sl),
                tag=f"{self.name}[{ic0}:{ic1},{c0}:{c1}]")
            for ic in range(ic0, ic1):
                pieces[ic] = (t, ic - ic0, width)

        def acc(ic, cc0, cc1):
            ensure((ic // ics_per_piece) * ics_per_piece)
            t, icl, w = pieces[ic]
            a = t.ap(icl * w + cc0, icl * w + cc1)
            return a, t.keys(icl * w + cc0, icl * w + cc1)
        return acc


class MK:
    def __init__(self, part="all", taps=()):
        self.part = part
        self.taps = set(taps)
        nc = self.nc = bass.Bass("TRN2", target_bir_lowering=False)
        self.plan = Plan(nc)
        self.bf = Arena(nc, "abf", BF16, 76 * 1024)
        self.f32a = Arena(nc, "af32", F32, 13 * 1024 + 512)
        self.ps = PsumPool(nc)
        self.wring_cols = 7168
        self.wring = [self.bf.alloc(self.wring_cols, f"wring{i}") for i in range(3)]
        self.wring_slot = 0
        self.tapped = {}

        d = self.d = {}
        ein, eout, internal = {}, {}, {}

        def dram(name, shape, dtype, kind):
            d[name] = nc.dram_tensor(name, shape, dtype, kind=kind).ap()

        s1 = part in ("s1", "all")
        rest = part in ("rest", "all")
        if s1:
            dram("x0", [2048, DMODEL], F32, "ExternalInput")
            dram("w_l0qkv", [4096, 3072], F32, "ExternalInput")
            dram("b_l0qkv", [1, 3072], F32, "ExternalInput")
        if rest:
            dram("w_l0o", [1024, 4096], F32, "ExternalInput")
            dram("b_l0o", [1, 4096], F32, "ExternalInput")
            dram("w_l0base_t", [4096, 4096], F32, "ExternalInput")
            dram("b_l0base", [1, 4096], F32, "ExternalInput")
            dram("a_t", [4096, 64], F32, "ExternalInput")
            dram("b_flat", [1024, 4096], F32, "ExternalInput")
            dram("r_t", [4096, 16], F32, "ExternalInput")
            dram("w_l1qkv_t", [4096, 6144], F32, "ExternalInput")
            dram("w_l1proj_t", [4096, 4096], F32, "ExternalInput")
            dram("gate_w", [4096, 8], F32, "ExternalInput")
            dram("up_w", [4096, 8192], F32, "ExternalInput")
            dram("down_w", [8192, 4096], F32, "ExternalInput")
            dram("fc1_w", [4096, 2048], F32, "ExternalInput")
            dram("fc2_w", [2048, 4096], F32, "ExternalInput")
            dram("expert_w", [2048, 2048], F32, "ExternalInput")
            dram("ln_g", [1, 4096], F32, "ExternalInput")
            dram("ln_b", [1, 4096], F32, "ExternalInput")
            dram("out", [2048, DMODEL], F32, "ExternalOutput")
        if part == "s1":
            dram("kin", [4096, 512], BF16, "ExternalOutput")
            dram("vin", [2048, 1024], BF16, "ExternalOutput")
            dram("qdr", [2048, 1024], BF16, "ExternalOutput")
            dram("x0fm", [DMODEL, 2048], BF16, "ExternalOutput")
        elif part == "rest":
            dram("kout0", [16384, 512], BF16, "ExternalInput")
            dram("kout1", [16384, 512], BF16, "ExternalInput")
            dram("vout0", [8192, 1024], BF16, "ExternalInput")
            dram("vout1", [8192, 1024], BF16, "ExternalInput")
            dram("qdr", [2048, 1024], BF16, "ExternalInput")
            dram("x0fm", [DMODEL, 2048], BF16, "ExternalInput")
        else:
            dram("kin", [4096, 512], BF16, "Internal")
            dram("vin", [2048, 1024], BF16, "Internal")
            dram("kout0", [16384, 512], BF16, "Internal")
            dram("kout1", [16384, 512], BF16, "Internal")
            dram("vout0", [8192, 1024], BF16, "Internal")
            dram("vout1", [8192, 1024], BF16, "Internal")
            dram("qdr", [2048, 1024], BF16, "Internal")
            dram("x0fm", [DMODEL, 2048], BF16, "Internal")
        names = [("x1fm", [DMODEL, 2048]), ("x2fm", [DMODEL, 2048]),
                 ("x3fm", [DMODEL, 2048])]
        for p in range(2):
            names += [(f"prefm{p}", [4096, TP]), (f"whtm{p}", [1024, 1024]),
                      (f"hdr{p}", [8192, TP]), (f"l1q{p}", [6144, TP])]
        for name, shape in names:
            if rest:
                dram(name, shape, BF16, "Internal")
        for t in self.taps:
            dram("tap_" + t, list(d[t].shape), d[t].dtype, "ExternalOutput")

        self._init_consts(s1, rest)

    def E(self, eng, fn, reads=(), writes=(), dma=None, tag=""):
        return self.plan.emit(eng, fn, reads, writes, dma, tag)

    def tap(self, name, keys):
        if name in self.taps:
            src, dst = self.d[name], self.d["tap_" + name]
            self.E("sync", (lambda e, s=src, t=dst: e.dma_start(out=t, in_=s)),
                   reads=keys, writes=[("tap", name)], dma=("tap", None), tag=f"tap_{name}")

    def _init_consts(self, s1, rest):
        nc, d = self.nc, self.d
        self.id_bf = self.bf.alloc(512, "id_bf")
        self.id_f32 = self.f32a.alloc(512, "id_f32")
        for t, dt in ((self.id_bf, "idb"), (self.id_f32, "idf")):
            ap = t.ap(0, 128)
            self.E("gpsimd", (lambda e, a=ap: e.memset(a, 0.0)), writes=t.keys(0, 128), tag="id0")
            self.E("gpsimd",
                   (lambda e, a=ap: e.affine_select(
                       out=a, in_=a, compare_op=ALU.not_equal, fill=1.0,
                       base=0, pattern=[[-1, 128]], channel_multiplier=1)),
                   reads=t.keys(0, 128), writes=t.keys(0, 128), tag="id1")
        self.ones_row = self.bf.alloc(512, "ones_row")
        self.E("gpsimd", (lambda e, a=self.ones_row.ap(0, 128, 0, 1): e.memset(a, 1.0)),
               writes=self.ones_row.keys(0, 128), tag="ones_r")
        self.ones_col = self.bf.alloc(512, "ones_col")
        self.E("gpsimd", (lambda e, a=self.ones_col.ap(0, 1): e.memset(a, 1.0)),
               writes=self.ones_col.keys(0, 1), tag="ones_c")
        self.ones_col_f = self.f32a.alloc(512, "ones_col_f")
        self.E("gpsimd", (lambda e, a=self.ones_col_f.ap(0, 1): e.memset(a, 1.0)),
               writes=self.ones_col_f.keys(0, 1), tag="ones_cf")
        self.ones_row_f = self.f32a.alloc(512, "ones_row_f")
        self.E("gpsimd", (lambda e, a=self.ones_row_f.ap(0, 128, 0, 1): e.memset(a, 1.0)),
               writes=self.ones_row_f.keys(0, 128), tag="ones_rf")
        self.epsc = self.f32a.alloc(512, "epsc")
        self.E("gpsimd", (lambda e, a=self.epsc.ap(0, 1): e.memset(a, LN_EPS)),
               writes=self.epsc.keys(0, 1), tag="epsc")

        if s1:
            self.bqkv = self.f32a.alloc(512, "bqkv")
            src = d["b_l0qkv"].rearrange("o (c p) -> p (o c)", p=128)
            self.E("gpsimd", (lambda e, a=self.bqkv.ap(0, 24), s=src:
                              e.dma_start(out=a, in_=s, allow_slow_non_contiguous=True)),
                   writes=self.bqkv.keys(0, 24), dma=("cst", None), tag="ld_bqkv")
            self.bq_s = self.f32a.alloc(512, "bq_s")
            self.E("vector", (lambda e: e.tensor_scalar_mul(self.bq_s.ap(0, 8), self.bqkv.ap(0, 8), L0_SCALE)),
                   reads=self.bqkv.keys(0, 24), writes=self.bq_s.keys(0, 8), tag="bq_s")
            vrow = self.f32a.alloc(1024, "vrow")
            self.E("gpsimd", (lambda e, a=vrow.ap(0, 1024, 0, 1), s=d["b_l0qkv"][0:1, 2048:3072]:
                              e.dma_start(out=a, in_=s)),
                   writes=vrow.keys(), dma=("cst", None), tag="ld_vrow")
            vrow_bf = self.bf.alloc(1024, "vrow_bf")
            self.E("vector", (lambda e: e.tensor_copy(vrow_bf.ap(0, 1024, 0, 1), vrow.ap(0, 1024, 0, 1))),
                   reads=vrow.keys(), writes=vrow_bf.keys(), tag="vrow_bf")
            self.vbias = self.bf.alloc(1024, "vbias")
            for h in range(2):
                pk, pap = self.ps.f()
                self.E("tensor",
                       (lambda e, p=pap, r=vrow_bf.ap(h * 512, (h + 1) * 512, 0, 1):
                        e.matmul(p, self.ones_row.ap(0, 128, 0, 1), r, start=True, stop=True)),
                       reads=self.ones_row.keys(0, 128) + vrow_bf.keys(), writes=[pk], tag="vb_mm")
                self.E("scalar", (lambda e, p=pap, o=self.vbias.ap(h * 512, (h + 1) * 512): e.copy(o, p)),
                       reads=[pk], writes=self.vbias.keys(h * 512, (h + 1) * 512), tag="vb_ev")
            vrow.free()
            vrow_bf.free()
        if rest:
            self.bo_t = self.f32a.alloc(512, "bo_t")
            self.bb_t = self.f32a.alloc(512, "bb_t")
            self.gam_t = self.f32a.alloc(512, "gam_t")
            self.bet_t = self.f32a.alloc(512, "bet_t")
            for name, tile in (("b_l0o", self.bo_t), ("b_l0base", self.bb_t),
                               ("ln_g", self.gam_t), ("ln_b", self.bet_t)):
                src = d[name].rearrange("o (c p) -> p (o c)", p=128)
                self.E("gpsimd", (lambda e, a=tile.ap(0, 32), s=src:
                                  e.dma_start(out=a, in_=s, allow_slow_non_contiguous=True)),
                       writes=tile.keys(0, 32), dma=("cst", None), tag=f"ld_{name}")


    _st_ctr = 0

    def st(self):
        MK._st_ctr += 1
        return ("st", MK._st_ctr % 8)

    _wg_ctr = 0

    def wg(self):
        MK._wg_ctr += 1
        return ("wg", MK._wg_ctr % 6)

    def xtm_keys(self, name, p):
        return [(name, p, tt) for tt in range(8)]

    def s1(self, p, ws_qkv):
        d, E, ps = self.d, self.E, self.ps
        xin = [self.bf.alloc(4096, "xin") for _ in range(3)]
        xfm = [self.bf.alloc(1024, f"xfm{cc}") for cc in range(32)]
        for tt in range(8):
            b, q0 = tt // 4, (tt % 4) * 128
            t = xin[tt % 3]
            r0 = b * 1024 + p * 512 + q0
            src = d["x0"][r0:r0 + 128, :]
            E("gpsimd", (lambda e, a=t.ap(), s=src: e.dma_start(out=a, in_=s)),
              writes=t.keys(), dma=("xin", tt % 3), tag=f"xin{p}{tt}")
            for ccg in range(8):
                pkT, papT = ps.b()
                for j in range(4):
                    cc = ccg * 4 + j
                    E("tensor", (lambda e, o=papT[:, j * 128:(j + 1) * 128],
                                 i=t.ap(cc * 128, (cc + 1) * 128),
                                 idn=self.id_bf.ap(0, 128): e.transpose(o, i, idn)),
                      reads=t.keys(cc * 128, (cc + 1) * 128) + self.id_bf.keys(0, 128),
                      writes=[pkT], tag=f"x0t{tt}{ccg}")
                for j in range(4):
                    cc = ccg * 4 + j
                    dst = xfm[cc].ap(tt * 128, (tt + 1) * 128)
                    E("scalar", (lambda e, o=dst, i=papT[:, j * 128:(j + 1) * 128]: e.copy(o, i)),
                      reads=[pkT], writes=xfm[cc].keys(tt * 128, (tt + 1) * 128),
                      tag=f"x0te{tt}{ccg}")
        for cc in range(32):
            dst = d["x0fm"][cc * 128:(cc + 1) * 128, p * 1024:(p + 1) * 1024]
            E("sync", (lambda e, a=xfm[cc].ap(), o=dst: e.dma_start(out=o, in_=a)),
              reads=xfm[cc].keys(), writes=[("x0fm", p, cc)], dma=self.st(), tag=f"x0st{cc}")
        qkst = [self.bf.alloc(512, "qkst") for _ in range(4)]
        ogs = [(i, min(i + 3, 16)) for i in range(0, 16, 3)]
        for (lo, hi) in ogs:
            accw = ws_qkv.stream_cols(lo * 128, hi * 128)
            banks = ps.fgroup((hi - lo) * 2)
            for cc in range(32):
                wap, wkeys = None, None
                for j, dc in enumerate(range(lo, hi)):
                    lap, lkeys = accw(cc, j * 128, (j + 1) * 128)
                    for th in range(2):
                        pk, pap = banks[j * 2 + th]
                        E("tensor",
                          (lambda e, o=pap, l=lap, r=xfm[cc].ap(th * 512, (th + 1) * 512),
                           s=(cc == 0), t=(cc == 31): e.matmul(o, l, r, start=s, stop=t)),
                          reads=lkeys + xfm[cc].keys(th * 512, (th + 1) * 512),
                          writes=[pk], tag=f"qk{dc}{th}")
            for j, dc in enumerate(range(lo, hi)):
                for th in range(2):
                    pk, pap = banks[j * 2 + th]
                    stg = qkst[(dc * 2 + th) % 4]
                    if dc < 8:
                        scale, bias_ap, bias_keys = L0_SCALE, self.bq_s.ap(dc, dc + 1), self.bq_s.keys(0, 8)
                    else:
                        scale, bias_ap, bias_keys = 1.0, self.bqkv.ap(dc, dc + 1), self.bqkv.keys(0, 24)
                    E("scalar",
                      (lambda e, o=stg.ap(0, 512), i=pap, b=bias_ap, s=scale:
                       e.activation(o, i, AF.Identity, bias=b, scale=s)),
                      reads=[pk] + bias_keys, writes=stg.keys(0, 512), tag=f"qkev{dc}{th}")
                    if dc < 8:
                        r0 = p * 1024 + dc * 128
                        dst = d["qdr"][r0:r0 + 128, th * 512:(th + 1) * 512]
                        wkey = ("qdr", p, dc, th)
                    else:
                        r0 = p * 2048 + th * 1024 + (dc - 8) * 128
                        dst = d["kin"][r0:r0 + 128, :]
                        wkey = ("kin", p, th, dc - 8)
                    E("sync", (lambda e, a=stg.ap(0, 512), o=dst: e.dma_start(out=o, in_=a)),
                      reads=stg.keys(0, 512), writes=[wkey], dma=self.st(), tag=f"qkst{dc}{th}")
        for dvh in range(2):
            accw = ws_qkv.stream_cols(2048 + dvh * 512, 2048 + (dvh + 1) * 512)
            for tg in ((0, 6), (6, 8)):
                banks = ps.fgroup(tg[1] - tg[0])
                for cc in range(32):
                    rap, rkeys = accw(cc, 0, 512)
                    for j, tt in enumerate(range(*tg)):
                        pk, pap = banks[j]
                        E("tensor",
                          (lambda e, o=pap, l=xfm[cc].ap(tt * 128, (tt + 1) * 128), r=rap,
                           s=(cc == 0), t=(cc == 31): e.matmul(o, l, r, start=s, stop=t)),
                          reads=rkeys + xfm[cc].keys(tt * 128, (tt + 1) * 128),
                          writes=[pk], tag=f"v{tt}{dvh}")
                for j, tt in enumerate(range(*tg)):
                    pk, pap = banks[j]
                    stg = qkst[tt % 4]
                    E("vector",
                      (lambda e, o=stg.ap(0, 512), i=pap, b=self.vbias.ap(dvh * 512, (dvh + 1) * 512):
                       e.tensor_add(o, i, b)),
                      reads=[pk] + self.vbias.keys(), writes=stg.keys(0, 512), tag=f"vev{tt}")
                    r0 = p * 1024 + tt * 128
                    dst = d["vin"][r0:r0 + 128, dvh * 512:(dvh + 1) * 512]
                    E("sync", (lambda e, a=stg.ap(0, 512), o=dst: e.dma_start(out=o, in_=a)),
                      reads=stg.keys(0, 512), writes=[("vin", p, tt, dvh)], dma=self.st(),
                      tag=f"vst{tt}")
        for t in xin + xfm + qkst:
            t.free()

    def allgather(self, p):
        d, E = self.d, self.E
        kin_keys = [("kin", p, b, dk) for b in range(2) for dk in range(8)]
        vin_keys = [("vin", p, tt, dvh) for tt in range(8) for dvh in range(2)]
        kin = d["kin"][p * 2048:(p + 1) * 2048, :]
        vin = d["vin"][p * 1024:(p + 1) * 1024, :]
        kout, vout = d[f"kout{p}"], d[f"vout{p}"]

        def fk(e):
            return e.collective_compute(
                "AllGather", ALU.bypass, replica_groups=[list(range(NCORES))],
                ins=[kin.opt()], outs=[kout.opt()])

        def fv(e):
            return e.collective_compute(
                "AllGather", ALU.bypass, replica_groups=[list(range(NCORES))],
                ins=[vin.opt()], outs=[vout.opt()])
        self.plan.emit("gpsimd", fk, reads=kin_keys, writes=[("kout", p)], tag=f"ag_k{p}")
        self.plan.emit("gpsimd", fv, reads=vin_keys, writes=[("vout", p)], tag=f"ag_v{p}")

    def flash(self, p):
        d, E, ps = self.d, self.E, self.ps
        qf = [self.bf.alloc(1024, f"qf{dc}") for dc in range(8)]
        for dc in range(8):
            r0 = p * 1024 + dc * 128
            E("sync", (lambda e, a=qf[dc].ap(), s=d["qdr"][r0:r0 + 128, :]: e.dma_start(out=a, in_=s)),
              reads=[("qdr", p, dc, th) for th in range(2)], writes=qf[dc].keys(),
              dma=("qf", None), tag=f"qf{dc}")
        attn = [self.bf.alloc(1024, f"attn{dc}") for dc in range(8)]
        kvk = [self.bf.alloc(4096, "kvk") for _ in range(2)]
        kvv = [self.bf.alloc(4096, "kvv") for _ in range(2)]
        pbuf = [self.bf.alloc(512, "pbuf") for _ in range(4)]
        pt4 = [self.bf.alloc(512, "pt4") for _ in range(2)]
        acc = [self.f32a.alloc(1024, f"acc{qt}") for qt in range(4)]
        ml = self.f32a.alloc(512, "ml")
        mlk = ml.keys()

        def mlap(c):
            return ml.ap(c, c + 1)
        for b in range(2):
            for kc in range(16):
                pp, r = kc // 8, kc % 8
                sl = kc % 2
                r0 = r * 2048 + b * 1024
                src = d[f"kout{pp}"][r0:r0 + 1024, :].rearrange("(dc p) k -> p dc k", p=128)
                E("sync", (lambda e, a=kvk[sl].ap(), s=src: e.dma_start(out=a, in_=s)),
                  reads=[("kout", pp)], writes=kvk[sl].keys(), dma=("kvk", sl), tag=f"kld{b}{kc}")
                r0 = r * 1024 + b * 512
                src = d[f"vout{pp}"][r0:r0 + 512, :].rearrange("(s p) v -> p s v", p=128)
                E("sync", (lambda e, a=kvv[sl].ap(), s=src: e.dma_start(out=a, in_=s)),
                  reads=[("vout", pp)], writes=kvv[sl].keys(), dma=("kvv", sl), tag=f"vld{b}{kc}")
                sbanks = []
                for qt in range(4):
                    pkS, papS = ps.f()
                    sbanks.append((pkS, papS))
                    for dc in range(8):
                        E("tensor",
                          (lambda e, o=papS, l=qf[dc].ap(b * 512 + qt * 128, b * 512 + (qt + 1) * 128),
                           r_=kvk[sl].ap(dc * 512, (dc + 1) * 512), s=(dc == 0), t=(dc == 7):
                           e.matmul(o, l, r_, start=s, stop=t)),
                          reads=qf[dc].keys() + kvk[sl].keys(dc * 512, (dc + 1) * 512),
                          writes=[pkS], tag=f"sc{b}{kc}{qt}")
                for qt in range(4):
                    pkS, papS = sbanks[qt]
                    par = kc % 2
                    mq, mq_old = mlap(qt * 2 + par), mlap(qt * 2 + 1 - par)
                    lq, lq_old = mlap(8 + qt * 2 + par), mlap(8 + qt * 2 + 1 - par)
                    nm, rs, al = mlap(16 + qt), mlap(20 + qt), mlap(24 + qt)
                    P = pbuf[qt]
                    if kc == 0:
                        E("vector", (lambda e, o=mq, i=papS: e.reduce_max(o, i, axis=AX.X)),
                          reads=[pkS], writes=mlk, tag="fm0")
                        E("vector", (lambda e, o=nm, i=mq: e.tensor_scalar_mul(o, i, -1.0)),
                          reads=mlk, writes=mlk, tag="fm1")
                        E("scalar", (lambda e, o=P.ap(0, 512), i=papS, bi=nm, ac=lq:
                                     e.activation(o, i, AF.Exp, bias=bi, accum_out=ac)),
                          reads=[pkS] + mlk, writes=P.keys() + mlk, tag="fexp0")
                    else:
                        cm = mlap(28 + qt)
                        E("vector", (lambda e, o=cm, i=papS: e.reduce_max(o, i, axis=AX.X)),
                          reads=[pkS], writes=mlk, tag="fcm")
                        E("vector", (lambda e, o=mq, a=mq_old, b_=cm: e.tensor_max(o, a, b_)),
                          reads=mlk, writes=mlk, tag="fmx")
                        E("vector", (lambda e, o=nm, i=mq: e.tensor_scalar_mul(o, i, -1.0)),
                          reads=mlk, writes=mlk, tag="fnm")
                        E("scalar", (lambda e, o=al, i=mq_old, bi=nm: e.activation(o, i, AF.Exp, bias=bi)),
                          reads=mlk, writes=mlk, tag="fal")
                        E("scalar", (lambda e, o=P.ap(0, 512), i=papS, bi=nm, ac=rs:
                                     e.activation(o, i, AF.Exp, bias=bi, accum_out=ac)),
                          reads=[pkS] + mlk, writes=P.keys() + mlk, tag="fexp")
                        E("vector", (lambda e, o=lq, i=lq_old, s=al, t=rs:
                                     e.scalar_tensor_tensor(o, i, s, t, ALU.mult, ALU.add)),
                          reads=mlk, writes=mlk, tag="fl")
                ptdone = {}

                def emit_pt(qt):
                    P = pbuf[qt]
                    pkT, papT = ps.b()
                    for ks in range(4):
                        E("tensor", (lambda e, o=papT[:, ks * 128:(ks + 1) * 128],
                                     i=P.ap(ks * 128, (ks + 1) * 128),
                                     idn=self.id_bf.ap(0, 128): e.transpose(o, i, idn)),
                          reads=P.keys() + self.id_bf.keys(0, 128), writes=[pkT], tag="fpt")
                    pt = pt4[qt % 2]
                    E("scalar", (lambda e, o=pt.ap(0, 512), i=papT[:, 0:512]: e.copy(o, i)),
                      reads=[pkT], writes=pt.keys(0, 512), tag="fptev")
                    ptdone[qt] = pt

                def emit_av(qt):
                    par = kc % 2
                    al = mlap(24 + qt)
                    pt = ptdone[qt]
                    for dvh in range(2):
                        pkA, papA = ps.f()
                        for ks in range(4):
                            E("tensor",
                              (lambda e, o=papA, l=pt.ap(ks * 128, (ks + 1) * 128),
                               r_=kvv[sl].ap(ks * 1024 + dvh * 512, ks * 1024 + (dvh + 1) * 512),
                               s=(ks == 0), t=(ks == 3): e.matmul(o, l, r_, start=s, stop=t)),
                              reads=pt.keys(ks * 128, (ks + 1) * 128) +
                              kvv[sl].keys(ks * 1024 + dvh * 512, ks * 1024 + (dvh + 1) * 512),
                              writes=[pkA], tag=f"av{b}{kc}{qt}{dvh}")
                        a_ap = acc[qt].ap(dvh * 512, (dvh + 1) * 512)
                        a_keys = acc[qt].keys(dvh * 512, (dvh + 1) * 512)
                        if kc == 0:
                            E("vector", (lambda e, o=a_ap, i=papA: e.tensor_copy(o, i)),
                              reads=[pkA], writes=a_keys, tag="facc0")
                        else:
                            E("vector", (lambda e, o=a_ap, i=papA, s=al:
                                         e.scalar_tensor_tensor(o, o, s, i, ALU.mult, ALU.add)),
                              reads=[pkA] + a_keys + mlk, writes=a_keys, tag="facc")
                emit_pt(0)
                emit_pt(1)
                emit_av(0)
                emit_pt(2)
                emit_av(1)
                emit_pt(3)
                emit_av(2)
                emit_av(3)
            for qt in range(4):
                lq = mlap(8 + qt * 2 + 1)
                rl = mlap(28 + qt)
                E("vector", (lambda e, o=rl, i=lq: e.reciprocal(o, i)), reads=mlk, writes=mlk, tag="frl")
                ao2 = self.bf.alloc(1024, "ao2")
                E("vector", (lambda e, o=ao2.ap(), i=acc[qt].ap(), s=rl:
                             e.tensor_scalar_mul(o, i, s)),
                  reads=acc[qt].keys() + mlk, writes=ao2.keys(), tag="fnorm")
                for half in range(2):
                    pkT, papT = ps.b()
                    for dc2 in range(4):
                        dc = half * 4 + dc2
                        E("tensor", (lambda e, o=papT[:, dc2 * 128:(dc2 + 1) * 128],
                                     i=ao2.ap(dc * 128, (dc + 1) * 128),
                                     idn=self.id_bf.ap(0, 128): e.transpose(o, i, idn)),
                          reads=ao2.keys() + self.id_bf.keys(0, 128), writes=[pkT], tag="faot")
                    for dc2 in range(4):
                        dc = half * 4 + dc2
                        dst = attn[dc].ap(b * 512 + qt * 128, b * 512 + (qt + 1) * 128)
                        E("scalar", (lambda e, o=dst, i=papT[:, dc2 * 128:(dc2 + 1) * 128]: e.copy(o, i)),
                          reads=[pkT], writes=attn[dc].keys(b * 512 + qt * 128, b * 512 + (qt + 1) * 128),
                          tag="faotev")
                ao2.free()
        for t in qf + kvk + kvv + pbuf + pt4 + acc + [ml]:
            t.free()
        return attn

    def _ln_stats(self, t, stat, sk, rstd_col, negmr_col):
        E = self.E
        for s8 in range(8):
            E("vector", (lambda e, o=stat.ap(s8 * 6, s8 * 6 + 6), i=t.ap(s8 * 512, (s8 + 1) * 512):
                         e.bn_stats(o, i)),
              reads=t.keys(s8 * 512, (s8 + 1) * 512), writes=sk, tag="bns")
        E("vector", (lambda e, o=stat.ap(48, 50), i=stat.ap(0, 48): e.bn_aggr(o, i)),
          reads=sk, writes=sk, tag="bna")
        E("scalar", (lambda e, o=stat.ap(rstd_col, rstd_col + 1), i=stat.ap(49, 50),
                     b=self.epsc.ap(0, 1): e.activation(o, i, AF.Ln, bias=b)),
          reads=sk + self.epsc.keys(0, 1), writes=sk, tag="lnv")
        E("scalar", (lambda e, o=stat.ap(rstd_col, rstd_col + 1), i=stat.ap(rstd_col, rstd_col + 1):
                     e.activation(o, i, AF.Exp, scale=-0.5)),
          reads=sk, writes=sk, tag="rstd")
        E("vector", (lambda e, o=stat.ap(negmr_col, negmr_col + 1), i=stat.ap(48, 49),
                     r=stat.ap(rstd_col, rstd_col + 1):
                     e.tensor_scalar(o, i, r, -1.0, ALU.mult, ALU.mult)),
          reads=sk, writes=sk, tag="negmr")

    def ln_begin(self, tag):
        st = {}
        st["srow"] = self.f32a.alloc(2048, f"lns_{tag}")
        st["sq"] = [self.bf.alloc(512, "lnsq") for _ in range(2)]
        st["rbf"] = self.f32a.alloc(1024, f"lnrb_{tag}")
        st["nbf"] = self.f32a.alloc(1024, f"lnnb_{tag}")
        st["rowbf"] = self.bf.alloc(512, "lnrow")
        self.E("vector", (lambda e, a=st["srow"].ap(0, 2048, 0, 1): e.memset(a, 0.0)),
               writes=st["srow"].keys(), tag=f"lnz_{tag}")
        return st

    def ln_chunk(self, st, pr_ap, pr_keys, th):
        E, ps = self.E, self.ps
        sqt = st["sq"][th]
        E("vector", (lambda e, o=sqt.ap(0, 512), a=pr_ap: e.tensor_mul(o, a, a)),
          reads=pr_keys, writes=sqt.keys(), tag="lnsq")
        for rhs_ap, rhs_keys, col0 in ((pr_ap, pr_keys, th * 512),
                                       (sqt.ap(0, 512), sqt.keys(), 1024 + th * 512)):
            pk, pap = ps.r()
            E("tensor", (lambda e, o=pap[0:1, :], l=self.ones_col.ap(0, 1), r=rhs_ap:
                         e.matmul(o, l, r, start=True, stop=True)),
              reads=self.ones_col.keys(0, 1) + rhs_keys, writes=[pk], tag="lnmm")
            E("vector", (lambda e, o=st["srow"].ap(col0, col0 + 512, 0, 1),
                         a=st["srow"].ap(col0, col0 + 512, 0, 1), b=pap[0:1, :]:
                         e.tensor_add(o, a, b)),
              reads=[pk] + st["srow"].keys(), writes=st["srow"].keys(), tag="lnacc")

    def ln_finalize(self, st):
        E, ps = self.E, self.ps
        srow, rowbf = st["srow"], st["rowbf"]
        sk = srow.keys()
        E("vector", (lambda e, o=srow.ap(0, 1024, 0, 1), i=srow.ap(0, 1024, 0, 1):
                     e.tensor_scalar_mul(o, i, 1.0 / 4096.0)),
          reads=sk, writes=sk, tag="lnmean")
        E("vector", (lambda e, o=srow.ap(1024, 2048, 0, 1), i=srow.ap(1024, 2048, 0, 1):
                     e.tensor_scalar_mul(o, i, 1.0 / 4096.0)),
          reads=sk, writes=sk, tag="lne2")
        msq = self.f32a.alloc(1024, "lnmsq")
        E("vector", (lambda e, o=msq.ap(0, 1024, 0, 1), a=srow.ap(0, 1024, 0, 1),
                     b=srow.ap(0, 1024, 0, 1): e.tensor_mul(o, a, b)),
          reads=sk, writes=msq.keys(), tag="lnm2")
        E("vector", (lambda e, o=srow.ap(1024, 2048, 0, 1), a=srow.ap(1024, 2048, 0, 1),
                     b=msq.ap(0, 1024, 0, 1): e.tensor_sub(o, a, b)),
          reads=sk + msq.keys(), writes=sk, tag="lnvar")
        msq.free()
        E("scalar", (lambda e, o=srow.ap(1024, 2048, 0, 1), i=srow.ap(1024, 2048, 0, 1),
                     b=self.epsc.ap(0, 1, 0, 1): e.activation(o, i, AF.Ln, bias=b)),
          reads=sk + self.epsc.keys(0, 1), writes=sk, tag="lnln")
        E("scalar", (lambda e, o=srow.ap(1024, 2048, 0, 1), i=srow.ap(1024, 2048, 0, 1):
                     e.activation(o, i, AF.Exp, scale=-0.5)),
          reads=sk, writes=sk, tag="lnexp")
        E("vector", (lambda e, o=srow.ap(0, 1024, 0, 1), a=srow.ap(0, 1024, 0, 1),
                     b=srow.ap(1024, 2048, 0, 1): e.tensor_mul(o, a, b)),
          reads=sk, writes=sk, tag="lnnm1")
        E("vector", (lambda e, o=srow.ap(0, 1024, 0, 1), i=srow.ap(0, 1024, 0, 1):
                     e.tensor_scalar_mul(o, i, -1.0)),
          reads=sk, writes=sk, tag="lnnm2")
        for srccol, dsttile in ((1024, st["rbf"]), (0, st["nbf"])):
            for th in range(2):
                pk, pap = ps.f()
                E("tensor", (lambda e, o=pap, l=self.ones_row_f.ap(0, 128, 0, 1),
                             r=srow.ap(srccol + th * 512, srccol + (th + 1) * 512, 0, 1):
                             e.matmul(o, l, r, start=True, stop=True)),
                  reads=self.ones_row_f.keys(0, 128) + sk, writes=[pk], tag="lnbc")
                E("scalar", (lambda e, o=dsttile.ap(th * 512, (th + 1) * 512), i=pap: e.copy(o, i)),
                  reads=[pk], writes=dsttile.keys(th * 512, (th + 1) * 512), tag="lnbce")

    def ln_apply(self, st, p, src_name, dst_name, free=True):
        d, E = self.d, self.E
        pre_keys = [("prefm", p, oc, th) for oc in range(32) for th in range(2)]
        xin = [self.bf.alloc(1024, "lnxin") for _ in range(3)]
        za = [self.bf.alloc(1024, "lnza") for _ in range(2)]
        for cc in range(32):
            t = xin[cc % 3]
            E("sync", (lambda e, a=t.ap(), s=d[src_name][cc * 128:(cc + 1) * 128, :]:
                       e.dma_start(out=a, in_=s)),
              reads=pre_keys, writes=t.keys(), dma=("lnld", cc % 3), tag=f"lnld{cc}")
            z = za[cc % 2]
            E("vector", (lambda e, o=z.ap(), a=t.ap(), b=st["rbf"].ap(0, 1024):
                         e.tensor_mul(o, a, b)),
              reads=t.keys() + st["rbf"].keys(), writes=z.keys(), tag=f"lnap1_{cc}")
            E("vector", (lambda e, o=z.ap(), a=z.ap(), b=st["nbf"].ap(0, 1024):
                         e.tensor_add(o, a, b)),
              reads=z.keys() + st["nbf"].keys(), writes=z.keys(), tag=f"lnap2_{cc}")
            dst = d[dst_name][cc * 128:(cc + 1) * 128, p * 1024:(p + 1) * 1024]
            E("sync", (lambda e, a=z.ap(), o=dst: e.dma_start(out=o, in_=a)),
              reads=z.keys(), writes=[(dst_name, p, cc)], dma=self.st(), tag=f"lnst{cc}")
        for t in xin + za:
            t.free()
        if free:
            for k in ("srow", "rbf", "nbf", "rowbf"):
                st[k].free()
            for t in st["sq"]:
                t.free()

    def proj_to_prefm(self, p, ws, attn_tiles, bias_tile, bias_keys, res_name, lnst):
        d, E, ps = self.d, self.E, self.ps
        nic = len(attn_tiles)
        t1r = [self.f32a.alloc(512, "t1r") for _ in range(2)]
        resr = [self.bf.alloc(512, "resr") for _ in range(4)]
        prest = [self.bf.alloc(512, "prest") for _ in range(4)]
        ogs = [(i, min(i + 3, 32)) for i in range(0, 32, 3)]
        for gi, (lo, hi) in enumerate(ogs):
            accw = ws.stream_cols(lo * 128, hi * 128)
            banks = ps.fgroup((hi - lo) * 2)
            for ic in range(nic):
                for j, oc in enumerate(range(lo, hi)):
                    lap, lkeys = accw(ic, j * 128, (j + 1) * 128)
                    for th in range(2):
                        pk, pap = banks[j * 2 + th]
                        E("tensor",
                          (lambda e, o=pap, l=lap, r=attn_tiles[ic].ap(th * 512, (th + 1) * 512),
                           s=(ic == 0), t=(ic == nic - 1): e.matmul(o, l, r, start=s, stop=t)),
                          reads=lkeys + attn_tiles[ic].keys(th * 512, (th + 1) * 512),
                          writes=[pk], tag=f"pj{oc}{th}")
            for j, oc in enumerate(range(lo, hi)):
                for th in range(2):
                    pk, pap = banks[j * 2 + th]
                    res = resr[(oc * 2 + th) % 4]
                    src = d[res_name][oc * 128:(oc + 1) * 128,
                                      p * 1024 + th * 512:p * 1024 + (th + 1) * 512]
                    E("sync", (lambda e, a=res.ap(0, 512), s=src: e.dma_start(out=a, in_=s)),
                      reads=[(res_name, p, oc)], writes=res.keys(0, 512),
                      dma=("res", (oc * 2 + th) % 4), tag=f"res{oc}{th}")
                    pr = prest[(oc * 2 + th) % 4]
                    if bias_tile is not None:
                        t1 = t1r[th]
                        E("scalar", (lambda e, o=t1.ap(0, 512), i=pap, b=bias_tile.ap(oc, oc + 1):
                                     e.activation(o, i, AF.Identity, bias=b)),
                          reads=[pk] + bias_keys, writes=t1.keys(0, 512), tag=f"pjb{oc}{th}")
                        E("vector", (lambda e, o=pr.ap(0, 512), i=t1.ap(0, 512), r=res.ap(0, 512):
                                     e.tensor_add(o, i, r)),
                          reads=t1.keys(0, 512) + res.keys(0, 512), writes=pr.keys(0, 512),
                          tag=f"pjr{oc}{th}")
                    else:
                        E("vector", (lambda e, o=pr.ap(0, 512), i=pap, r=res.ap(0, 512):
                                     e.tensor_add(o, i, r)),
                          reads=[pk] + res.keys(0, 512), writes=pr.keys(0, 512), tag=f"pjr{oc}{th}")
                    self.ln_chunk(lnst, pr.ap(0, 512), pr.keys(0, 512), th)
                    dst = d[f"prefm{p}"][oc * 128:(oc + 1) * 128, th * 512:(th + 1) * 512]
                    E("sync", (lambda e, a=pr.ap(0, 512), o=dst: e.dma_start(out=o, in_=a)),
                      reads=pr.keys(0, 512), writes=[("prefm", p, oc, th)], dma=self.st(),
                      tag=f"pjst{oc}{th}")
        for t in t1r + resr + prest:
            t.free()

    def l0_attn(self, p, ws_o):
        attn = self.flash(p)
        lnst = self.ln_begin(f"l0a{p}")
        self.proj_to_prefm(p, ws_o, attn, self.bo_t, self.bo_t.keys(0, 32), "x0fm", lnst)
        for t in attn:
            t.free()
        self.ln_finalize(lnst)
        self.ln_apply(lnst, p, f"prefm{p}", "x1fm")

    def load_fm(self, name, p, fam):
        d, E = self.d, self.E
        xfm = [self.bf.alloc(1024, f"{name}fm{cc}") for cc in range(32)]
        for cc in range(32):
            src = d[name][cc * 128:(cc + 1) * 128, p * 1024:(p + 1) * 1024]
            E("sync", (lambda e, a=xfm[cc].ap(), s=src: e.dma_start(out=a, in_=s)),
              reads=[(name, p, cc)], writes=xfm[cc].keys(), dma=(fam, None),
              tag=f"{name}fm{cc}")
        return xfm

    def l0_ffn(self, p, ws_base, ws_bflat, rt_tile, at_tile):
        d, E, ps = self.d, self.E, self.ps
        xfm = self.load_fm("x1fm", p, "xfm")
        lf = self.f32a.alloc(1024, "lf")
        for th in range(2):
            pk, pap = ps.f()
            for cc in range(32):
                E("tensor",
                  (lambda e, o=pap[0:16, :], l=rt_tile.ap(cc * 16, (cc + 1) * 16),
                   r=xfm[cc].ap(th * 512, (th + 1) * 512), s=(cc == 0), t=(cc == 31):
                   e.matmul(o, l, r, start=s, stop=t)),
                  reads=rt_tile.keys() + xfm[cc].keys(th * 512, (th + 1) * 512),
                  writes=[pk], tag="rl")
            E("scalar", (lambda e, o=lf.ap(th * 512, (th + 1) * 512, 0, 16), i=pap[0:16, :]:
                         e.copy(o, i)),
              reads=[pk], writes=lf.keys(th * 512, (th + 1) * 512), tag="rlev")
        comb = [self.f32a.alloc(512, f"comb{tt}") for tt in range(8)]
        for tt in range(8):
            pk, pap = ps.f()
            E("tensor", (lambda e, o=pap[:, 0:16], i=lf.ap(tt * 128, (tt + 1) * 128, 0, 16),
                         idn=self.id_f32.ap(0, 16, 0, 16): e.transpose(o, i, idn)),
              reads=lf.keys() + self.id_f32.keys(0, 128), writes=[pk], tag="rltr")
            c = comb[tt]
            ck = c.keys()
            E("scalar", (lambda e, o=c.ap(0, 16), i=pap[:, 0:16]: e.copy(o, i)),
              reads=[pk], writes=ck, tag="rltm")
            E("vector", (lambda e, o=c.ap(48, 49), i=c.ap(0, 16): e.reduce_max(o, i, axis=AX.X)),
              reads=ck, writes=ck, tag="rm1")
            E("vector", (lambda e, o=c.ap(53, 54), i=c.ap(48, 49): e.tensor_scalar_mul(o, i, -1.0)),
              reads=ck, writes=ck, tag="rnm")
            E("scalar", (lambda e, o=c.ap(16, 32), i=c.ap(0, 16), b=c.ap(53, 54), ac=c.ap(49, 50):
                         e.activation(o, i, AF.Exp, bias=b, accum_out=ac)),
              reads=ck, writes=ck, tag="rexp")
            E("vector", (lambda e, o=c.ap(32, 48), i=c.ap(0, 16), s=c.ap(48, 49):
                         e.tensor_scalar(o, i, s, None, ALU.is_equal)),
              reads=ck, writes=ck, tag="req")
            E("vector", (lambda e, o=c.ap(32, 48), i=c.ap(32, 48), t2=c.ap(0, 16):
                         e.scalar_tensor_tensor(o, i, -1e30, t2, ALU.mult, ALU.add)),
              reads=ck, writes=ck, tag="rmsk")
            E("vector", (lambda e, o=c.ap(50, 51), i=c.ap(32, 48): e.reduce_max(o, i, axis=AX.X)),
              reads=ck, writes=ck, tag="rm2")
            E("vector", (lambda e, o=c.ap(32, 48), i=c.ap(0, 16), s=c.ap(50, 51):
                         e.tensor_scalar(o, i, s, None, ALU.is_ge)),
              reads=ck, writes=ck, tag="rind")
            E("vector", (lambda e, o=c.ap(16, 32), i=c.ap(16, 32), t2=c.ap(32, 48):
                         e.tensor_mul(o, i, t2)),
              reads=ck, writes=ck, tag="rcu")
            E("vector", (lambda e, o=c.ap(51, 52), i=c.ap(16, 32): e.reduce_sum(o, i, axis=AX.X)),
              reads=ck, writes=ck, tag="rs")
            E("vector", (lambda e, o=c.ap(52, 53), i=c.ap(49, 50), t2=c.ap(51, 52):
                         e.scalar_tensor_tensor(o, i, 1e-6, t2, ALU.mult, ALU.add)),
              reads=ck, writes=ck, tag="rden")
            E("vector", (lambda e, o=c.ap(52, 53), i=c.ap(52, 53): e.reciprocal(o, i)),
              reads=ck, writes=ck, tag="rrcp")
            E("vector", (lambda e, o=c.ap(32, 48), i=c.ap(16, 32), s=c.ap(52, 53):
                         e.tensor_scalar(o, i, s, 0.5, ALU.mult, ALU.mult)),
              reads=ck, writes=ck, tag="rcomb")
        lf.free()
        rhfm = self.bf.alloc(1024, "rhfm")
        for th in range(2):
            pk, pap = ps.f()
            for cc in range(32):
                E("tensor",
                  (lambda e, o=pap[0:64, :], l=at_tile.ap(cc * 64, (cc + 1) * 64),
                   r=xfm[cc].ap(th * 512, (th + 1) * 512), s=(cc == 0), t=(cc == 31):
                   e.matmul(o, l, r, start=s, stop=t)),
                  reads=at_tile.keys() + xfm[cc].keys(th * 512, (th + 1) * 512),
                  writes=[pk], tag="rh")
            E("scalar", (lambda e, o=rhfm.ap(th * 512, (th + 1) * 512, 0, 64), i=pap[0:64, :]:
                         e.copy(o, i)),
              reads=[pk], writes=rhfm.keys(th * 512, (th + 1) * 512), tag="rhev")
        whst = [self.bf.alloc(1024, "whst") for _ in range(2)]
        rhtm = self.bf.alloc(512, "rhtm")
        for tt in range(8):
            pk, pap = ps.b()
            E("tensor", (lambda e, o=pap[:, 0:64], i=rhfm.ap(tt * 128, (tt + 1) * 128, 0, 64),
                         idn=self.id_bf.ap(0, 64, 0, 64): e.transpose(o, i, idn)),
              reads=rhfm.keys() + self.id_bf.keys(0, 128), writes=[pk], tag="rhtr")
            E("scalar", (lambda e, o=rhtm.ap(0, 64), i=pap[:, 0:64]: e.copy(o, i)),
              reads=[pk], writes=rhtm.keys(0, 64), tag="rhtm")
            w = whst[tt % 2]
            for ei in range(16):
                E("vector", (lambda e, o=w.ap(ei * 64, (ei + 1) * 64), i=rhtm.ap(0, 64),
                             s=comb[tt].ap(32 + ei, 33 + ei):
                             e.tensor_scalar(o, i, s, None, ALU.mult)),
                  reads=rhtm.keys(0, 64) + comb[tt].keys(), writes=w.keys(ei * 64, (ei + 1) * 64),
                  tag=f"wh{tt}{ei}")
            E("sync", (lambda e, a=w.ap(), o=d[f"whtm{p}"][tt * 128:(tt + 1) * 128, :]:
                       e.dma_start(out=o, in_=a)),
              reads=w.keys(), writes=[("whtm", p, tt)], dma=self.st(), tag=f"whst{tt}")
        rhfm.free(); rhtm.free()
        for t in whst:
            t.free()
        whfm = [self.bf.alloc(1024, f"whfm{ec}") for ec in range(8)]
        for ec in range(8):
            src = d[f"whtm{p}"][:, ec * 128:(ec + 1) * 128]
            E("sync", (lambda e, a=whfm[ec].ap(), s=src: e.dma_start_transpose(out=a, in_=s)),
              reads=[("whtm", p, tt) for tt in range(8)], writes=whfm[ec].keys(),
              dma=("whfm", None), tag=f"whfm{ec}")
        lnst = self.ln_begin(f"l0f{p}")
        prest = [self.bf.alloc(512, "prest") for _ in range(4)]
        t1r = [self.f32a.alloc(512, "t1r") for _ in range(2)]
        ogs = [(i, min(i + 3, 32)) for i in range(0, 32, 3)]
        for (lo, hi) in ogs:
            accb = ws_base.stream_cols(lo * 128, hi * 128)
            accd = ws_bflat.stream_cols(lo * 128, hi * 128)
            banks = ps.fgroup((hi - lo) * 2)
            for ic in range(32):
                for j, oc in enumerate(range(lo, hi)):
                    lap, lkeys = accb(ic, j * 128, (j + 1) * 128)
                    for th in range(2):
                        pk, pap = banks[j * 2 + th]
                        E("tensor",
                          (lambda e, o=pap, l=lap, r=xfm[ic].ap(th * 512, (th + 1) * 512),
                           s=(ic == 0): e.matmul(o, l, r, start=s, stop=False)),
                          reads=lkeys + xfm[ic].keys(th * 512, (th + 1) * 512),
                          writes=[pk], tag=f"bs{oc}{th}")
            for ic in range(8):
                for j, oc in enumerate(range(lo, hi)):
                    lap, lkeys = accd(ic, j * 128, (j + 1) * 128)
                    for th in range(2):
                        pk, pap = banks[j * 2 + th]
                        E("tensor",
                          (lambda e, o=pap, l=lap, r=whfm[ic].ap(th * 512, (th + 1) * 512),
                           t=(ic == 7): e.matmul(o, l, r, start=False, stop=t)),
                          reads=lkeys + whfm[ic].keys(th * 512, (th + 1) * 512),
                          writes=[pk], tag=f"dl{oc}{th}")
            for j, oc in enumerate(range(lo, hi)):
                for th in range(2):
                    pk, pap = banks[j * 2 + th]
                    t1 = t1r[th]
                    E("scalar", (lambda e, o=t1.ap(0, 512), i=pap, b=self.bb_t.ap(oc, oc + 1):
                                 e.activation(o, i, AF.Identity, bias=b)),
                      reads=[pk] + self.bb_t.keys(0, 32), writes=t1.keys(0, 512), tag=f"bb{oc}{th}")
                    pr = prest[(oc * 2 + th) % 4]
                    E("vector", (lambda e, o=pr.ap(0, 512), i=t1.ap(0, 512),
                                 r=xfm[oc].ap(th * 512, (th + 1) * 512):
                                 e.tensor_add(o, i, r)),
                      reads=t1.keys(0, 512) + xfm[oc].keys(th * 512, (th + 1) * 512),
                      writes=pr.keys(0, 512), tag=f"br{oc}{th}")
                    self.ln_chunk(lnst, pr.ap(0, 512), pr.keys(0, 512), th)
                    dst = d[f"prefm{p}"][oc * 128:(oc + 1) * 128, th * 512:(th + 1) * 512]
                    E("sync", (lambda e, a=pr.ap(0, 512), o=dst: e.dma_start(out=o, in_=a)),
                      reads=pr.keys(0, 512), writes=[("prefm", p, oc, th)], dma=self.st(),
                      tag=f"bst{oc}{th}")
        for t in xfm + whfm + comb + prest + t1r:
            t.free()
        self.ln_finalize(lnst)
        self.ln_apply(lnst, p, f"prefm{p}", "x2fm")

    def l1_attn(self, p, ws_qkv, ws_proj):
        d, E, ps = self.d, self.E, self.ps
        xfm = self.load_fm("x2fm", p, "xfm")
        qst = [self.bf.alloc(512, "qst") for _ in range(4)]
        ogs = [(i, min(i + 3, 48)) for i in range(0, 48, 3)]
        for (lo, hi) in ogs:
            accw = ws_qkv.stream_cols(lo * 128, hi * 128)
            banks = ps.fgroup((hi - lo) * 2)
            for hc in range(32):
                for j, pc in enumerate(range(lo, hi)):
                    lap, lkeys = accw(hc, j * 128, (j + 1) * 128)
                    for th in range(2):
                        pk, pap = banks[j * 2 + th]
                        E("tensor",
                          (lambda e, o=pap, l=lap, r=xfm[hc].ap(th * 512, (th + 1) * 512),
                           s=(hc == 0), t=(hc == 31): e.matmul(o, l, r, start=s, stop=t)),
                          reads=lkeys + xfm[hc].keys(th * 512, (th + 1) * 512),
                          writes=[pk], tag=f"qkv1_{pc}{th}")
            for j, pc in enumerate(range(lo, hi)):
                for th in range(2):
                    pk, pap = banks[j * 2 + th]
                    stg = qst[(pc * 2 + th) % 4]
                    E("scalar", (lambda e, o=stg.ap(0, 512), i=pap: e.copy(o, i)),
                      reads=[pk], writes=stg.keys(0, 512), tag=f"qkv1e{pc}{th}")
                    dst = d[f"l1q{p}"][pc * 128:(pc + 1) * 128, th * 512:(th + 1) * 512]
                    E("sync", (lambda e, a=stg.ap(0, 512), o=dst: e.dma_start(out=o, in_=a)),
                      reads=stg.keys(0, 512), writes=[("l1q", p, pc, th)], dma=self.st(),
                      tag=f"qkv1s{pc}{th}")
        for t in xfm + qst:
            t.free()
        attn32 = [self.bf.alloc(1024, f"a32_{h}") for h in range(32)]
        qg = [[self.bf.alloc(1024, "qg") for _ in range(6)] for _ in range(2)]
        scr = self.bf.alloc(2048, "l1scr")
        scrf = self.f32a.alloc(2048, "l1scrf")
        srow = self.f32a.alloc(1024, "srow")
        wrow = self.bf.alloc(512, "wrow")
        for g in range(8):
            qs = qg[g % 2]
            for pc6 in range(6):
                pc = g * 6 + pc6
                src_ = d[f"l1q{p}"][pc * 128:(pc + 1) * 128, :]
                E("sync", (lambda e, a=qs[pc6].ap(), s=src_: e.dma_start(out=a, in_=s)),
                  reads=[("l1q", p, pc, th) for th in range(2)], writes=qs[pc6].keys(),
                  dma=("l1ld", g % 3, (p, g)), tag=f"l1ld{pc}")
            k, v = qs[4], qs[5]
            E("vector", (lambda e, o=scrf.ap(0, 512), a=k.ap(512, 1024), b=k.ap(0, 512):
                         e.tensor_sub(o, a, b)),
              reads=k.keys(), writes=scrf.keys(0, 512), tag=f"kd{g}")
            E("vector", (lambda e, o=scr.ap(512, 1024), a=v.ap(512, 1024), b=v.ap(0, 512):
                         e.tensor_sub(o, a, b)),
              reads=v.keys(), writes=scr.keys(512, 1024), tag=f"vd{g}")
            for j in range(4):
                h = g * 4 + j
                at = attn32[h]
                E("scalar", (lambda e, o=at.ap(0, 512), i=v.ap(0, 512): e.copy(o, i)),
                  reads=v.keys(), writes=at.keys(0, 512), tag=f"o0_{h}")
                E("vector", (lambda e, o=scrf.ap(512, 1024), a=qs[j].ap(512, 1024),
                             b=scrf.ap(0, 512): e.tensor_mul(o, a, b)),
                  reads=qs[j].keys() + scrf.keys(0, 512), writes=scrf.keys(512, 1024),
                  tag=f"pr{h}")
                pkD, papD = ps.f()
                E("tensor", (lambda e, o=papD[0:1, :], l=self.ones_col_f.ap(0, 1),
                             r=scrf.ap(512, 1024): e.matmul(o, l, r, start=True, stop=True)),
                  reads=self.ones_col_f.keys(0, 1) + scrf.keys(512, 1024), writes=[pkD],
                  tag=f"dot{h}")
                E("scalar", (lambda e, o=srow.ap(0, 512, 0, 1), i=papD[0:1, :]:
                             e.activation(o, i, AF.Exp, scale=-L1_SCALE)),
                  reads=[pkD], writes=srow.keys(0, 512), tag=f"se{h}")
                E("vector", (lambda e, o=srow.ap(512, 1024, 0, 1), i=srow.ap(0, 512, 0, 1):
                             e.tensor_scalar_add(o, i, 1.0)),
                  reads=srow.keys(0, 512), writes=srow.keys(512, 1024), tag=f"sd{h}")
                E("vector", (lambda e, o=srow.ap(512, 1024, 0, 1), i=srow.ap(512, 1024, 0, 1):
                             e.reciprocal(o, i)),
                  reads=srow.keys(512, 1024), writes=srow.keys(512, 1024), tag=f"sr{h}")
                E("vector", (lambda e, o=wrow.ap(0, 512, 0, 1), i=srow.ap(512, 1024, 0, 1):
                             e.tensor_copy(o, i)),
                  reads=srow.keys(512, 1024), writes=wrow.keys(), tag=f"swb{h}")
                pkB, papB = ps.f()
                E("tensor", (lambda e, o=papB, l=self.ones_row.ap(0, 128, 0, 1),
                             r=wrow.ap(0, 512, 0, 1): e.matmul(o, l, r, start=True, stop=True)),
                  reads=self.ones_row.keys(0, 128) + wrow.keys(), writes=[pkB], tag=f"wb{h}")
                E("vector", (lambda e, o=scrf.ap(1024, 1536), a=scr.ap(512, 1024), b=papB:
                             e.tensor_mul(o, a, b)),
                  reads=scr.keys(512, 1024) + [pkB], writes=scrf.keys(1024, 1536), tag=f"wm{h}")
                E("vector", (lambda e, o=at.ap(512, 1024), a=scrf.ap(1024, 1536), b=v.ap(0, 512):
                             e.tensor_add(o, a, b)),
                  reads=scrf.keys(1024, 1536) + v.keys(), writes=at.keys(512, 1024), tag=f"o1_{h}")
        for s in qg:
            for t in s:
                t.free()
        scr.free(); scrf.free(); srow.free(); wrow.free()
        lnst = self.ln_begin(f"l1a{p}")
        self.proj_to_prefm(p, ws_proj, attn32, None, [], "x2fm", lnst)
        for t in attn32:
            t.free()
        self.ln_finalize(lnst)
        self.ln_apply(lnst, p, f"prefm{p}", "x3fm")

    def l1_ffn(self, p, ws_up, ws_down, ws_fc1, ws_fc2, ws_exp, gw_tile):
        d, E, ps = self.d, self.E, self.ps
        xfm = self.load_fm("x3fm", p, "xfm")
        hst = [self.bf.alloc(512, "hst") for _ in range(4)]
        ogs = [(i, min(i + 3, 64)) for i in range(0, 64, 3)]
        for (lo, hi) in ogs:
            accw = ws_up.stream_cols(lo * 128, hi * 128)
            banks = ps.fgroup((hi - lo) * 2)
            for cc in range(32):
                for j, io in enumerate(range(lo, hi)):
                    lap, lkeys = accw(cc, j * 128, (j + 1) * 128)
                    for th in range(2):
                        pk, pap = banks[j * 2 + th]
                        E("tensor",
                          (lambda e, o=pap, l=lap, r=xfm[cc].ap(th * 512, (th + 1) * 512),
                           s=(cc == 0), t=(cc == 31): e.matmul(o, l, r, start=s, stop=t)),
                          reads=lkeys + xfm[cc].keys(th * 512, (th + 1) * 512),
                          writes=[pk], tag=f"up{io}{th}")
            for j, io in enumerate(range(lo, hi)):
                for th in range(2):
                    pk, pap = banks[j * 2 + th]
                    hs = hst[(io * 2 + th) % 4]
                    E("scalar", (lambda e, o=hs.ap(0, 512), i=pap: e.activation(o, i, AF.Relu)),
                      reads=[pk], writes=hs.keys(0, 512), tag=f"uprl{io}{th}")
                    E("vector", (lambda e, o=hs.ap(0, 512), i=hs.ap(0, 512): e.tensor_mul(o, i, i)),
                      reads=hs.keys(0, 512), writes=hs.keys(0, 512), tag=f"upsq{io}{th}")
                    dst = d[f"hdr{p}"][io * 128:(io + 1) * 128, th * 512:(th + 1) * 512]
                    E("sync", (lambda e, a=hs.ap(0, 512), o=dst: e.dma_start(out=o, in_=a)),
                      reads=hs.keys(0, 512), writes=[("hdr", p, io, th)], dma=self.st(),
                      tag=f"upst{io}{th}")
        for t in hst:
            t.free()

        rwrow = self.f32a.alloc(1024, "rwrow")
        rwb = [self.f32a.alloc(512, f"rwb{th}") for th in range(2)]
        gl = self.f32a.alloc(512, "gl")
        glk = gl.keys()
        for tt in range(8):
            pk, pap = ps.f()
            for cc in range(32):
                E("tensor",
                  (lambda e, o=pap[:, 0:8], l=xfm[cc].ap(tt * 128, (tt + 1) * 128),
                   r=gw_tile.ap(cc * 8, (cc + 1) * 8), s=(cc == 0), t=(cc == 31):
                   e.matmul(o, l, r, start=s, stop=t)),
                  reads=gw_tile.keys() + xfm[cc].keys(tt * 128, (tt + 1) * 128),
                  writes=[pk], tag=f"gt{tt}")
            E("scalar", (lambda e, o=gl.ap(0, 8), i=pap[:, 0:8]: e.copy(o, i)),
              reads=[pk], writes=glk, tag=f"gtev{tt}")
            E("vector", (lambda e, o=gl.ap(16, 17), i=gl.ap(0, 8): e.reduce_max(o, i, axis=AX.X)),
              reads=glk, writes=glk, tag="gm1")
            E("vector", (lambda e, o=gl.ap(8, 16), i=gl.ap(0, 8), s=gl.ap(16, 17):
                         e.tensor_scalar(o, i, s, None, ALU.is_equal)),
              reads=glk, writes=glk, tag="geq")
            E("vector", (lambda e, o=gl.ap(8, 16), i=gl.ap(8, 16), t2=gl.ap(0, 8):
                         e.scalar_tensor_tensor(o, i, -1e30, t2, ALU.mult, ALU.add)),
              reads=glk, writes=glk, tag="gmsk")
            E("vector", (lambda e, o=gl.ap(17, 18), i=gl.ap(8, 16): e.reduce_max(o, i, axis=AX.X)),
              reads=glk, writes=glk, tag="gm2")
            E("vector", (lambda e, o=gl.ap(18, 19), a=gl.ap(16, 17), b=gl.ap(17, 18):
                         e.tensor_add(o, a, b)),
              reads=glk, writes=glk, tag="grw")
            pkT, papT = ps.f()
            E("tensor", (lambda e, o=papT[0:1, 0:128], i=gl.ap(18, 19),
                         idn=self.id_f32.ap(0, 128): e.transpose(o, i, idn)),
              reads=glk + self.id_f32.keys(0, 128), writes=[pkT], tag=f"gtr{tt}")
            E("scalar", (lambda e, o=rwrow.ap(tt * 128, (tt + 1) * 128, 0, 1), i=papT[0:1, 0:128]:
                         e.copy(o, i)),
              reads=[pkT], writes=rwrow.keys(), tag=f"grow{tt}")
        gl.free()
        rwrow_bf = self.bf.alloc(1024, "rwrow_bf")
        E("vector", (lambda e, o=rwrow_bf.ap(0, 1024, 0, 1), i=rwrow.ap(0, 1024, 0, 1):
                     e.tensor_copy(o, i)),
          reads=rwrow.keys(), writes=rwrow_bf.keys(), tag="growbf")
        for th in range(2):
            pkB, papB = ps.f()
            E("tensor", (lambda e, o=papB, l=self.ones_row.ap(0, 128, 0, 1),
                         r=rwrow_bf.ap(th * 512, (th + 1) * 512, 0, 1):
                         e.matmul(o, l, r, start=True, stop=True)),
              reads=self.ones_row.keys(0, 128) + rwrow_bf.keys(), writes=[pkB], tag=f"rwb{th}")
            E("scalar", (lambda e, o=rwb[th].ap(0, 512), i=papB: e.copy(o, i)),
              reads=[pkB], writes=rwb[th].keys(), tag=f"rwbe{th}")
        xlat = [self.bf.alloc(1024, f"xlat{lc}") for lc in range(16)]
        for (lo, hi) in [(i, min(i + 3, 16)) for i in range(0, 16, 3)]:
            accw = ws_fc1.stream_cols(lo * 128, hi * 128)
            banks = ps.fgroup((hi - lo) * 2)
            for cc in range(32):
                for j, lc in enumerate(range(lo, hi)):
                    lap, lkeys = accw(cc, j * 128, (j + 1) * 128)
                    for th in range(2):
                        pk, pap = banks[j * 2 + th]
                        E("tensor",
                          (lambda e, o=pap, l=lap, r=xfm[cc].ap(th * 512, (th + 1) * 512),
                           s=(cc == 0), t=(cc == 31): e.matmul(o, l, r, start=s, stop=t)),
                          reads=lkeys + xfm[cc].keys(th * 512, (th + 1) * 512),
                          writes=[pk], tag=f"f1{lc}{th}")
            for j, lc in enumerate(range(lo, hi)):
                for th in range(2):
                    pk, pap = banks[j * 2 + th]
                    E("scalar", (lambda e, o=xlat[lc].ap(th * 512, (th + 1) * 512), i=pap:
                                 e.copy(o, i)),
                      reads=[pk], writes=xlat[lc].keys(th * 512, (th + 1) * 512), tag=f"f1e{lc}{th}")
        for t in xfm:
            t.free()
        moe = [self.bf.alloc(1024, f"moe{mc}") for mc in range(16)]
        for (lo, hi) in [(i, min(i + 3, 16)) for i in range(0, 16, 3)]:
            accw = ws_exp.stream_cols(lo * 128, hi * 128)
            banks = ps.fgroup((hi - lo) * 2)
            for lc in range(16):
                for j, mc in enumerate(range(lo, hi)):
                    lap, lkeys = accw(lc, j * 128, (j + 1) * 128)
                    for th in range(2):
                        pk, pap = banks[j * 2 + th]
                        E("tensor",
                          (lambda e, o=pap, l=lap, r=xlat[lc].ap(th * 512, (th + 1) * 512),
                           s=(lc == 0), t=(lc == 15): e.matmul(o, l, r, start=s, stop=t)),
                          reads=lkeys + xlat[lc].keys(th * 512, (th + 1) * 512),
                          writes=[pk], tag=f"ex{mc}{th}")
            for j, mc in enumerate(range(lo, hi)):
                for th in range(2):
                    pk, pap = banks[j * 2 + th]
                    E("vector", (lambda e, o=moe[mc].ap(th * 512, (th + 1) * 512), i=pap,
                                 r=rwb[th].ap(0, 512): e.tensor_mul(o, i, r)),
                      reads=[pk] + rwb[th].keys(), writes=moe[mc].keys(th * 512, (th + 1) * 512),
                      tag=f"exe{mc}{th}")
        for t in xlat:
            t.free()
        routed = [self.bf.alloc(1024, f"rt{hc}") for hc in range(32)]
        for (lo, hi) in [(i, min(i + 3, 32)) for i in range(0, 32, 3)]:
            accw = ws_fc2.stream_cols(lo * 128, hi * 128)
            banks = ps.fgroup((hi - lo) * 2)
            for lc in range(16):
                for j, hc in enumerate(range(lo, hi)):
                    lap, lkeys = accw(lc, j * 128, (j + 1) * 128)
                    for th in range(2):
                        pk, pap = banks[j * 2 + th]
                        E("tensor",
                          (lambda e, o=pap, l=lap, r=moe[lc].ap(th * 512, (th + 1) * 512),
                           s=(lc == 0), t=(lc == 15): e.matmul(o, l, r, start=s, stop=t)),
                          reads=lkeys + moe[lc].keys(th * 512, (th + 1) * 512),
                          writes=[pk], tag=f"f2{hc}{th}")
            for j, hc in enumerate(range(lo, hi)):
                for th in range(2):
                    pk, pap = banks[j * 2 + th]
                    E("scalar", (lambda e, o=routed[hc].ap(th * 512, (th + 1) * 512), i=pap:
                                 e.copy(o, i)),
                      reads=[pk], writes=routed[hc].keys(th * 512, (th + 1) * 512),
                      tag=f"f2e{hc}{th}")
        for t in moe:
            t.free()
        lnst = self.ln_begin(f"l1f{p}")
        hrd = [self.bf.alloc(1024, "hrd") for _ in range(3)]
        prest = [self.bf.alloc(512, "prest") for _ in range(4)]
        for og, (lo, hi) in enumerate([(i, min(i + 3, 32)) for i in range(0, 32, 3)]):
            accw = ws_down.stream_cols(lo * 128, hi * 128)
            banks = ps.fgroup((hi - lo) * 2)
            for ic in range(64):
                hb = hrd[ic % 3]
                src_ = d[f"hdr{p}"][ic * 128:(ic + 1) * 128, :]
                E("sync", (lambda e, a=hb.ap(0, 1024), s=src_: e.dma_start(out=a, in_=s)),
                  reads=[("hdr", p, ic, 0), ("hdr", p, ic, 1)], writes=hb.keys(0, 1024),
                  dma=("hrd", (og * 64 + ic) % 3), tag=f"hld{og}{ic}")
                for j, oc in enumerate(range(lo, hi)):
                    lap, lkeys = accw(ic, j * 128, (j + 1) * 128)
                    for th in range(2):
                        pk, pap = banks[j * 2 + th]
                        E("tensor",
                          (lambda e, o=pap, l=lap, r=hb.ap(th * 512, (th + 1) * 512),
                           s=(ic == 0), t=(ic == 63): e.matmul(o, l, r, start=s, stop=t)),
                          reads=lkeys + hb.keys(th * 512, (th + 1) * 512), writes=[pk],
                          tag=f"dn{oc}{th}")
            for j, oc in enumerate(range(lo, hi)):
                for th in range(2):
                    pk, pap = banks[j * 2 + th]
                    pr = prest[(oc * 2 + th) % 4]
                    E("vector", (lambda e, o=pr.ap(0, 512), i=pap,
                                 r=routed[oc].ap(th * 512, (th + 1) * 512): e.tensor_add(o, i, r)),
                      reads=[pk] + routed[oc].keys(th * 512, (th + 1) * 512),
                      writes=pr.keys(0, 512), tag=f"dne{oc}{th}")
                    self.ln_chunk(lnst, pr.ap(0, 512), pr.keys(0, 512), th)
                    dst = d[f"prefm{p}"][oc * 128:(oc + 1) * 128, th * 512:(th + 1) * 512]
                    E("sync", (lambda e, a=pr.ap(0, 512), o=dst: e.dma_start(out=o, in_=a)),
                      reads=pr.keys(0, 512), writes=[("prefm", p, oc, th)], dma=self.st(),
                      tag=f"dnst{oc}{th}")
        for t in hrd + prest + rwb + [rwrow, rwrow_bf]:
            t.free()

        self.ln_finalize(lnst)
        pre_keys = [("prefm", p, oc, th) for oc in range(32) for th in range(2)]
        xin2 = [self.bf.alloc(1024, "epx") for _ in range(3)]
        x3r = [self.bf.alloc(1024, "epx3") for _ in range(3)]
        za = [self.bf.alloc(1024, "epza") for _ in range(2)]
        for cc in range(32):
            t = xin2[cc % 3]
            E("sync", (lambda e, a=t.ap(), s=d[f"prefm{p}"][cc * 128:(cc + 1) * 128, :]:
                       e.dma_start(out=a, in_=s)),
              reads=pre_keys, writes=t.keys(), dma=("lnld", cc % 3), tag=f"epld{cc}")
            x3 = x3r[cc % 3]
            src_ = d["x3fm"][cc * 128:(cc + 1) * 128, p * 1024:(p + 1) * 1024]
            E("sync", (lambda e, a=x3.ap(), s=src_: e.dma_start(out=a, in_=s)),
              reads=[("x3fm", p, cc)], writes=x3.keys(), dma=("res", cc % 3), tag=f"epx3{cc}")
            z = za[cc % 2]
            E("vector", (lambda e, o=z.ap(), a=t.ap(), b=lnst["rbf"].ap(0, 1024):
                         e.tensor_mul(o, a, b)),
              reads=t.keys() + lnst["rbf"].keys(), writes=z.keys(), tag=f"ep1_{cc}")
            E("vector", (lambda e, o=z.ap(), a=z.ap(), b=lnst["nbf"].ap(0, 1024):
                         e.tensor_add(o, a, b)),
              reads=z.keys() + lnst["nbf"].keys(), writes=z.keys(), tag=f"ep2_{cc}")
            E("vector", (lambda e, o=z.ap(), i=z.ap(), g=self.gam_t.ap(cc, cc + 1),
                         b=self.bet_t.ap(cc, cc + 1):
                         e.tensor_scalar(o, i, g, b, ALU.mult, ALU.add)),
              reads=z.keys() + self.gam_t.keys(0, 32) + self.bet_t.keys(0, 32),
              writes=z.keys(), tag=f"ep3_{cc}")
            pre2 = routed[cc]
            E("vector", (lambda e, o=pre2.ap(), a=z.ap(), b=x3.ap(): e.tensor_add(o, a, b)),
              reads=z.keys() + x3.keys(), writes=pre2.keys(), tag=f"ep4_{cc}")
        for t in xin2 + x3r + za:
            t.free()
        for k in ("srow", "rbf", "nbf", "rowbf"):
            lnst[k].free()
        for t in lnst["sq"]:
            t.free()
        pretm = [self.bf.alloc(4096, "pretm") for _ in range(2)]
        ostg = [self.f32a.alloc(1024, "ostg") for _ in range(2)]
        stat = self.f32a.alloc(512, "stat2")
        sk = stat.keys()
        for tt in range(8):
            t = pretm[tt % 2]
            for ccg in range(8):
                pkT, papT = ps.b()
                for j in range(4):
                    cc = ccg * 4 + j
                    E("tensor", (lambda e, o=papT[:, j * 128:(j + 1) * 128],
                                 i=routed[cc].ap(tt * 128, (tt + 1) * 128),
                                 idn=self.id_bf.ap(0, 128): e.transpose(o, i, idn)),
                      reads=routed[cc].keys(tt * 128, (tt + 1) * 128) + self.id_bf.keys(0, 128),
                      writes=[pkT], tag=f"eptr{tt}{ccg}")
                E("scalar", (lambda e, o=t.ap(ccg * 512, (ccg + 1) * 512), i=papT[:, 0:512]:
                             e.copy(o, i)),
                  reads=[pkT], writes=t.keys(ccg * 512, (ccg + 1) * 512), tag=f"eptre{tt}{ccg}")
            self._ln_stats(t, stat, sk, 50, 51)
            b_, q0 = tt // 4, (tt % 4) * 128
            orow = b_ * 1024 + p * 512 + q0
            for ch in range(4):
                o = ostg[ch % 2]
                E("vector", (lambda e, o_=o.ap(), i=t.ap(ch * 1024, (ch + 1) * 1024),
                             r=stat.ap(50, 51), m=stat.ap(51, 52):
                             e.tensor_scalar(o_, i, r, m, ALU.mult, ALU.add)),
                  reads=t.keys() + sk, writes=o.keys(), tag=f"eout{tt}{ch}")
                dst = d["out"][orow:orow + 128, ch * 1024:(ch + 1) * 1024]
                E("sync", (lambda e, a=o.ap(), o_=dst: e.dma_start(out=o_, in_=a)),
                  reads=o.keys(), writes=[("out", p, tt, ch)], dma=self.st(), tag=f"est{tt}{ch}")
        for t in pretm + ostg + [stat]:
            t.free()
        for t in routed:
            t.free()


def build_program(part, taps=()):
    mk = MK(part, taps)
    d = mk.d
    if part in ("s1", "all"):
        ws_qkv = WStream(mk, "wqkv", d["w_l0qkv"])
        for p in range(2):
            mk.s1(p, ws_qkv)
            if part == "all":
                mk.allgather(p)
    if part == "all":
        mk.bqkv.free(); mk.bq_s.free(); mk.vbias.free()
    if part in ("rest", "all"):
        rt = mk.bf.alloc(512, "rt")
        at = mk.bf.alloc(2048, "at")
        gw = mk.bf.alloc(512, "gw")
        for tile, name, width in ((rt, "r_t", 16), (at, "a_t", 64), (gw, "gate_w", 8)):
            src = d[name].rearrange("(cc p) e -> p cc e", p=128)
            dst = tile.ap(0, 32 * width).rearrange("p (cc e) -> p cc e", e=width)
            mk.E("gpsimd", (lambda e, a=dst, s=src: e.dma_start(out=a, in_=s)),
                 writes=tile.keys(0, 32 * width), dma=("cst", None), tag=f"ld_{name}")
        ws_o = WStream(mk, "wo", d["w_l0o"])
        ws_base = WStream(mk, "wbase", d["w_l0base_t"])
        ws_bflat = WStream(mk, "wbflat", d["b_flat"])
        ws_l1qkv = WStream(mk, "wl1qkv", d["w_l1qkv_t"])
        ws_l1proj = WStream(mk, "wl1proj", d["w_l1proj_t"])
        ws_up = WStream(mk, "wup", d["up_w"])
        ws_down = WStream(mk, "wdown", d["down_w"])
        ws_fc1 = WStream(mk, "wfc1", d["fc1_w"])
        ws_fc2 = WStream(mk, "wfc2", d["fc2_w"])
        ws_exp = WStream(mk, "wexp", d["expert_w"])
        for p in range(2):
            mk.l0_attn(p, ws_o)
            mk.tap("x1fm", [("x1fm", p, cc) for cc in range(32)])
        for p in range(2):
            mk.l0_ffn(p, ws_base, ws_bflat, rt, at)
            mk.tap("x2fm", [("x2fm", p, cc) for cc in range(32)])
        for p in range(2):
            mk.l1_attn(p, ws_l1qkv, ws_l1proj)
            mk.tap("x3fm", [("x3fm", p, cc) for cc in range(32)])
        for p in range(2):
            mk.l1_ffn(p, ws_up, ws_down, ws_fc1, ws_fc2, ws_exp, gw)
    mk.plan.build()
    return mk


_progs = {}


def _get_prog(part, taps=()):
    key = (part, tuple(sorted(taps)))
    if key not in _progs:
        _progs[key] = build_program(part, taps)
    return _progs[key]


REST_W = ("w_l0o", "b_l0o", "w_l0base_t", "b_l0base", "a_t", "b_flat", "r_t",
          "w_l1qkv_t", "w_l1proj_t", "gate_w", "up_w", "down_w", "fc1_w",
          "fc2_w", "expert_w", "ln_g", "ln_b")


def host_layout(inputs):
    C = np.ascontiguousarray
    w = {}
    w["w_l0qkv"] = C(inputs["L0_attn_Wqkv"])
    w["b_l0qkv"] = C(inputs["L0_attn_bqkv"]).reshape(1, 3072)
    w["w_l0o"] = C(inputs["L0_attn_Wo"])
    w["b_l0o"] = C(inputs["L0_attn_bo"]).reshape(1, 4096)
    w["w_l0base_t"] = C(inputs["L0_ffn_base_w"].T)
    w["b_l0base"] = C(inputs["L0_ffn_base_b"]).reshape(1, 4096)
    w["a_t"] = C(inputs["L0_ffn_A"].T)
    w["b_flat"] = C(np.transpose(inputs["L0_ffn_B"], (0, 2, 1)).reshape(1024, 4096))
    w["r_t"] = C(inputs["L0_ffn_router_w"].T)
    w["w_l1qkv_t"] = C(inputs["L1_attn_weight_qkv"].T)
    w["w_l1proj_t"] = C(inputs["L1_attn_weight_proj"].T)
    w["gate_w"] = C(inputs["L1_ffn_gate_w"])
    w["up_w"] = C(inputs["L1_ffn_up_w"])
    w["down_w"] = C(inputs["L1_ffn_down_w"])
    w["fc1_w"] = C(inputs["L1_ffn_fc1_w"])
    w["fc2_w"] = C(inputs["L1_ffn_fc2_w"])
    w["expert_w"] = C(inputs["L1_ffn_expert_w"])
    w["ln_g"] = C(inputs["L1_ffn_ln_gamma"]).reshape(1, 4096)
    w["ln_b"] = C(inputs["L1_ffn_ln_beta"]).reshape(1, 4096)
    return w


def shard_x(x):
    return [np.ascontiguousarray(x[:, r * 1024:(r + 1) * 1024, :].reshape(2048, DMODEL))
            for r in range(NCORES)]


FUSED = True


def kernel(**inputs):
    w = host_layout(inputs)
    xs = shard_x(np.asarray(inputs["L0_attn_x"]))
    if FUSED:
        mk = _get_prog("all")
        ins = [{**{n: w[n] for n in REST_W}, "x0": xs[r],
                "w_l0qkv": w["w_l0qkv"], "b_l0qkv": w["b_l0qkv"]}
               for r in range(NCORES)]
        res = run_bass_kernel_spmd(mk.nc, ins, core_ids=list(range(NCORES)))
        out = np.empty((2, SEQ, DMODEL), np.float32)
        for r in range(NCORES):
            out[:, r * 1024:(r + 1) * 1024, :] = res.results[r]["out"].reshape(2, 1024, DMODEL)
        return out
    mk1 = _get_prog("s1")
    in1 = [{"x0": xs[r], "w_l0qkv": w["w_l0qkv"], "b_l0qkv": w["b_l0qkv"]}
           for r in range(NCORES)]
    r1 = run_bass_kernel_spmd(mk1.nc, in1, core_ids=list(range(NCORES)))
    ksp = {f"kout{p}": np.concatenate([r1.results[r]["kin"][p * 2048:(p + 1) * 2048]
                                        for r in range(NCORES)], axis=0) for p in range(2)}
    vsp = {f"vout{p}": np.concatenate([r1.results[r]["vin"][p * 1024:(p + 1) * 1024]
                                       for r in range(NCORES)], axis=0) for p in range(2)}
    mk2 = _get_prog("rest")
    in2 = [{**{n: w[n] for n in REST_W}, **ksp, **vsp,
            "qdr": r1.results[r]["qdr"], "x0fm": r1.results[r]["x0fm"]}
           for r in range(NCORES)]
    r2 = run_bass_kernel_spmd(mk2.nc, in2, core_ids=list(range(NCORES)))
    out = np.empty((2, SEQ, DMODEL), np.float32)
    for r in range(NCORES):
        out[:, r * 1024:(r + 1) * 1024, :] = r2.results[r]["out"].reshape(2, 1024, DMODEL)
    return out


# version 06
# speedup vs baseline: 1.0172x; 1.0172x over previous
import numpy as np
import ml_dtypes
import concourse.bass as bass
import concourse.mybir as mybir
from concourse.bass_utils import run_bass_kernel_spmd

F32 = mybir.dt.float32
BF16 = mybir.dt.bfloat16
AF = mybir.ActivationFunctionType
ALU = mybir.AluOpType
AX = mybir.AxisListType
BFNP = ml_dtypes.bfloat16

NCORES = 8
DMODEL = 4096
CC = 32
TP = 1024
NP2 = 2
SEQ = 8192
POS_PER_CORE = 1024
LN_EPS = 1e-5
L0_SCALE = 1.0 / 32.0
L1_SCALE = 1.0 / (128.0 ** 0.5)

ENGINES = ("tensor", "scalar", "vector", "gpsimd", "sync")


class Op:
    __slots__ = ("eng", "fn", "deps", "is_dma", "dma_key", "consumed",
                 "event", "vc", "waits", "inc", "idx", "tag")

    def __init__(self):
        self.deps = ()
        self.consumed = False
        self.event = None
        self.vc = None
        self.waits = None
        self.inc = None


class Plan:
    def __init__(self, nc):
        self.nc = nc
        self.ops = []
        self.by_eng = {e: [] for e in ENGINES}
        self.res = {}
        self.n_sems = 0

    def emit(self, eng, fn, reads=(), writes=(), dma=None, tag=""):
        deps = set()
        for k in reads:
            e = self.res.get(k)
            if e is not None and e[0] is not None:
                deps.add(e[0])
        for k in writes:
            e = self.res.get(k)
            if e is not None:
                if e[0] is not None:
                    deps.add(e[0])
                deps.update(e[1])
        op = Op()
        op.eng = eng
        op.fn = fn
        op.is_dma = dma is not None
        op.dma_key = dma
        op.tag = tag
        op.idx = len(self.ops)
        kept = []
        for d in deps:
            if eng == "tensor" and d.eng == "tensor" and not d.is_dma and not op.is_dma:
                continue
            kept.append(d)
        op.deps = tuple(kept)
        self.ops.append(op)
        self.by_eng[eng].append(op)
        for k in reads:
            ent = self.res.get(k)
            if ent is None:
                self.res[k] = [None, [op]]
            else:
                ent[1].append(op)
        for k in writes:
            self.res[k] = [op, []]
        return op

    def build(self):
        nc = self.nc
        for op in self.ops:
            for d in op.deps:
                d.consumed = True
        eng_sem = {e: nc.alloc_semaphore(name=f"sq_{e}") for e in ENGINES}
        self.n_sems = len(ENGINES)
        dma_sems = {}
        fam_counts = {}
        fam_gen_pending = {}

        def get_dma_sem(semkey):
            ent = dma_sems.get(semkey)
            if ent is None:
                h = nc.alloc_semaphore(name=f"dm{len(dma_sems)}")
                self.n_sems += 1
                ent = dma_sems[semkey] = [h, 0]
            return ent

        cnt = {e: 0 for e in ENGINES}
        vc = {e: {} for e in ENGINES}

        for op in self.ops:
            myvc = vc[op.eng]
            needed = {}
            for d in op.deps:
                assert d.event is not None, f"dep without event: {d.tag}"
                s, v = d.event
                if d.is_dma and d.dma_key[1] is None:
                    semkey = ("F", d.dma_key[0])
                    v = dma_sems[semkey][1]
                if myvc.get(s, 0) < v and needed.get(s, 0) < v:
                    needed[s] = v
            for d in op.deps:
                for s2, v2 in d.vc.items():
                    if myvc.get(s2, 0) < v2:
                        myvc[s2] = v2
            for s, v in needed.items():
                if myvc.get(s, 0) < v:
                    myvc[s] = v
            if op.is_dma:
                fam, slot = op.dma_key[0], op.dma_key[1]
                gen = op.dma_key[2] if len(op.dma_key) > 2 else None
                if slot is None:
                    semkey = ("F", fam)
                else:
                    semkey = ("S", fam, slot)
                ent = get_dma_sem(semkey)
                h, prev = ent[0], ent[1]
                same_gen = gen is not None and len(ent) > 2 and ent[2] == gen
                if prev > 0 and not same_gen and myvc.get(id(h), 0) < prev:
                    if needed.get(id(h), 0) < prev:
                        needed[id(h)] = prev
                    myvc[id(h)] = prev
                ent[1] = prev + 16
                if len(ent) > 2:
                    ent[2] = gen
                else:
                    ent.append(gen)
                op.event = (id(h), ent[1])
                op.inc = (h, 16)
            elif op.consumed:
                cnt[op.eng] += 1
                op.event = (id(eng_sem[op.eng]), cnt[op.eng])
                op.inc = (eng_sem[op.eng], 1)
            else:
                op.event = (id(eng_sem[op.eng]), cnt[op.eng])
                op.inc = None
            op.waits = needed
            snap = dict(myvc)
            if op.inc is not None:
                snap[op.event[0]] = op.event[1]
            op.vc = snap

        id2h = {}
        for e in ENGINES:
            id2h[id(eng_sem[e])] = eng_sem[e]
        for ent in dma_sems.values():
            id2h[id(ent[0])] = ent[0]

        tail = [(eng_sem[e], cnt[e]) for e in ENGINES if cnt[e] > 0]
        tail += [(ent[0], ent[1]) for ent in dma_sems.values()]

        with nc.Block() as block:
            def mk(eng_ops, is_sync):
                def body(eng):
                    for op in eng_ops:
                        for s, v in sorted(op.waits.items(), key=lambda kv: kv[1]):
                            eng.wait_ge(id2h[s], v)
                        inst = op.fn(eng)
                        if op.inc is not None:
                            inst.then_inc(op.inc[0], op.inc[1])
                    if is_sync:
                        for h, v in tail:
                            eng.wait_ge(h, v)
                        for h, _ in tail:
                            eng.sem_clear(h)
                return body
            for e in ENGINES:
                if self.by_eng[e] or e == "sync":
                    getattr(block, e)(mk(self.by_eng[e], e == "sync"))
        return eng_sem, dma_sems


class Arena:
    QUANT = 512

    def __init__(self, nc, name, dtype, cols):
        self.name = name
        self.dtype = dtype
        self.cols = cols
        self.t = nc.alloc_sbuf_tensor(name, [128, cols], dtype).ap()
        self.free_list = [(0, cols)]
        self.peak = 0
        self.used = 0

    def alloc(self, cols, tag=""):
        cols = ((cols + self.QUANT - 1) // self.QUANT) * self.QUANT
        for i, (off, size) in enumerate(self.free_list):
            if size >= cols:
                if size == cols:
                    self.free_list.pop(i)
                else:
                    self.free_list[i] = (off + cols, size - cols)
                self.used += cols
                self.peak = max(self.peak, self.used)
                return Tile(self, off, cols)
        raise MemoryError(f"arena {self.name} out of space for {cols} ({tag}); used={self.used}/{self.cols}")

    def free(self, tile):
        self.used -= tile.cols
        self.free_list.append((tile.off, tile.cols))
        self.free_list.sort()
        merged = []
        for off, size in self.free_list:
            if merged and merged[-1][0] + merged[-1][1] == off:
                merged[-1] = (merged[-1][0], merged[-1][1] + size)
            else:
                merged.append((off, size))
        self.free_list = [tuple(x) for x in merged]


class Tile:
    __slots__ = ("arena", "off", "cols")

    def __init__(self, arena, off, cols):
        self.arena = arena
        self.off = off
        self.cols = cols

    def ap(self, c0=0, c1=None, p0=0, p1=128):
        c1 = self.cols if c1 is None else c1
        return self.arena.t[p0:p1, self.off + c0:self.off + c1]

    def keys(self, c0=0, c1=None):
        c1 = self.cols if c1 is None else c1
        q = Arena.QUANT
        lo = (self.off + c0) // q
        hi = (self.off + c1 + q - 1) // q
        return [(self.arena.name, i) for i in range(lo, hi)]

    def free(self):
        self.arena.free(self)


def _ceil(a, b):
    return (a + b - 1) // b


class PsumPool:

    def __init__(self, nc):
        self.f32 = [(f"ps{i}", nc.alloc_psum_tensor(f"ps{i}", [128, 512], F32).ap())
                    for i in range(8)]
        self.bf16 = [(k, ap.bitcast(BF16)) for k, ap in self.f32[6:8]]
        self.i_f = 0
        self.i_b = 0

    def f(self):
        k, ap = self.f32[self.i_f % 6]
        self.i_f += 1
        return k, ap

    def fgroup(self, n):
        assert n <= 6
        return [self.f() for _ in range(n)]

    def r(self):
        k, ap = self.f32[6 + self.i_b % 2]
        self.i_b += 1
        return k, ap

    def b(self):
        k, ap = self.bf16[self.i_b % 2]
        self.i_b += 1
        return k, ap


class WStream:
    def __init__(self, mk, name, w_ap):
        self.mk = mk
        self.name = name
        self.w = w_ap
        self.K = w_ap.shape[0]
        self.F = w_ap.shape[1]

    def stream_cols(self, c0, c1):
        width = c1 - c0
        mk = self.mk
        ics_per_piece = max(1, mk.wring_cols // width)
        n_ic = self.K // 128
        pieces = {}
        emitted = {}

        def ensure(ic0):
            if ic0 in emitted:
                return
            emitted[ic0] = True
            ic1 = min(ic0 + ics_per_piece, n_ic)
            sl = mk.wring_slot % len(mk.wring)
            t = mk.wring[sl]
            mk.wring_slot += 1
            nic = ic1 - ic0
            src = self.w[ic0 * 128:ic1 * 128, c0:c1].rearrange(
                "(ic p) f -> p ic f", p=128)
            dst = t.ap(0, nic * width).rearrange("p (ic f) -> p ic f", f=width)
            mk.plan.emit(
                "gpsimd",
                (lambda e, dst=dst, src=src: e.dma_start(out=dst, in_=src)),
                reads=[], writes=t.keys(0, nic * width),
                dma=("wg", sl),
                tag=f"{self.name}[{ic0}:{ic1},{c0}:{c1}]")
            for ic in range(ic0, ic1):
                pieces[ic] = (t, ic - ic0, width)

        def acc(ic, cc0, cc1):
            ensure((ic // ics_per_piece) * ics_per_piece)
            t, icl, w = pieces[ic]
            a = t.ap(icl * w + cc0, icl * w + cc1)
            return a, t.keys(icl * w + cc0, icl * w + cc1)
        return acc


class MK:
    def __init__(self, part="all", taps=()):
        self.part = part
        self.taps = set(taps)
        nc = self.nc = bass.Bass("TRN2", target_bir_lowering=False)
        self.plan = Plan(nc)
        self.bf = Arena(nc, "abf", BF16, 76 * 1024)
        self.f32a = Arena(nc, "af32", F32, 13 * 1024 + 512)
        self.ps = PsumPool(nc)
        self.wring_cols = 7168
        self.wring = [self.bf.alloc(self.wring_cols, f"wring{i}") for i in range(3)]
        self.wring_slot = 0
        self.tapped = {}

        d = self.d = {}
        ein, eout, internal = {}, {}, {}

        def dram(name, shape, dtype, kind):
            d[name] = nc.dram_tensor(name, shape, dtype, kind=kind).ap()

        s1 = part in ("s1", "all")
        rest = part in ("rest", "all")
        if s1:
            dram("x0", [2048, DMODEL], F32, "ExternalInput")
            dram("w_l0qkv", [4096, 3072], F32, "ExternalInput")
            dram("b_l0qkv", [1, 3072], F32, "ExternalInput")
        if rest:
            dram("w_l0o", [1024, 4096], F32, "ExternalInput")
            dram("b_l0o", [1, 4096], F32, "ExternalInput")
            dram("w_l0base_t", [4096, 4096], F32, "ExternalInput")
            dram("b_l0base", [1, 4096], F32, "ExternalInput")
            dram("a_t", [4096, 64], F32, "ExternalInput")
            dram("b_flat", [1024, 4096], F32, "ExternalInput")
            dram("r_t", [4096, 16], F32, "ExternalInput")
            dram("w_l1qkv_t", [4096, 6144], F32, "ExternalInput")
            dram("w_l1proj_t", [4096, 4096], F32, "ExternalInput")
            dram("gate_w", [4096, 8], F32, "ExternalInput")
            dram("up_w", [4096, 8192], F32, "ExternalInput")
            dram("down_w", [8192, 4096], F32, "ExternalInput")
            dram("fc1_w", [4096, 2048], F32, "ExternalInput")
            dram("fc2_w", [2048, 4096], F32, "ExternalInput")
            dram("expert_w", [2048, 2048], F32, "ExternalInput")
            dram("ln_g", [1, 4096], F32, "ExternalInput")
            dram("ln_b", [1, 4096], F32, "ExternalInput")
            dram("out", [2048, DMODEL], F32, "ExternalOutput")
        if part == "s1":
            dram("kin", [4096, 512], BF16, "ExternalOutput")
            dram("vin", [2048, 1024], BF16, "ExternalOutput")
            dram("qdr", [2048, 1024], BF16, "ExternalOutput")
            dram("x0fm", [DMODEL, 2048], BF16, "ExternalOutput")
        elif part == "rest":
            dram("kout0", [16384, 512], BF16, "ExternalInput")
            dram("kout1", [16384, 512], BF16, "ExternalInput")
            dram("vout0", [8192, 1024], BF16, "ExternalInput")
            dram("vout1", [8192, 1024], BF16, "ExternalInput")
            dram("qdr", [2048, 1024], BF16, "ExternalInput")
            dram("x0fm", [DMODEL, 2048], BF16, "ExternalInput")
        else:
            dram("kin", [4096, 512], BF16, "Internal")
            dram("vin", [2048, 1024], BF16, "Internal")
            dram("kout0", [16384, 512], BF16, "Internal")
            dram("kout1", [16384, 512], BF16, "Internal")
            dram("vout0", [8192, 1024], BF16, "Internal")
            dram("vout1", [8192, 1024], BF16, "Internal")
            dram("qdr", [2048, 1024], BF16, "Internal")
            dram("x0fm", [DMODEL, 2048], BF16, "Internal")
        names = [("x1fm", [DMODEL, 2048]), ("x2fm", [DMODEL, 2048]),
                 ("x3fm", [DMODEL, 2048])]
        for p in range(2):
            names += [(f"prefm{p}", [4096, TP]), (f"whtm{p}", [1024, 1024]),
                      (f"hdr{p}", [8192, TP]), (f"l1q{p}", [6144, TP])]
        for name, shape in names:
            if rest:
                dram(name, shape, BF16, "Internal")
        for t in self.taps:
            dram("tap_" + t, list(d[t].shape), d[t].dtype, "ExternalOutput")

        self._init_consts(s1, rest)

    def E(self, eng, fn, reads=(), writes=(), dma=None, tag=""):
        return self.plan.emit(eng, fn, reads, writes, dma, tag)

    def tap(self, name, keys):
        if name in self.taps:
            src, dst = self.d[name], self.d["tap_" + name]
            self.E("sync", (lambda e, s=src, t=dst: e.dma_start(out=t, in_=s)),
                   reads=keys, writes=[("tap", name)], dma=("tap", None), tag=f"tap_{name}")

    def _init_consts(self, s1, rest):
        nc, d = self.nc, self.d
        self.id_bf = self.bf.alloc(512, "id_bf")
        self.id_f32 = self.f32a.alloc(512, "id_f32")
        for t, dt in ((self.id_bf, "idb"), (self.id_f32, "idf")):
            ap = t.ap(0, 128)
            self.E("gpsimd", (lambda e, a=ap: e.memset(a, 0.0)), writes=t.keys(0, 128), tag="id0")
            self.E("gpsimd",
                   (lambda e, a=ap: e.affine_select(
                       out=a, in_=a, compare_op=ALU.not_equal, fill=1.0,
                       base=0, pattern=[[-1, 128]], channel_multiplier=1)),
                   reads=t.keys(0, 128), writes=t.keys(0, 128), tag="id1")
        self.ones_row = self.bf.alloc(512, "ones_row")
        self.E("gpsimd", (lambda e, a=self.ones_row.ap(0, 128, 0, 1): e.memset(a, 1.0)),
               writes=self.ones_row.keys(0, 128), tag="ones_r")
        self.ones_col = self.bf.alloc(512, "ones_col")
        self.E("gpsimd", (lambda e, a=self.ones_col.ap(0, 1): e.memset(a, 1.0)),
               writes=self.ones_col.keys(0, 1), tag="ones_c")
        self.ones_col_f = self.f32a.alloc(512, "ones_col_f")
        self.E("gpsimd", (lambda e, a=self.ones_col_f.ap(0, 1): e.memset(a, 1.0)),
               writes=self.ones_col_f.keys(0, 1), tag="ones_cf")
        self.ones_row_f = self.f32a.alloc(512, "ones_row_f")
        self.E("gpsimd", (lambda e, a=self.ones_row_f.ap(0, 128, 0, 1): e.memset(a, 1.0)),
               writes=self.ones_row_f.keys(0, 128), tag="ones_rf")
        self.epsc = self.f32a.alloc(512, "epsc")
        self.E("gpsimd", (lambda e, a=self.epsc.ap(0, 1): e.memset(a, LN_EPS)),
               writes=self.epsc.keys(0, 1), tag="epsc")

        if s1:
            self.bqkv = self.f32a.alloc(512, "bqkv")
            src = d["b_l0qkv"].rearrange("o (c p) -> p (o c)", p=128)
            self.E("gpsimd", (lambda e, a=self.bqkv.ap(0, 24), s=src:
                              e.dma_start(out=a, in_=s, allow_slow_non_contiguous=True)),
                   writes=self.bqkv.keys(0, 24), dma=("cst", None), tag="ld_bqkv")
            self.bq_s = self.f32a.alloc(512, "bq_s")
            self.E("vector", (lambda e: e.tensor_scalar_mul(self.bq_s.ap(0, 8), self.bqkv.ap(0, 8), L0_SCALE)),
                   reads=self.bqkv.keys(0, 24), writes=self.bq_s.keys(0, 8), tag="bq_s")
            vrow = self.f32a.alloc(1024, "vrow")
            self.E("gpsimd", (lambda e, a=vrow.ap(0, 1024, 0, 1), s=d["b_l0qkv"][0:1, 2048:3072]:
                              e.dma_start(out=a, in_=s)),
                   writes=vrow.keys(), dma=("cst", None), tag="ld_vrow")
            vrow_bf = self.bf.alloc(1024, "vrow_bf")
            self.E("vector", (lambda e: e.tensor_copy(vrow_bf.ap(0, 1024, 0, 1), vrow.ap(0, 1024, 0, 1))),
                   reads=vrow.keys(), writes=vrow_bf.keys(), tag="vrow_bf")
            self.vbias = self.bf.alloc(1024, "vbias")
            for h in range(2):
                pk, pap = self.ps.f()
                self.E("tensor",
                       (lambda e, p=pap, r=vrow_bf.ap(h * 512, (h + 1) * 512, 0, 1):
                        e.matmul(p, self.ones_row.ap(0, 128, 0, 1), r, start=True, stop=True)),
                       reads=self.ones_row.keys(0, 128) + vrow_bf.keys(), writes=[pk], tag="vb_mm")
                self.E("scalar", (lambda e, p=pap, o=self.vbias.ap(h * 512, (h + 1) * 512): e.copy(o, p)),
                       reads=[pk], writes=self.vbias.keys(h * 512, (h + 1) * 512), tag="vb_ev")
            vrow.free()
            vrow_bf.free()
        if rest:
            self.bo_t = self.f32a.alloc(512, "bo_t")
            self.bb_t = self.f32a.alloc(512, "bb_t")
            self.gam_t = self.f32a.alloc(512, "gam_t")
            self.bet_t = self.f32a.alloc(512, "bet_t")
            for name, tile in (("b_l0o", self.bo_t), ("b_l0base", self.bb_t),
                               ("ln_g", self.gam_t), ("ln_b", self.bet_t)):
                src = d[name].rearrange("o (c p) -> p (o c)", p=128)
                self.E("gpsimd", (lambda e, a=tile.ap(0, 32), s=src:
                                  e.dma_start(out=a, in_=s, allow_slow_non_contiguous=True)),
                       writes=tile.keys(0, 32), dma=("cst", None), tag=f"ld_{name}")


    _st_ctr = 0

    def st(self):
        MK._st_ctr += 1
        return ("st", MK._st_ctr % 8)

    _wg_ctr = 0

    def wg(self):
        MK._wg_ctr += 1
        return ("wg", MK._wg_ctr % 6)

    def xtm_keys(self, name, p):
        return [(name, p, tt) for tt in range(8)]

    def s1(self, p, ws_qkv):
        d, E, ps = self.d, self.E, self.ps
        xin = [self.bf.alloc(4096, "xin") for _ in range(3)]
        xfm = [self.bf.alloc(1024, f"xfm{cc}") for cc in range(32)]
        for tt in range(8):
            b, q0 = tt // 4, (tt % 4) * 128
            t = xin[tt % 3]
            r0 = b * 1024 + p * 512 + q0
            src = d["x0"][r0:r0 + 128, :]
            E("gpsimd", (lambda e, a=t.ap(), s=src: e.dma_start(out=a, in_=s)),
              writes=t.keys(), dma=("xin", tt % 3), tag=f"xin{p}{tt}")
            for ccg in range(8):
                pkT, papT = ps.b()
                for j in range(4):
                    cc = ccg * 4 + j
                    E("tensor", (lambda e, o=papT[:, j * 128:(j + 1) * 128],
                                 i=t.ap(cc * 128, (cc + 1) * 128),
                                 idn=self.id_bf.ap(0, 128): e.transpose(o, i, idn)),
                      reads=t.keys(cc * 128, (cc + 1) * 128) + self.id_bf.keys(0, 128),
                      writes=[pkT], tag=f"x0t{tt}{ccg}")
                for j in range(4):
                    cc = ccg * 4 + j
                    dst = xfm[cc].ap(tt * 128, (tt + 1) * 128)
                    E("scalar", (lambda e, o=dst, i=papT[:, j * 128:(j + 1) * 128]: e.copy(o, i)),
                      reads=[pkT], writes=xfm[cc].keys(tt * 128, (tt + 1) * 128),
                      tag=f"x0te{tt}{ccg}")
        for cc in range(32):
            dst = d["x0fm"][cc * 128:(cc + 1) * 128, p * 1024:(p + 1) * 1024]
            E("sync", (lambda e, a=xfm[cc].ap(), o=dst: e.dma_start(out=o, in_=a)),
              reads=xfm[cc].keys(), writes=[("x0fm", p, cc)], dma=self.st(), tag=f"x0st{cc}")
        qkst = [self.bf.alloc(512, "qkst") for _ in range(4)]
        ogs = [(i, min(i + 3, 16)) for i in range(0, 16, 3)]
        for (lo, hi) in ogs:
            accw = ws_qkv.stream_cols(lo * 128, hi * 128)
            banks = ps.fgroup((hi - lo) * 2)
            for cc in range(32):
                wap, wkeys = None, None
                for j, dc in enumerate(range(lo, hi)):
                    lap, lkeys = accw(cc, j * 128, (j + 1) * 128)
                    for th in range(2):
                        pk, pap = banks[j * 2 + th]
                        E("tensor",
                          (lambda e, o=pap, l=lap, r=xfm[cc].ap(th * 512, (th + 1) * 512),
                           s=(cc == 0), t=(cc == 31): e.matmul(o, l, r, start=s, stop=t)),
                          reads=lkeys + xfm[cc].keys(th * 512, (th + 1) * 512),
                          writes=[pk], tag=f"qk{dc}{th}")
            for j, dc in enumerate(range(lo, hi)):
                for th in range(2):
                    pk, pap = banks[j * 2 + th]
                    stg = qkst[(dc * 2 + th) % 4]
                    if dc < 8:
                        scale, bias_ap, bias_keys = L0_SCALE, self.bq_s.ap(dc, dc + 1), self.bq_s.keys(0, 8)
                    else:
                        scale, bias_ap, bias_keys = 1.0, self.bqkv.ap(dc, dc + 1), self.bqkv.keys(0, 24)
                    E("scalar",
                      (lambda e, o=stg.ap(0, 512), i=pap, b=bias_ap, s=scale:
                       e.activation(o, i, AF.Identity, bias=b, scale=s)),
                      reads=[pk] + bias_keys, writes=stg.keys(0, 512), tag=f"qkev{dc}{th}")
                    if dc < 8:
                        r0 = p * 1024 + dc * 128
                        dst = d["qdr"][r0:r0 + 128, th * 512:(th + 1) * 512]
                        wkey = ("qdr", p, dc, th)
                    else:
                        r0 = p * 2048 + th * 1024 + (dc - 8) * 128
                        dst = d["kin"][r0:r0 + 128, :]
                        wkey = ("kin", p, th, dc - 8)
                    E("sync", (lambda e, a=stg.ap(0, 512), o=dst: e.dma_start(out=o, in_=a)),
                      reads=stg.keys(0, 512), writes=[wkey], dma=self.st(), tag=f"qkst{dc}{th}")
        for dvh in range(2):
            accw = ws_qkv.stream_cols(2048 + dvh * 512, 2048 + (dvh + 1) * 512)
            for tg in ((0, 6), (6, 8)):
                banks = ps.fgroup(tg[1] - tg[0])
                for cc in range(32):
                    rap, rkeys = accw(cc, 0, 512)
                    for j, tt in enumerate(range(*tg)):
                        pk, pap = banks[j]
                        E("tensor",
                          (lambda e, o=pap, l=xfm[cc].ap(tt * 128, (tt + 1) * 128), r=rap,
                           s=(cc == 0), t=(cc == 31): e.matmul(o, l, r, start=s, stop=t)),
                          reads=rkeys + xfm[cc].keys(tt * 128, (tt + 1) * 128),
                          writes=[pk], tag=f"v{tt}{dvh}")
                for j, tt in enumerate(range(*tg)):
                    pk, pap = banks[j]
                    stg = qkst[tt % 4]
                    E("vector",
                      (lambda e, o=stg.ap(0, 512), i=pap, b=self.vbias.ap(dvh * 512, (dvh + 1) * 512):
                       e.tensor_add(o, i, b)),
                      reads=[pk] + self.vbias.keys(), writes=stg.keys(0, 512), tag=f"vev{tt}")
                    r0 = p * 1024 + tt * 128
                    dst = d["vin"][r0:r0 + 128, dvh * 512:(dvh + 1) * 512]
                    E("sync", (lambda e, a=stg.ap(0, 512), o=dst: e.dma_start(out=o, in_=a)),
                      reads=stg.keys(0, 512), writes=[("vin", p, tt, dvh)], dma=self.st(),
                      tag=f"vst{tt}")
        for t in xin + xfm + qkst:
            t.free()

    def allgather(self, p):
        d, E = self.d, self.E
        kin_keys = [("kin", p, b, dk) for b in range(2) for dk in range(8)]
        vin_keys = [("vin", p, tt, dvh) for tt in range(8) for dvh in range(2)]
        kin = d["kin"][p * 2048:(p + 1) * 2048, :]
        vin = d["vin"][p * 1024:(p + 1) * 1024, :]
        kout, vout = d[f"kout{p}"], d[f"vout{p}"]

        def fk(e):
            return e.collective_compute(
                "AllGather", ALU.bypass, replica_groups=[list(range(NCORES))],
                ins=[kin.opt()], outs=[kout.opt()])

        def fv(e):
            return e.collective_compute(
                "AllGather", ALU.bypass, replica_groups=[list(range(NCORES))],
                ins=[vin.opt()], outs=[vout.opt()])
        self.plan.emit("gpsimd", fk, reads=kin_keys, writes=[("kout", p)], tag=f"ag_k{p}")
        self.plan.emit("gpsimd", fv, reads=vin_keys, writes=[("vout", p)], tag=f"ag_v{p}")

    def flash(self, p):
        d, E, ps = self.d, self.E, self.ps
        qf = [self.bf.alloc(1024, f"qf{dc}") for dc in range(8)]
        for dc in range(8):
            r0 = p * 1024 + dc * 128
            E("sync", (lambda e, a=qf[dc].ap(), s=d["qdr"][r0:r0 + 128, :]: e.dma_start(out=a, in_=s)),
              reads=[("qdr", p, dc, th) for th in range(2)], writes=qf[dc].keys(),
              dma=("qf", None), tag=f"qf{dc}")
        attn = [self.bf.alloc(1024, f"attn{dc}") for dc in range(8)]
        kvk = [self.bf.alloc(4096, "kvk") for _ in range(2)]
        kvv = [self.bf.alloc(4096, "kvv") for _ in range(2)]
        pbuf = [self.bf.alloc(512, "pbuf") for _ in range(4)]
        pt4 = [self.bf.alloc(512, "pt4") for _ in range(2)]
        acc = [self.f32a.alloc(1024, f"acc{qt}") for qt in range(4)]
        ml = self.f32a.alloc(512, "ml")
        mlk = ml.keys()

        def mlap(c):
            return ml.ap(c, c + 1)
        for b in range(2):
            for kc in range(16):
                pp, r = kc // 8, kc % 8
                sl = kc % 2
                r0 = r * 2048 + b * 1024
                src = d[f"kout{pp}"][r0:r0 + 1024, :].rearrange("(dc p) k -> p dc k", p=128)
                E("sync", (lambda e, a=kvk[sl].ap(), s=src: e.dma_start(out=a, in_=s)),
                  reads=[("kout", pp)], writes=kvk[sl].keys(), dma=("kvk", sl), tag=f"kld{b}{kc}")
                r0 = r * 1024 + b * 512
                src = d[f"vout{pp}"][r0:r0 + 512, :].rearrange("(s p) v -> p s v", p=128)
                E("sync", (lambda e, a=kvv[sl].ap(), s=src: e.dma_start(out=a, in_=s)),
                  reads=[("vout", pp)], writes=kvv[sl].keys(), dma=("kvv", sl), tag=f"vld{b}{kc}")
                sbanks = []
                for qt in range(4):
                    pkS, papS = ps.f()
                    sbanks.append((pkS, papS))
                    for dc in range(8):
                        E("tensor",
                          (lambda e, o=papS, l=qf[dc].ap(b * 512 + qt * 128, b * 512 + (qt + 1) * 128),
                           r_=kvk[sl].ap(dc * 512, (dc + 1) * 512), s=(dc == 0), t=(dc == 7):
                           e.matmul(o, l, r_, start=s, stop=t)),
                          reads=qf[dc].keys() + kvk[sl].keys(dc * 512, (dc + 1) * 512),
                          writes=[pkS], tag=f"sc{b}{kc}{qt}")
                for qt in range(4):
                    pkS, papS = sbanks[qt]
                    par = kc % 2
                    mq, mq_old = mlap(qt * 2 + par), mlap(qt * 2 + 1 - par)
                    lq, lq_old = mlap(8 + qt * 2 + par), mlap(8 + qt * 2 + 1 - par)
                    nm, rs, al = mlap(16 + qt), mlap(20 + qt), mlap(24 + qt)
                    P = pbuf[qt]
                    if kc == 0:
                        E("vector", (lambda e, o=mq, i=papS: e.reduce_max(o, i, axis=AX.X)),
                          reads=[pkS], writes=mlk, tag="fm0")
                        E("vector", (lambda e, o=nm, i=mq: e.tensor_scalar_mul(o, i, -1.0)),
                          reads=mlk, writes=mlk, tag="fm1")
                        E("scalar", (lambda e, o=P.ap(0, 512), i=papS, bi=nm, ac=lq:
                                     e.activation(o, i, AF.Exp, bias=bi, accum_out=ac)),
                          reads=[pkS] + mlk, writes=P.keys() + mlk, tag="fexp0")
                    else:
                        cm = mlap(28 + qt)
                        E("vector", (lambda e, o=cm, i=papS: e.reduce_max(o, i, axis=AX.X)),
                          reads=[pkS], writes=mlk, tag="fcm")
                        E("vector", (lambda e, o=mq, a=mq_old, b_=cm: e.tensor_max(o, a, b_)),
                          reads=mlk, writes=mlk, tag="fmx")
                        E("vector", (lambda e, o=nm, i=mq: e.tensor_scalar_mul(o, i, -1.0)),
                          reads=mlk, writes=mlk, tag="fnm")
                        E("scalar", (lambda e, o=al, i=mq_old, bi=nm: e.activation(o, i, AF.Exp, bias=bi)),
                          reads=mlk, writes=mlk, tag="fal")
                        E("scalar", (lambda e, o=P.ap(0, 512), i=papS, bi=nm, ac=rs:
                                     e.activation(o, i, AF.Exp, bias=bi, accum_out=ac)),
                          reads=[pkS] + mlk, writes=P.keys() + mlk, tag="fexp")
                        E("vector", (lambda e, o=lq, i=lq_old, s=al, t=rs:
                                     e.scalar_tensor_tensor(o, i, s, t, ALU.mult, ALU.add)),
                          reads=mlk, writes=mlk, tag="fl")
                ptdone = {}

                def emit_pt(qt):
                    P = pbuf[qt]
                    pkT, papT = ps.b()
                    for ks in range(4):
                        E("tensor", (lambda e, o=papT[:, ks * 128:(ks + 1) * 128],
                                     i=P.ap(ks * 128, (ks + 1) * 128),
                                     idn=self.id_bf.ap(0, 128): e.transpose(o, i, idn)),
                          reads=P.keys() + self.id_bf.keys(0, 128), writes=[pkT], tag="fpt")
                    pt = pt4[qt % 2]
                    E("scalar", (lambda e, o=pt.ap(0, 512), i=papT[:, 0:512]: e.copy(o, i)),
                      reads=[pkT], writes=pt.keys(0, 512), tag="fptev")
                    ptdone[qt] = pt

                def emit_av(qt):
                    par = kc % 2
                    al = mlap(24 + qt)
                    pt = ptdone[qt]
                    for dvh in range(2):
                        pkA, papA = ps.f()
                        for ks in range(4):
                            E("tensor",
                              (lambda e, o=papA, l=pt.ap(ks * 128, (ks + 1) * 128),
                               r_=kvv[sl].ap(ks * 1024 + dvh * 512, ks * 1024 + (dvh + 1) * 512),
                               s=(ks == 0), t=(ks == 3): e.matmul(o, l, r_, start=s, stop=t)),
                              reads=pt.keys(ks * 128, (ks + 1) * 128) +
                              kvv[sl].keys(ks * 1024 + dvh * 512, ks * 1024 + (dvh + 1) * 512),
                              writes=[pkA], tag=f"av{b}{kc}{qt}{dvh}")
                        a_ap = acc[qt].ap(dvh * 512, (dvh + 1) * 512)
                        a_keys = acc[qt].keys(dvh * 512, (dvh + 1) * 512)
                        if kc == 0:
                            E("vector", (lambda e, o=a_ap, i=papA: e.tensor_copy(o, i)),
                              reads=[pkA], writes=a_keys, tag="facc0")
                        else:
                            E("vector", (lambda e, o=a_ap, i=papA, s=al:
                                         e.scalar_tensor_tensor(o, o, s, i, ALU.mult, ALU.add)),
                              reads=[pkA] + a_keys + mlk, writes=a_keys, tag="facc")
                emit_pt(0)
                emit_pt(1)
                emit_av(0)
                emit_pt(2)
                emit_av(1)
                emit_pt(3)
                emit_av(2)
                emit_av(3)
            for qt in range(4):
                lq = mlap(8 + qt * 2 + 1)
                rl = mlap(28 + qt)
                E("vector", (lambda e, o=rl, i=lq: e.reciprocal(o, i)), reads=mlk, writes=mlk, tag="frl")
                ao2 = self.bf.alloc(1024, "ao2")
                E("vector", (lambda e, o=ao2.ap(), i=acc[qt].ap(), s=rl:
                             e.tensor_scalar_mul(o, i, s)),
                  reads=acc[qt].keys() + mlk, writes=ao2.keys(), tag="fnorm")
                for half in range(2):
                    pkT, papT = ps.b()
                    for dc2 in range(4):
                        dc = half * 4 + dc2
                        E("tensor", (lambda e, o=papT[:, dc2 * 128:(dc2 + 1) * 128],
                                     i=ao2.ap(dc * 128, (dc + 1) * 128),
                                     idn=self.id_bf.ap(0, 128): e.transpose(o, i, idn)),
                          reads=ao2.keys() + self.id_bf.keys(0, 128), writes=[pkT], tag="faot")
                    for dc2 in range(4):
                        dc = half * 4 + dc2
                        dst = attn[dc].ap(b * 512 + qt * 128, b * 512 + (qt + 1) * 128)
                        E("scalar", (lambda e, o=dst, i=papT[:, dc2 * 128:(dc2 + 1) * 128]: e.copy(o, i)),
                          reads=[pkT], writes=attn[dc].keys(b * 512 + qt * 128, b * 512 + (qt + 1) * 128),
                          tag="faotev")
                ao2.free()
        for t in qf + kvk + kvv + pbuf + pt4 + acc + [ml]:
            t.free()
        return attn

    def _ln_stats(self, t, stat, sk, rstd_col, negmr_col):
        E = self.E
        for s8 in range(8):
            E("vector", (lambda e, o=stat.ap(s8 * 6, s8 * 6 + 6), i=t.ap(s8 * 512, (s8 + 1) * 512):
                         e.bn_stats(o, i)),
              reads=t.keys(s8 * 512, (s8 + 1) * 512), writes=sk, tag="bns")
        E("vector", (lambda e, o=stat.ap(48, 50), i=stat.ap(0, 48): e.bn_aggr(o, i)),
          reads=sk, writes=sk, tag="bna")
        E("scalar", (lambda e, o=stat.ap(rstd_col, rstd_col + 1), i=stat.ap(49, 50),
                     b=self.epsc.ap(0, 1): e.activation(o, i, AF.Ln, bias=b)),
          reads=sk + self.epsc.keys(0, 1), writes=sk, tag="lnv")
        E("scalar", (lambda e, o=stat.ap(rstd_col, rstd_col + 1), i=stat.ap(rstd_col, rstd_col + 1):
                     e.activation(o, i, AF.Exp, scale=-0.5)),
          reads=sk, writes=sk, tag="rstd")
        E("vector", (lambda e, o=stat.ap(negmr_col, negmr_col + 1), i=stat.ap(48, 49),
                     r=stat.ap(rstd_col, rstd_col + 1):
                     e.tensor_scalar(o, i, r, -1.0, ALU.mult, ALU.mult)),
          reads=sk, writes=sk, tag="negmr")

    def ln_begin(self, tag):
        st = {}
        st["srow"] = self.f32a.alloc(2048, f"lns_{tag}")
        st["sq"] = [self.bf.alloc(512, "lnsq") for _ in range(2)]
        st["rbf"] = self.f32a.alloc(1024, f"lnrb_{tag}")
        st["nbf"] = self.f32a.alloc(1024, f"lnnb_{tag}")
        st["rowbf"] = self.bf.alloc(512, "lnrow")
        self.E("vector", (lambda e, a=st["srow"].ap(0, 2048, 0, 1): e.memset(a, 0.0)),
               writes=st["srow"].keys(), tag=f"lnz_{tag}")
        return st

    def ln_chunk(self, st, pr_ap, pr_keys, th):
        E, ps = self.E, self.ps
        sqt = st["sq"][th]
        E("vector", (lambda e, o=sqt.ap(0, 512), a=pr_ap: e.tensor_mul(o, a, a)),
          reads=pr_keys, writes=sqt.keys(), tag="lnsq")
        for rhs_ap, rhs_keys, col0 in ((pr_ap, pr_keys, th * 512),
                                       (sqt.ap(0, 512), sqt.keys(), 1024 + th * 512)):
            pk, pap = ps.r()
            E("tensor", (lambda e, o=pap[0:1, :], l=self.ones_col.ap(0, 1), r=rhs_ap:
                         e.matmul(o, l, r, start=True, stop=True)),
              reads=self.ones_col.keys(0, 1) + rhs_keys, writes=[pk], tag="lnmm")
            E("vector", (lambda e, o=st["srow"].ap(col0, col0 + 512, 0, 1),
                         a=st["srow"].ap(col0, col0 + 512, 0, 1), b=pap[0:1, :]:
                         e.tensor_add(o, a, b)),
              reads=[pk] + st["srow"].keys(), writes=st["srow"].keys(), tag="lnacc")

    def ln_finalize(self, st):
        E, ps = self.E, self.ps
        srow, rowbf = st["srow"], st["rowbf"]
        sk = srow.keys()
        E("vector", (lambda e, o=srow.ap(0, 1024, 0, 1), i=srow.ap(0, 1024, 0, 1):
                     e.tensor_scalar_mul(o, i, 1.0 / 4096.0)),
          reads=sk, writes=sk, tag="lnmean")
        E("vector", (lambda e, o=srow.ap(1024, 2048, 0, 1), i=srow.ap(1024, 2048, 0, 1):
                     e.tensor_scalar_mul(o, i, 1.0 / 4096.0)),
          reads=sk, writes=sk, tag="lne2")
        msq = self.f32a.alloc(1024, "lnmsq")
        E("vector", (lambda e, o=msq.ap(0, 1024, 0, 1), a=srow.ap(0, 1024, 0, 1),
                     b=srow.ap(0, 1024, 0, 1): e.tensor_mul(o, a, b)),
          reads=sk, writes=msq.keys(), tag="lnm2")
        E("vector", (lambda e, o=srow.ap(1024, 2048, 0, 1), a=srow.ap(1024, 2048, 0, 1),
                     b=msq.ap(0, 1024, 0, 1): e.tensor_sub(o, a, b)),
          reads=sk + msq.keys(), writes=sk, tag="lnvar")
        msq.free()
        E("scalar", (lambda e, o=srow.ap(1024, 2048, 0, 1), i=srow.ap(1024, 2048, 0, 1),
                     b=self.epsc.ap(0, 1, 0, 1): e.activation(o, i, AF.Ln, bias=b)),
          reads=sk + self.epsc.keys(0, 1), writes=sk, tag="lnln")
        E("scalar", (lambda e, o=srow.ap(1024, 2048, 0, 1), i=srow.ap(1024, 2048, 0, 1):
                     e.activation(o, i, AF.Exp, scale=-0.5)),
          reads=sk, writes=sk, tag="lnexp")
        E("vector", (lambda e, o=srow.ap(0, 1024, 0, 1), a=srow.ap(0, 1024, 0, 1),
                     b=srow.ap(1024, 2048, 0, 1): e.tensor_mul(o, a, b)),
          reads=sk, writes=sk, tag="lnnm1")
        E("vector", (lambda e, o=srow.ap(0, 1024, 0, 1), i=srow.ap(0, 1024, 0, 1):
                     e.tensor_scalar_mul(o, i, -1.0)),
          reads=sk, writes=sk, tag="lnnm2")
        for srccol, dsttile in ((1024, st["rbf"]), (0, st["nbf"])):
            for th in range(2):
                pk, pap = ps.f()
                E("tensor", (lambda e, o=pap, l=self.ones_row_f.ap(0, 128, 0, 1),
                             r=srow.ap(srccol + th * 512, srccol + (th + 1) * 512, 0, 1):
                             e.matmul(o, l, r, start=True, stop=True)),
                  reads=self.ones_row_f.keys(0, 128) + sk, writes=[pk], tag="lnbc")
                E("scalar", (lambda e, o=dsttile.ap(th * 512, (th + 1) * 512), i=pap: e.copy(o, i)),
                  reads=[pk], writes=dsttile.keys(th * 512, (th + 1) * 512), tag="lnbce")

    def ln_apply(self, st, p, src_name, dst_name, free=True):
        d, E = self.d, self.E
        pre_keys = [("prefm", p, oc, th) for oc in range(32) for th in range(2)]
        xin = [self.bf.alloc(1024, "lnxin") for _ in range(3)]
        zf = [self.f32a.alloc(1024, "lnzf") for _ in range(2)]
        zo = [self.bf.alloc(1024, "lnzo") for _ in range(2)]
        for cc in range(32):
            t = xin[cc % 3]
            E("sync", (lambda e, a=t.ap(), s=d[src_name][cc * 128:(cc + 1) * 128, :]:
                       e.dma_start(out=a, in_=s)),
              reads=pre_keys, writes=t.keys(), dma=("lnld", cc % 3), tag=f"lnld{cc}")
            z = zf[cc % 2]
            E("vector", (lambda e, o=z.ap(), a=t.ap(), b=st["rbf"].ap(0, 1024):
                         e.tensor_mul(o, a, b)),
              reads=t.keys() + st["rbf"].keys(), writes=z.keys(), tag=f"lnap1_{cc}")
            o2 = zo[cc % 2]
            E("vector", (lambda e, o=o2.ap(), a=z.ap(), b=st["nbf"].ap(0, 1024):
                         e.tensor_add(o, a, b)),
              reads=z.keys() + st["nbf"].keys(), writes=o2.keys(), tag=f"lnap2_{cc}")
            dst = d[dst_name][cc * 128:(cc + 1) * 128, p * 1024:(p + 1) * 1024]
            E("sync", (lambda e, a=o2.ap(), o=dst: e.dma_start(out=o, in_=a)),
              reads=o2.keys(), writes=[(dst_name, p, cc)], dma=self.st(), tag=f"lnst{cc}")
        for t in xin + zf + zo:
            t.free()
        if free:
            for k in ("srow", "rbf", "nbf", "rowbf"):
                st[k].free()
            for t in st["sq"]:
                t.free()

    def proj_to_prefm(self, p, ws, attn_tiles, bias_tile, bias_keys, res_name, lnst):
        d, E, ps = self.d, self.E, self.ps
        nic = len(attn_tiles)
        t1r = [self.f32a.alloc(512, "t1r") for _ in range(2)]
        resr = [self.bf.alloc(512, "resr") for _ in range(4)]
        prest = [self.bf.alloc(512, "prest") for _ in range(4)]
        ogs = [(i, min(i + 3, 32)) for i in range(0, 32, 3)]
        for gi, (lo, hi) in enumerate(ogs):
            accw = ws.stream_cols(lo * 128, hi * 128)
            banks = ps.fgroup((hi - lo) * 2)
            for ic in range(nic):
                for j, oc in enumerate(range(lo, hi)):
                    lap, lkeys = accw(ic, j * 128, (j + 1) * 128)
                    for th in range(2):
                        pk, pap = banks[j * 2 + th]
                        E("tensor",
                          (lambda e, o=pap, l=lap, r=attn_tiles[ic].ap(th * 512, (th + 1) * 512),
                           s=(ic == 0), t=(ic == nic - 1): e.matmul(o, l, r, start=s, stop=t)),
                          reads=lkeys + attn_tiles[ic].keys(th * 512, (th + 1) * 512),
                          writes=[pk], tag=f"pj{oc}{th}")
            for j, oc in enumerate(range(lo, hi)):
                for th in range(2):
                    pk, pap = banks[j * 2 + th]
                    res = resr[(oc * 2 + th) % 4]
                    src = d[res_name][oc * 128:(oc + 1) * 128,
                                      p * 1024 + th * 512:p * 1024 + (th + 1) * 512]
                    E("sync", (lambda e, a=res.ap(0, 512), s=src: e.dma_start(out=a, in_=s)),
                      reads=[(res_name, p, oc)], writes=res.keys(0, 512),
                      dma=("res", (oc * 2 + th) % 4), tag=f"res{oc}{th}")
                    pr = prest[(oc * 2 + th) % 4]
                    if bias_tile is not None:
                        t1 = t1r[th]
                        E("scalar", (lambda e, o=t1.ap(0, 512), i=pap, b=bias_tile.ap(oc, oc + 1):
                                     e.activation(o, i, AF.Identity, bias=b)),
                          reads=[pk] + bias_keys, writes=t1.keys(0, 512), tag=f"pjb{oc}{th}")
                        E("vector", (lambda e, o=pr.ap(0, 512), i=t1.ap(0, 512), r=res.ap(0, 512):
                                     e.tensor_add(o, i, r)),
                          reads=t1.keys(0, 512) + res.keys(0, 512), writes=pr.keys(0, 512),
                          tag=f"pjr{oc}{th}")
                    else:
                        E("vector", (lambda e, o=pr.ap(0, 512), i=pap, r=res.ap(0, 512):
                                     e.tensor_add(o, i, r)),
                          reads=[pk] + res.keys(0, 512), writes=pr.keys(0, 512), tag=f"pjr{oc}{th}")
                    self.ln_chunk(lnst, pr.ap(0, 512), pr.keys(0, 512), th)
                    dst = d[f"prefm{p}"][oc * 128:(oc + 1) * 128, th * 512:(th + 1) * 512]
                    E("sync", (lambda e, a=pr.ap(0, 512), o=dst: e.dma_start(out=o, in_=a)),
                      reads=pr.keys(0, 512), writes=[("prefm", p, oc, th)], dma=self.st(),
                      tag=f"pjst{oc}{th}")
        for t in t1r + resr + prest:
            t.free()

    def l0_attn(self, p, ws_o):
        attn = self.flash(p)
        lnst = self.ln_begin(f"l0a{p}")
        self.proj_to_prefm(p, ws_o, attn, self.bo_t, self.bo_t.keys(0, 32), "x0fm", lnst)
        for t in attn:
            t.free()
        self.ln_finalize(lnst)
        self.ln_apply(lnst, p, f"prefm{p}", "x1fm")

    def load_fm(self, name, p, fam):
        d, E = self.d, self.E
        xfm = [self.bf.alloc(1024, f"{name}fm{cc}") for cc in range(32)]
        for cc in range(32):
            src = d[name][cc * 128:(cc + 1) * 128, p * 1024:(p + 1) * 1024]
            E("sync", (lambda e, a=xfm[cc].ap(), s=src: e.dma_start(out=a, in_=s)),
              reads=[(name, p, cc)], writes=xfm[cc].keys(), dma=(fam, None),
              tag=f"{name}fm{cc}")
        return xfm

    def l0_ffn(self, p, ws_base, ws_bflat, rt_tile, at_tile):
        d, E, ps = self.d, self.E, self.ps
        xfm = self.load_fm("x1fm", p, "xfm")
        lf = self.f32a.alloc(1024, "lf")
        for th in range(2):
            pk, pap = ps.f()
            for cc in range(32):
                E("tensor",
                  (lambda e, o=pap[0:16, :], l=rt_tile.ap(cc * 16, (cc + 1) * 16),
                   r=xfm[cc].ap(th * 512, (th + 1) * 512), s=(cc == 0), t=(cc == 31):
                   e.matmul(o, l, r, start=s, stop=t)),
                  reads=rt_tile.keys() + xfm[cc].keys(th * 512, (th + 1) * 512),
                  writes=[pk], tag="rl")
            E("scalar", (lambda e, o=lf.ap(th * 512, (th + 1) * 512, 0, 16), i=pap[0:16, :]:
                         e.copy(o, i)),
              reads=[pk], writes=lf.keys(th * 512, (th + 1) * 512), tag="rlev")
        comb = [self.f32a.alloc(512, f"comb{tt}") for tt in range(8)]
        for tt in range(8):
            pk, pap = ps.f()
            E("tensor", (lambda e, o=pap[:, 0:16], i=lf.ap(tt * 128, (tt + 1) * 128, 0, 16),
                         idn=self.id_f32.ap(0, 16, 0, 16): e.transpose(o, i, idn)),
              reads=lf.keys() + self.id_f32.keys(0, 128), writes=[pk], tag="rltr")
            c = comb[tt]
            ck = c.keys()
            E("scalar", (lambda e, o=c.ap(0, 16), i=pap[:, 0:16]: e.copy(o, i)),
              reads=[pk], writes=ck, tag="rltm")
            E("vector", (lambda e, o=c.ap(48, 49), i=c.ap(0, 16): e.reduce_max(o, i, axis=AX.X)),
              reads=ck, writes=ck, tag="rm1")
            E("vector", (lambda e, o=c.ap(53, 54), i=c.ap(48, 49): e.tensor_scalar_mul(o, i, -1.0)),
              reads=ck, writes=ck, tag="rnm")
            E("scalar", (lambda e, o=c.ap(16, 32), i=c.ap(0, 16), b=c.ap(53, 54), ac=c.ap(49, 50):
                         e.activation(o, i, AF.Exp, bias=b, accum_out=ac)),
              reads=ck, writes=ck, tag="rexp")
            E("vector", (lambda e, o=c.ap(32, 48), i=c.ap(0, 16), s=c.ap(48, 49):
                         e.tensor_scalar(o, i, s, None, ALU.is_equal)),
              reads=ck, writes=ck, tag="req")
            E("vector", (lambda e, o=c.ap(32, 48), i=c.ap(32, 48), t2=c.ap(0, 16):
                         e.scalar_tensor_tensor(o, i, -1e30, t2, ALU.mult, ALU.add)),
              reads=ck, writes=ck, tag="rmsk")
            E("vector", (lambda e, o=c.ap(50, 51), i=c.ap(32, 48): e.reduce_max(o, i, axis=AX.X)),
              reads=ck, writes=ck, tag="rm2")
            E("vector", (lambda e, o=c.ap(32, 48), i=c.ap(0, 16), s=c.ap(50, 51):
                         e.tensor_scalar(o, i, s, None, ALU.is_ge)),
              reads=ck, writes=ck, tag="rind")
            E("vector", (lambda e, o=c.ap(16, 32), i=c.ap(16, 32), t2=c.ap(32, 48):
                         e.tensor_mul(o, i, t2)),
              reads=ck, writes=ck, tag="rcu")
            E("vector", (lambda e, o=c.ap(51, 52), i=c.ap(16, 32): e.reduce_sum(o, i, axis=AX.X)),
              reads=ck, writes=ck, tag="rs")
            E("vector", (lambda e, o=c.ap(52, 53), i=c.ap(49, 50), t2=c.ap(51, 52):
                         e.scalar_tensor_tensor(o, i, 1e-6, t2, ALU.mult, ALU.add)),
              reads=ck, writes=ck, tag="rden")
            E("vector", (lambda e, o=c.ap(52, 53), i=c.ap(52, 53): e.reciprocal(o, i)),
              reads=ck, writes=ck, tag="rrcp")
            E("vector", (lambda e, o=c.ap(32, 48), i=c.ap(16, 32), s=c.ap(52, 53):
                         e.tensor_scalar(o, i, s, 0.5, ALU.mult, ALU.mult)),
              reads=ck, writes=ck, tag="rcomb")
        lf.free()
        rhfm = self.bf.alloc(1024, "rhfm")
        for th in range(2):
            pk, pap = ps.f()
            for cc in range(32):
                E("tensor",
                  (lambda e, o=pap[0:64, :], l=at_tile.ap(cc * 64, (cc + 1) * 64),
                   r=xfm[cc].ap(th * 512, (th + 1) * 512), s=(cc == 0), t=(cc == 31):
                   e.matmul(o, l, r, start=s, stop=t)),
                  reads=at_tile.keys() + xfm[cc].keys(th * 512, (th + 1) * 512),
                  writes=[pk], tag="rh")
            E("scalar", (lambda e, o=rhfm.ap(th * 512, (th + 1) * 512, 0, 64), i=pap[0:64, :]:
                         e.copy(o, i)),
              reads=[pk], writes=rhfm.keys(th * 512, (th + 1) * 512), tag="rhev")
        whst = [self.bf.alloc(1024, "whst") for _ in range(2)]
        rhtm = self.bf.alloc(512, "rhtm")
        for tt in range(8):
            pk, pap = ps.b()
            E("tensor", (lambda e, o=pap[:, 0:64], i=rhfm.ap(tt * 128, (tt + 1) * 128, 0, 64),
                         idn=self.id_bf.ap(0, 64, 0, 64): e.transpose(o, i, idn)),
              reads=rhfm.keys() + self.id_bf.keys(0, 128), writes=[pk], tag="rhtr")
            E("scalar", (lambda e, o=rhtm.ap(0, 64), i=pap[:, 0:64]: e.copy(o, i)),
              reads=[pk], writes=rhtm.keys(0, 64), tag="rhtm")
            w = whst[tt % 2]
            for ei in range(16):
                E("vector", (lambda e, o=w.ap(ei * 64, (ei + 1) * 64), i=rhtm.ap(0, 64),
                             s=comb[tt].ap(32 + ei, 33 + ei):
                             e.tensor_scalar(o, i, s, None, ALU.mult)),
                  reads=rhtm.keys(0, 64) + comb[tt].keys(), writes=w.keys(ei * 64, (ei + 1) * 64),
                  tag=f"wh{tt}{ei}")
            E("sync", (lambda e, a=w.ap(), o=d[f"whtm{p}"][tt * 128:(tt + 1) * 128, :]:
                       e.dma_start(out=o, in_=a)),
              reads=w.keys(), writes=[("whtm", p, tt)], dma=self.st(), tag=f"whst{tt}")
        rhfm.free(); rhtm.free()
        for t in whst:
            t.free()
        whfm = [self.bf.alloc(1024, f"whfm{ec}") for ec in range(8)]
        for ec in range(8):
            src = d[f"whtm{p}"][:, ec * 128:(ec + 1) * 128]
            E("sync", (lambda e, a=whfm[ec].ap(), s=src: e.dma_start_transpose(out=a, in_=s)),
              reads=[("whtm", p, tt) for tt in range(8)], writes=whfm[ec].keys(),
              dma=("whfm", None), tag=f"whfm{ec}")
        lnst = self.ln_begin(f"l0f{p}")
        prest = [self.bf.alloc(512, "prest") for _ in range(4)]
        t1r = [self.f32a.alloc(512, "t1r") for _ in range(2)]
        ogs = [(i, min(i + 3, 32)) for i in range(0, 32, 3)]
        for (lo, hi) in ogs:
            accb = ws_base.stream_cols(lo * 128, hi * 128)
            accd = ws_bflat.stream_cols(lo * 128, hi * 128)
            banks = ps.fgroup((hi - lo) * 2)
            for ic in range(32):
                for j, oc in enumerate(range(lo, hi)):
                    lap, lkeys = accb(ic, j * 128, (j + 1) * 128)
                    for th in range(2):
                        pk, pap = banks[j * 2 + th]
                        E("tensor",
                          (lambda e, o=pap, l=lap, r=xfm[ic].ap(th * 512, (th + 1) * 512),
                           s=(ic == 0): e.matmul(o, l, r, start=s, stop=False)),
                          reads=lkeys + xfm[ic].keys(th * 512, (th + 1) * 512),
                          writes=[pk], tag=f"bs{oc}{th}")
            for ic in range(8):
                for j, oc in enumerate(range(lo, hi)):
                    lap, lkeys = accd(ic, j * 128, (j + 1) * 128)
                    for th in range(2):
                        pk, pap = banks[j * 2 + th]
                        E("tensor",
                          (lambda e, o=pap, l=lap, r=whfm[ic].ap(th * 512, (th + 1) * 512),
                           t=(ic == 7): e.matmul(o, l, r, start=False, stop=t)),
                          reads=lkeys + whfm[ic].keys(th * 512, (th + 1) * 512),
                          writes=[pk], tag=f"dl{oc}{th}")
            for j, oc in enumerate(range(lo, hi)):
                for th in range(2):
                    pk, pap = banks[j * 2 + th]
                    t1 = t1r[th]
                    E("scalar", (lambda e, o=t1.ap(0, 512), i=pap, b=self.bb_t.ap(oc, oc + 1):
                                 e.activation(o, i, AF.Identity, bias=b)),
                      reads=[pk] + self.bb_t.keys(0, 32), writes=t1.keys(0, 512), tag=f"bb{oc}{th}")
                    pr = prest[(oc * 2 + th) % 4]
                    E("vector", (lambda e, o=pr.ap(0, 512), i=t1.ap(0, 512),
                                 r=xfm[oc].ap(th * 512, (th + 1) * 512):
                                 e.tensor_add(o, i, r)),
                      reads=t1.keys(0, 512) + xfm[oc].keys(th * 512, (th + 1) * 512),
                      writes=pr.keys(0, 512), tag=f"br{oc}{th}")
                    self.ln_chunk(lnst, pr.ap(0, 512), pr.keys(0, 512), th)
                    dst = d[f"prefm{p}"][oc * 128:(oc + 1) * 128, th * 512:(th + 1) * 512]
                    E("sync", (lambda e, a=pr.ap(0, 512), o=dst: e.dma_start(out=o, in_=a)),
                      reads=pr.keys(0, 512), writes=[("prefm", p, oc, th)], dma=self.st(),
                      tag=f"bst{oc}{th}")
        for t in xfm + whfm + comb + prest + t1r:
            t.free()
        self.ln_finalize(lnst)
        self.ln_apply(lnst, p, f"prefm{p}", "x2fm")

    def l1_attn(self, p, ws_qkv, ws_proj):
        d, E, ps = self.d, self.E, self.ps
        xfm = self.load_fm("x2fm", p, "xfm")
        qst = [self.bf.alloc(512, "qst") for _ in range(4)]
        ogs = [(i, min(i + 3, 48)) for i in range(0, 48, 3)]
        for (lo, hi) in ogs:
            accw = ws_qkv.stream_cols(lo * 128, hi * 128)
            banks = ps.fgroup((hi - lo) * 2)
            for hc in range(32):
                for j, pc in enumerate(range(lo, hi)):
                    lap, lkeys = accw(hc, j * 128, (j + 1) * 128)
                    for th in range(2):
                        pk, pap = banks[j * 2 + th]
                        E("tensor",
                          (lambda e, o=pap, l=lap, r=xfm[hc].ap(th * 512, (th + 1) * 512),
                           s=(hc == 0), t=(hc == 31): e.matmul(o, l, r, start=s, stop=t)),
                          reads=lkeys + xfm[hc].keys(th * 512, (th + 1) * 512),
                          writes=[pk], tag=f"qkv1_{pc}{th}")
            for j, pc in enumerate(range(lo, hi)):
                for th in range(2):
                    pk, pap = banks[j * 2 + th]
                    stg = qst[(pc * 2 + th) % 4]
                    E("scalar", (lambda e, o=stg.ap(0, 512), i=pap: e.copy(o, i)),
                      reads=[pk], writes=stg.keys(0, 512), tag=f"qkv1e{pc}{th}")
                    dst = d[f"l1q{p}"][pc * 128:(pc + 1) * 128, th * 512:(th + 1) * 512]
                    E("sync", (lambda e, a=stg.ap(0, 512), o=dst: e.dma_start(out=o, in_=a)),
                      reads=stg.keys(0, 512), writes=[("l1q", p, pc, th)], dma=self.st(),
                      tag=f"qkv1s{pc}{th}")
        for t in xfm + qst:
            t.free()
        attn32 = [self.bf.alloc(1024, f"a32_{h}") for h in range(32)]
        qg = [[self.bf.alloc(1024, "qg") for _ in range(6)] for _ in range(2)]
        scr = self.bf.alloc(2048, "l1scr")
        scrf = self.f32a.alloc(2048, "l1scrf")
        srow = self.f32a.alloc(1024, "srow")
        wrow = self.bf.alloc(512, "wrow")
        for g in range(8):
            qs = qg[g % 2]
            for pc6 in range(6):
                pc = g * 6 + pc6
                src_ = d[f"l1q{p}"][pc * 128:(pc + 1) * 128, :]
                E("sync", (lambda e, a=qs[pc6].ap(), s=src_: e.dma_start(out=a, in_=s)),
                  reads=[("l1q", p, pc, th) for th in range(2)], writes=qs[pc6].keys(),
                  dma=("l1ld", g % 3, (p, g)), tag=f"l1ld{pc}")
            k, v = qs[4], qs[5]
            E("vector", (lambda e, o=scrf.ap(0, 512), a=k.ap(512, 1024), b=k.ap(0, 512):
                         e.tensor_sub(o, a, b)),
              reads=k.keys(), writes=scrf.keys(0, 512), tag=f"kd{g}")
            E("vector", (lambda e, o=scr.ap(512, 1024), a=v.ap(512, 1024), b=v.ap(0, 512):
                         e.tensor_sub(o, a, b)),
              reads=v.keys(), writes=scr.keys(512, 1024), tag=f"vd{g}")
            for j in range(4):
                h = g * 4 + j
                at = attn32[h]
                E("scalar", (lambda e, o=at.ap(0, 512), i=v.ap(0, 512): e.copy(o, i)),
                  reads=v.keys(), writes=at.keys(0, 512), tag=f"o0_{h}")
                E("vector", (lambda e, o=scrf.ap(512, 1024), a=qs[j].ap(512, 1024),
                             b=scrf.ap(0, 512): e.tensor_mul(o, a, b)),
                  reads=qs[j].keys() + scrf.keys(0, 512), writes=scrf.keys(512, 1024),
                  tag=f"pr{h}")
                pkD, papD = ps.f()
                E("tensor", (lambda e, o=papD[0:1, :], l=self.ones_col_f.ap(0, 1),
                             r=scrf.ap(512, 1024): e.matmul(o, l, r, start=True, stop=True)),
                  reads=self.ones_col_f.keys(0, 1) + scrf.keys(512, 1024), writes=[pkD],
                  tag=f"dot{h}")
                E("scalar", (lambda e, o=srow.ap(0, 512, 0, 1), i=papD[0:1, :]:
                             e.activation(o, i, AF.Exp, scale=-L1_SCALE)),
                  reads=[pkD], writes=srow.keys(0, 512), tag=f"se{h}")
                E("vector", (lambda e, o=srow.ap(512, 1024, 0, 1), i=srow.ap(0, 512, 0, 1):
                             e.tensor_scalar_add(o, i, 1.0)),
                  reads=srow.keys(0, 512), writes=srow.keys(512, 1024), tag=f"sd{h}")
                E("vector", (lambda e, o=srow.ap(512, 1024, 0, 1), i=srow.ap(512, 1024, 0, 1):
                             e.reciprocal(o, i)),
                  reads=srow.keys(512, 1024), writes=srow.keys(512, 1024), tag=f"sr{h}")
                E("vector", (lambda e, o=wrow.ap(0, 512, 0, 1), i=srow.ap(512, 1024, 0, 1):
                             e.tensor_copy(o, i)),
                  reads=srow.keys(512, 1024), writes=wrow.keys(), tag=f"swb{h}")
                pkB, papB = ps.f()
                E("tensor", (lambda e, o=papB, l=self.ones_row.ap(0, 128, 0, 1),
                             r=wrow.ap(0, 512, 0, 1): e.matmul(o, l, r, start=True, stop=True)),
                  reads=self.ones_row.keys(0, 128) + wrow.keys(), writes=[pkB], tag=f"wb{h}")
                E("vector", (lambda e, o=scrf.ap(1024, 1536), a=scr.ap(512, 1024), b=papB:
                             e.tensor_mul(o, a, b)),
                  reads=scr.keys(512, 1024) + [pkB], writes=scrf.keys(1024, 1536), tag=f"wm{h}")
                E("vector", (lambda e, o=at.ap(512, 1024), a=scrf.ap(1024, 1536), b=v.ap(0, 512):
                             e.tensor_add(o, a, b)),
                  reads=scrf.keys(1024, 1536) + v.keys(), writes=at.keys(512, 1024), tag=f"o1_{h}")
        for s in qg:
            for t in s:
                t.free()
        scr.free(); scrf.free(); srow.free(); wrow.free()
        lnst = self.ln_begin(f"l1a{p}")
        self.proj_to_prefm(p, ws_proj, attn32, None, [], "x2fm", lnst)
        for t in attn32:
            t.free()
        self.ln_finalize(lnst)
        self.ln_apply(lnst, p, f"prefm{p}", "x3fm")

    def l1_ffn(self, p, ws_up, ws_down, ws_fc1, ws_fc2, ws_exp, gw_tile):
        d, E, ps = self.d, self.E, self.ps
        xfm = self.load_fm("x3fm", p, "xfm")
        hst = [self.bf.alloc(512, "hst") for _ in range(4)]
        ogs = [(i, min(i + 3, 64)) for i in range(0, 64, 3)]
        for (lo, hi) in ogs:
            accw = ws_up.stream_cols(lo * 128, hi * 128)
            banks = ps.fgroup((hi - lo) * 2)
            for cc in range(32):
                for j, io in enumerate(range(lo, hi)):
                    lap, lkeys = accw(cc, j * 128, (j + 1) * 128)
                    for th in range(2):
                        pk, pap = banks[j * 2 + th]
                        E("tensor",
                          (lambda e, o=pap, l=lap, r=xfm[cc].ap(th * 512, (th + 1) * 512),
                           s=(cc == 0), t=(cc == 31): e.matmul(o, l, r, start=s, stop=t)),
                          reads=lkeys + xfm[cc].keys(th * 512, (th + 1) * 512),
                          writes=[pk], tag=f"up{io}{th}")
            for j, io in enumerate(range(lo, hi)):
                for th in range(2):
                    pk, pap = banks[j * 2 + th]
                    hs = hst[(io * 2 + th) % 4]
                    E("scalar", (lambda e, o=hs.ap(0, 512), i=pap: e.activation(o, i, AF.Relu)),
                      reads=[pk], writes=hs.keys(0, 512), tag=f"uprl{io}{th}")
                    E("vector", (lambda e, o=hs.ap(0, 512), i=hs.ap(0, 512): e.tensor_mul(o, i, i)),
                      reads=hs.keys(0, 512), writes=hs.keys(0, 512), tag=f"upsq{io}{th}")
                    dst = d[f"hdr{p}"][io * 128:(io + 1) * 128, th * 512:(th + 1) * 512]
                    E("sync", (lambda e, a=hs.ap(0, 512), o=dst: e.dma_start(out=o, in_=a)),
                      reads=hs.keys(0, 512), writes=[("hdr", p, io, th)], dma=self.st(),
                      tag=f"upst{io}{th}")
        for t in hst:
            t.free()

        rwrow = self.f32a.alloc(1024, "rwrow")
        rwb = [self.f32a.alloc(512, f"rwb{th}") for th in range(2)]
        gl = self.f32a.alloc(512, "gl")
        glk = gl.keys()
        for tt in range(8):
            pk, pap = ps.f()
            for cc in range(32):
                E("tensor",
                  (lambda e, o=pap[:, 0:8], l=xfm[cc].ap(tt * 128, (tt + 1) * 128),
                   r=gw_tile.ap(cc * 8, (cc + 1) * 8), s=(cc == 0), t=(cc == 31):
                   e.matmul(o, l, r, start=s, stop=t)),
                  reads=gw_tile.keys() + xfm[cc].keys(tt * 128, (tt + 1) * 128),
                  writes=[pk], tag=f"gt{tt}")
            E("scalar", (lambda e, o=gl.ap(0, 8), i=pap[:, 0:8]: e.copy(o, i)),
              reads=[pk], writes=glk, tag=f"gtev{tt}")
            E("vector", (lambda e, o=gl.ap(16, 17), i=gl.ap(0, 8): e.reduce_max(o, i, axis=AX.X)),
              reads=glk, writes=glk, tag="gm1")
            E("vector", (lambda e, o=gl.ap(8, 16), i=gl.ap(0, 8), s=gl.ap(16, 17):
                         e.tensor_scalar(o, i, s, None, ALU.is_equal)),
              reads=glk, writes=glk, tag="geq")
            E("vector", (lambda e, o=gl.ap(8, 16), i=gl.ap(8, 16), t2=gl.ap(0, 8):
                         e.scalar_tensor_tensor(o, i, -1e30, t2, ALU.mult, ALU.add)),
              reads=glk, writes=glk, tag="gmsk")
            E("vector", (lambda e, o=gl.ap(17, 18), i=gl.ap(8, 16): e.reduce_max(o, i, axis=AX.X)),
              reads=glk, writes=glk, tag="gm2")
            E("vector", (lambda e, o=gl.ap(18, 19), a=gl.ap(16, 17), b=gl.ap(17, 18):
                         e.tensor_add(o, a, b)),
              reads=glk, writes=glk, tag="grw")
            pkT, papT = ps.f()
            E("tensor", (lambda e, o=papT[0:1, 0:128], i=gl.ap(18, 19),
                         idn=self.id_f32.ap(0, 128): e.transpose(o, i, idn)),
              reads=glk + self.id_f32.keys(0, 128), writes=[pkT], tag=f"gtr{tt}")
            E("scalar", (lambda e, o=rwrow.ap(tt * 128, (tt + 1) * 128, 0, 1), i=papT[0:1, 0:128]:
                         e.copy(o, i)),
              reads=[pkT], writes=rwrow.keys(), tag=f"grow{tt}")
        gl.free()
        rwrow_bf = self.bf.alloc(1024, "rwrow_bf")
        E("vector", (lambda e, o=rwrow_bf.ap(0, 1024, 0, 1), i=rwrow.ap(0, 1024, 0, 1):
                     e.tensor_copy(o, i)),
          reads=rwrow.keys(), writes=rwrow_bf.keys(), tag="growbf")
        for th in range(2):
            pkB, papB = ps.f()
            E("tensor", (lambda e, o=papB, l=self.ones_row.ap(0, 128, 0, 1),
                         r=rwrow_bf.ap(th * 512, (th + 1) * 512, 0, 1):
                         e.matmul(o, l, r, start=True, stop=True)),
              reads=self.ones_row.keys(0, 128) + rwrow_bf.keys(), writes=[pkB], tag=f"rwb{th}")
            E("scalar", (lambda e, o=rwb[th].ap(0, 512), i=papB: e.copy(o, i)),
              reads=[pkB], writes=rwb[th].keys(), tag=f"rwbe{th}")
        xlat = [self.bf.alloc(1024, f"xlat{lc}") for lc in range(16)]
        for (lo, hi) in [(i, min(i + 3, 16)) for i in range(0, 16, 3)]:
            accw = ws_fc1.stream_cols(lo * 128, hi * 128)
            banks = ps.fgroup((hi - lo) * 2)
            for cc in range(32):
                for j, lc in enumerate(range(lo, hi)):
                    lap, lkeys = accw(cc, j * 128, (j + 1) * 128)
                    for th in range(2):
                        pk, pap = banks[j * 2 + th]
                        E("tensor",
                          (lambda e, o=pap, l=lap, r=xfm[cc].ap(th * 512, (th + 1) * 512),
                           s=(cc == 0), t=(cc == 31): e.matmul(o, l, r, start=s, stop=t)),
                          reads=lkeys + xfm[cc].keys(th * 512, (th + 1) * 512),
                          writes=[pk], tag=f"f1{lc}{th}")
            for j, lc in enumerate(range(lo, hi)):
                for th in range(2):
                    pk, pap = banks[j * 2 + th]
                    E("scalar", (lambda e, o=xlat[lc].ap(th * 512, (th + 1) * 512), i=pap:
                                 e.copy(o, i)),
                      reads=[pk], writes=xlat[lc].keys(th * 512, (th + 1) * 512), tag=f"f1e{lc}{th}")
        for t in xfm:
            t.free()
        moe = [self.bf.alloc(1024, f"moe{mc}") for mc in range(16)]
        for (lo, hi) in [(i, min(i + 3, 16)) for i in range(0, 16, 3)]:
            accw = ws_exp.stream_cols(lo * 128, hi * 128)
            banks = ps.fgroup((hi - lo) * 2)
            for lc in range(16):
                for j, mc in enumerate(range(lo, hi)):
                    lap, lkeys = accw(lc, j * 128, (j + 1) * 128)
                    for th in range(2):
                        pk, pap = banks[j * 2 + th]
                        E("tensor",
                          (lambda e, o=pap, l=lap, r=xlat[lc].ap(th * 512, (th + 1) * 512),
                           s=(lc == 0), t=(lc == 15): e.matmul(o, l, r, start=s, stop=t)),
                          reads=lkeys + xlat[lc].keys(th * 512, (th + 1) * 512),
                          writes=[pk], tag=f"ex{mc}{th}")
            for j, mc in enumerate(range(lo, hi)):
                for th in range(2):
                    pk, pap = banks[j * 2 + th]
                    E("vector", (lambda e, o=moe[mc].ap(th * 512, (th + 1) * 512), i=pap,
                                 r=rwb[th].ap(0, 512): e.tensor_mul(o, i, r)),
                      reads=[pk] + rwb[th].keys(), writes=moe[mc].keys(th * 512, (th + 1) * 512),
                      tag=f"exe{mc}{th}")
        for t in xlat:
            t.free()
        routed = [self.bf.alloc(1024, f"rt{hc}") for hc in range(32)]
        for (lo, hi) in [(i, min(i + 3, 32)) for i in range(0, 32, 3)]:
            accw = ws_fc2.stream_cols(lo * 128, hi * 128)
            banks = ps.fgroup((hi - lo) * 2)
            for lc in range(16):
                for j, hc in enumerate(range(lo, hi)):
                    lap, lkeys = accw(lc, j * 128, (j + 1) * 128)
                    for th in range(2):
                        pk, pap = banks[j * 2 + th]
                        E("tensor",
                          (lambda e, o=pap, l=lap, r=moe[lc].ap(th * 512, (th + 1) * 512),
                           s=(lc == 0), t=(lc == 15): e.matmul(o, l, r, start=s, stop=t)),
                          reads=lkeys + moe[lc].keys(th * 512, (th + 1) * 512),
                          writes=[pk], tag=f"f2{hc}{th}")
            for j, hc in enumerate(range(lo, hi)):
                for th in range(2):
                    pk, pap = banks[j * 2 + th]
                    E("scalar", (lambda e, o=routed[hc].ap(th * 512, (th + 1) * 512), i=pap:
                                 e.copy(o, i)),
                      reads=[pk], writes=routed[hc].keys(th * 512, (th + 1) * 512),
                      tag=f"f2e{hc}{th}")
        for t in moe:
            t.free()
        lnst = self.ln_begin(f"l1f{p}")
        hrd = [self.bf.alloc(1024, "hrd") for _ in range(3)]
        prest = [self.bf.alloc(512, "prest") for _ in range(4)]
        for og, (lo, hi) in enumerate([(i, min(i + 3, 32)) for i in range(0, 32, 3)]):
            accw = ws_down.stream_cols(lo * 128, hi * 128)
            banks = ps.fgroup((hi - lo) * 2)
            for ic in range(64):
                hb = hrd[ic % 3]
                src_ = d[f"hdr{p}"][ic * 128:(ic + 1) * 128, :]
                E("sync", (lambda e, a=hb.ap(0, 1024), s=src_: e.dma_start(out=a, in_=s)),
                  reads=[("hdr", p, ic, 0), ("hdr", p, ic, 1)], writes=hb.keys(0, 1024),
                  dma=("hrd", (og * 64 + ic) % 3), tag=f"hld{og}{ic}")
                for j, oc in enumerate(range(lo, hi)):
                    lap, lkeys = accw(ic, j * 128, (j + 1) * 128)
                    for th in range(2):
                        pk, pap = banks[j * 2 + th]
                        E("tensor",
                          (lambda e, o=pap, l=lap, r=hb.ap(th * 512, (th + 1) * 512),
                           s=(ic == 0), t=(ic == 63): e.matmul(o, l, r, start=s, stop=t)),
                          reads=lkeys + hb.keys(th * 512, (th + 1) * 512), writes=[pk],
                          tag=f"dn{oc}{th}")
            for j, oc in enumerate(range(lo, hi)):
                for th in range(2):
                    pk, pap = banks[j * 2 + th]
                    pr = prest[(oc * 2 + th) % 4]
                    E("vector", (lambda e, o=pr.ap(0, 512), i=pap,
                                 r=routed[oc].ap(th * 512, (th + 1) * 512): e.tensor_add(o, i, r)),
                      reads=[pk] + routed[oc].keys(th * 512, (th + 1) * 512),
                      writes=pr.keys(0, 512), tag=f"dne{oc}{th}")
                    self.ln_chunk(lnst, pr.ap(0, 512), pr.keys(0, 512), th)
                    dst = d[f"prefm{p}"][oc * 128:(oc + 1) * 128, th * 512:(th + 1) * 512]
                    E("sync", (lambda e, a=pr.ap(0, 512), o=dst: e.dma_start(out=o, in_=a)),
                      reads=pr.keys(0, 512), writes=[("prefm", p, oc, th)], dma=self.st(),
                      tag=f"dnst{oc}{th}")
        for t in hrd + prest + rwb + [rwrow, rwrow_bf]:
            t.free()

        self.ln_finalize(lnst)
        pre_keys = [("prefm", p, oc, th) for oc in range(32) for th in range(2)]
        xin2 = [self.bf.alloc(1024, "epx") for _ in range(3)]
        x3r = [self.bf.alloc(1024, "epx3") for _ in range(3)]
        za = [self.f32a.alloc(1024, "epza") for _ in range(2)]
        zbf = [self.bf.alloc(1024, "epzb") for _ in range(2)]
        for cc in range(32):
            t = xin2[cc % 3]
            E("sync", (lambda e, a=t.ap(), s=d[f"prefm{p}"][cc * 128:(cc + 1) * 128, :]:
                       e.dma_start(out=a, in_=s)),
              reads=pre_keys, writes=t.keys(), dma=("lnld", cc % 3), tag=f"epld{cc}")
            x3 = x3r[cc % 3]
            src_ = d["x3fm"][cc * 128:(cc + 1) * 128, p * 1024:(p + 1) * 1024]
            E("sync", (lambda e, a=x3.ap(), s=src_: e.dma_start(out=a, in_=s)),
              reads=[("x3fm", p, cc)], writes=x3.keys(), dma=("res", cc % 3), tag=f"epx3{cc}")
            z = za[cc % 2]
            E("vector", (lambda e, o=z.ap(), a=t.ap(), b=lnst["rbf"].ap(0, 1024):
                         e.tensor_mul(o, a, b)),
              reads=t.keys() + lnst["rbf"].keys(), writes=z.keys(), tag=f"ep1_{cc}")
            E("vector", (lambda e, o=z.ap(), a=z.ap(), b=lnst["nbf"].ap(0, 1024):
                         e.tensor_add(o, a, b)),
              reads=z.keys() + lnst["nbf"].keys(), writes=z.keys(), tag=f"ep2_{cc}")
            zb = zbf[cc % 2]
            E("vector", (lambda e, o=zb.ap(), i=z.ap(), g=self.gam_t.ap(cc, cc + 1),
                         b=self.bet_t.ap(cc, cc + 1):
                         e.tensor_scalar(o, i, g, b, ALU.mult, ALU.add)),
              reads=z.keys() + self.gam_t.keys(0, 32) + self.bet_t.keys(0, 32),
              writes=zb.keys(), tag=f"ep3_{cc}")
            pre2 = routed[cc]
            E("vector", (lambda e, o=pre2.ap(), a=zb.ap(), b=x3.ap(): e.tensor_add(o, a, b)),
              reads=zb.keys() + x3.keys(), writes=pre2.keys(), tag=f"ep4_{cc}")
        for t in xin2 + x3r + za + zbf:
            t.free()
        for k in ("srow", "rbf", "nbf", "rowbf"):
            lnst[k].free()
        for t in lnst["sq"]:
            t.free()
        pretm = [self.bf.alloc(4096, "pretm") for _ in range(2)]
        ostg = [self.f32a.alloc(1024, "ostg") for _ in range(2)]
        stat = self.f32a.alloc(512, "stat2")
        sk = stat.keys()
        for tt in range(8):
            t = pretm[tt % 2]
            for ccg in range(8):
                pkT, papT = ps.b()
                for j in range(4):
                    cc = ccg * 4 + j
                    E("tensor", (lambda e, o=papT[:, j * 128:(j + 1) * 128],
                                 i=routed[cc].ap(tt * 128, (tt + 1) * 128),
                                 idn=self.id_bf.ap(0, 128): e.transpose(o, i, idn)),
                      reads=routed[cc].keys(tt * 128, (tt + 1) * 128) + self.id_bf.keys(0, 128),
                      writes=[pkT], tag=f"eptr{tt}{ccg}")
                E("scalar", (lambda e, o=t.ap(ccg * 512, (ccg + 1) * 512), i=papT[:, 0:512]:
                             e.copy(o, i)),
                  reads=[pkT], writes=t.keys(ccg * 512, (ccg + 1) * 512), tag=f"eptre{tt}{ccg}")
            self._ln_stats(t, stat, sk, 50, 51)
            b_, q0 = tt // 4, (tt % 4) * 128
            orow = b_ * 1024 + p * 512 + q0
            for ch in range(4):
                o = ostg[ch % 2]
                E("vector", (lambda e, o_=o.ap(), i=t.ap(ch * 1024, (ch + 1) * 1024),
                             r=stat.ap(50, 51), m=stat.ap(51, 52):
                             e.tensor_scalar(o_, i, r, m, ALU.mult, ALU.add)),
                  reads=t.keys() + sk, writes=o.keys(), tag=f"eout{tt}{ch}")
                dst = d["out"][orow:orow + 128, ch * 1024:(ch + 1) * 1024]
                E("sync", (lambda e, a=o.ap(), o_=dst: e.dma_start(out=o_, in_=a)),
                  reads=o.keys(), writes=[("out", p, tt, ch)], dma=self.st(), tag=f"est{tt}{ch}")
        for t in pretm + ostg + [stat]:
            t.free()
        for t in routed:
            t.free()


def build_program(part, taps=()):
    mk = MK(part, taps)
    d = mk.d
    if part in ("s1", "all"):
        ws_qkv = WStream(mk, "wqkv", d["w_l0qkv"])
        for p in range(2):
            mk.s1(p, ws_qkv)
            if part == "all":
                mk.allgather(p)
    if part == "all":
        mk.bqkv.free(); mk.bq_s.free(); mk.vbias.free()
    if part in ("rest", "all"):
        rt = mk.bf.alloc(512, "rt")
        at = mk.bf.alloc(2048, "at")
        gw = mk.bf.alloc(512, "gw")
        for tile, name, width in ((rt, "r_t", 16), (at, "a_t", 64), (gw, "gate_w", 8)):
            src = d[name].rearrange("(cc p) e -> p cc e", p=128)
            dst = tile.ap(0, 32 * width).rearrange("p (cc e) -> p cc e", e=width)
            mk.E("gpsimd", (lambda e, a=dst, s=src: e.dma_start(out=a, in_=s)),
                 writes=tile.keys(0, 32 * width), dma=("cst", None), tag=f"ld_{name}")
        ws_o = WStream(mk, "wo", d["w_l0o"])
        ws_base = WStream(mk, "wbase", d["w_l0base_t"])
        ws_bflat = WStream(mk, "wbflat", d["b_flat"])
        ws_l1qkv = WStream(mk, "wl1qkv", d["w_l1qkv_t"])
        ws_l1proj = WStream(mk, "wl1proj", d["w_l1proj_t"])
        ws_up = WStream(mk, "wup", d["up_w"])
        ws_down = WStream(mk, "wdown", d["down_w"])
        ws_fc1 = WStream(mk, "wfc1", d["fc1_w"])
        ws_fc2 = WStream(mk, "wfc2", d["fc2_w"])
        ws_exp = WStream(mk, "wexp", d["expert_w"])
        for p in range(2):
            mk.l0_attn(p, ws_o)
            mk.tap("x1fm", [("x1fm", p, cc) for cc in range(32)])
        for p in range(2):
            mk.l0_ffn(p, ws_base, ws_bflat, rt, at)
            mk.tap("x2fm", [("x2fm", p, cc) for cc in range(32)])
        for p in range(2):
            mk.l1_attn(p, ws_l1qkv, ws_l1proj)
            mk.tap("x3fm", [("x3fm", p, cc) for cc in range(32)])
        for p in range(2):
            mk.l1_ffn(p, ws_up, ws_down, ws_fc1, ws_fc2, ws_exp, gw)
    mk.plan.build()
    return mk


_progs = {}


def _get_prog(part, taps=()):
    key = (part, tuple(sorted(taps)))
    if key not in _progs:
        _progs[key] = build_program(part, taps)
    return _progs[key]


REST_W = ("w_l0o", "b_l0o", "w_l0base_t", "b_l0base", "a_t", "b_flat", "r_t",
          "w_l1qkv_t", "w_l1proj_t", "gate_w", "up_w", "down_w", "fc1_w",
          "fc2_w", "expert_w", "ln_g", "ln_b")


def host_layout(inputs):
    C = np.ascontiguousarray
    w = {}
    w["w_l0qkv"] = C(inputs["L0_attn_Wqkv"])
    w["b_l0qkv"] = C(inputs["L0_attn_bqkv"]).reshape(1, 3072)
    w["w_l0o"] = C(inputs["L0_attn_Wo"])
    w["b_l0o"] = C(inputs["L0_attn_bo"]).reshape(1, 4096)
    w["w_l0base_t"] = C(inputs["L0_ffn_base_w"].T)
    w["b_l0base"] = C(inputs["L0_ffn_base_b"]).reshape(1, 4096)
    w["a_t"] = C(inputs["L0_ffn_A"].T)
    w["b_flat"] = C(np.transpose(inputs["L0_ffn_B"], (0, 2, 1)).reshape(1024, 4096))
    w["r_t"] = C(inputs["L0_ffn_router_w"].T)
    w["w_l1qkv_t"] = C(inputs["L1_attn_weight_qkv"].T)
    w["w_l1proj_t"] = C(inputs["L1_attn_weight_proj"].T)
    w["gate_w"] = C(inputs["L1_ffn_gate_w"])
    w["up_w"] = C(inputs["L1_ffn_up_w"])
    w["down_w"] = C(inputs["L1_ffn_down_w"])
    w["fc1_w"] = C(inputs["L1_ffn_fc1_w"])
    w["fc2_w"] = C(inputs["L1_ffn_fc2_w"])
    w["expert_w"] = C(inputs["L1_ffn_expert_w"])
    w["ln_g"] = C(inputs["L1_ffn_ln_gamma"]).reshape(1, 4096)
    w["ln_b"] = C(inputs["L1_ffn_ln_beta"]).reshape(1, 4096)
    return w


def shard_x(x):
    return [np.ascontiguousarray(x[:, r * 1024:(r + 1) * 1024, :].reshape(2048, DMODEL))
            for r in range(NCORES)]


FUSED = True


def kernel(**inputs):
    w = host_layout(inputs)
    xs = shard_x(np.asarray(inputs["L0_attn_x"]))
    if FUSED:
        mk = _get_prog("all")
        ins = [{**{n: w[n] for n in REST_W}, "x0": xs[r],
                "w_l0qkv": w["w_l0qkv"], "b_l0qkv": w["b_l0qkv"]}
               for r in range(NCORES)]
        res = run_bass_kernel_spmd(mk.nc, ins, core_ids=list(range(NCORES)))
        out = np.empty((2, SEQ, DMODEL), np.float32)
        for r in range(NCORES):
            out[:, r * 1024:(r + 1) * 1024, :] = res.results[r]["out"].reshape(2, 1024, DMODEL)
        return out
    mk1 = _get_prog("s1")
    in1 = [{"x0": xs[r], "w_l0qkv": w["w_l0qkv"], "b_l0qkv": w["b_l0qkv"]}
           for r in range(NCORES)]
    r1 = run_bass_kernel_spmd(mk1.nc, in1, core_ids=list(range(NCORES)))
    ksp = {f"kout{p}": np.concatenate([r1.results[r]["kin"][p * 2048:(p + 1) * 2048]
                                        for r in range(NCORES)], axis=0) for p in range(2)}
    vsp = {f"vout{p}": np.concatenate([r1.results[r]["vin"][p * 1024:(p + 1) * 1024]
                                       for r in range(NCORES)], axis=0) for p in range(2)}
    mk2 = _get_prog("rest")
    in2 = [{**{n: w[n] for n in REST_W}, **ksp, **vsp,
            "qdr": r1.results[r]["qdr"], "x0fm": r1.results[r]["x0fm"]}
           for r in range(NCORES)]
    r2 = run_bass_kernel_spmd(mk2.nc, in2, core_ids=list(range(NCORES)))
    out = np.empty((2, SEQ, DMODEL), np.float32)
    for r in range(NCORES):
        out[:, r * 1024:(r + 1) * 1024, :] = r2.results[r]["out"].reshape(2, 1024, DMODEL)
    return out


# version 07
# speedup vs baseline: 1.0283x; 1.0109x over previous
import numpy as np
import ml_dtypes
import concourse.bass as bass
import concourse.mybir as mybir
from concourse.bass_utils import run_bass_kernel_spmd

F32 = mybir.dt.float32
BF16 = mybir.dt.bfloat16
AF = mybir.ActivationFunctionType
ALU = mybir.AluOpType
AX = mybir.AxisListType
BFNP = ml_dtypes.bfloat16

NCORES = 8
DMODEL = 4096
CC = 32
TP = 1024
NP2 = 2
SEQ = 8192
POS_PER_CORE = 1024
LN_EPS = 1e-5
L0_SCALE = 1.0 / 32.0
L1_SCALE = 1.0 / (128.0 ** 0.5)

ENGINES = ("tensor", "scalar", "vector", "gpsimd", "sync")


class Op:
    __slots__ = ("eng", "fn", "deps", "is_dma", "dma_key", "consumed",
                 "event", "vc", "waits", "inc", "idx", "tag")

    def __init__(self):
        self.deps = ()
        self.consumed = False
        self.event = None
        self.vc = None
        self.waits = None
        self.inc = None


class Plan:
    def __init__(self, nc):
        self.nc = nc
        self.ops = []
        self.by_eng = {e: [] for e in ENGINES}
        self.res = {}
        self.n_sems = 0

    def emit(self, eng, fn, reads=(), writes=(), dma=None, tag=""):
        deps = set()
        for k in reads:
            e = self.res.get(k)
            if e is not None and e[0] is not None:
                deps.add(e[0])
        for k in writes:
            e = self.res.get(k)
            if e is not None:
                if e[0] is not None:
                    deps.add(e[0])
                deps.update(e[1])
        op = Op()
        op.eng = eng
        op.fn = fn
        op.is_dma = dma is not None
        op.dma_key = dma
        op.tag = tag
        op.idx = len(self.ops)
        kept = []
        for d in deps:
            if eng == "tensor" and d.eng == "tensor" and not d.is_dma and not op.is_dma:
                continue
            kept.append(d)
        op.deps = tuple(kept)
        self.ops.append(op)
        self.by_eng[eng].append(op)
        for k in reads:
            ent = self.res.get(k)
            if ent is None:
                self.res[k] = [None, [op]]
            else:
                ent[1].append(op)
        for k in writes:
            self.res[k] = [op, []]
        return op

    def build(self):
        nc = self.nc
        for op in self.ops:
            for d in op.deps:
                d.consumed = True
        eng_sem = {e: nc.alloc_semaphore(name=f"sq_{e}") for e in ENGINES}
        self.n_sems = len(ENGINES)
        dma_sems = {}
        fam_counts = {}
        fam_gen_pending = {}

        def get_dma_sem(semkey):
            ent = dma_sems.get(semkey)
            if ent is None:
                h = nc.alloc_semaphore(name=f"dm{len(dma_sems)}")
                self.n_sems += 1
                ent = dma_sems[semkey] = [h, 0]
            return ent

        cnt = {e: 0 for e in ENGINES}
        vc = {e: {} for e in ENGINES}

        for op in self.ops:
            myvc = vc[op.eng]
            needed = {}
            for d in op.deps:
                assert d.event is not None, f"dep without event: {d.tag}"
                s, v = d.event
                if d.is_dma and d.dma_key[1] is None:
                    semkey = ("F", d.dma_key[0])
                    v = dma_sems[semkey][1]
                if myvc.get(s, 0) < v and needed.get(s, 0) < v:
                    needed[s] = v
            for d in op.deps:
                for s2, v2 in d.vc.items():
                    if myvc.get(s2, 0) < v2:
                        myvc[s2] = v2
            for s, v in needed.items():
                if myvc.get(s, 0) < v:
                    myvc[s] = v
            if op.is_dma:
                fam, slot = op.dma_key[0], op.dma_key[1]
                gen = op.dma_key[2] if len(op.dma_key) > 2 else None
                if slot is None:
                    semkey = ("F", fam)
                else:
                    semkey = ("S", fam, slot)
                ent = get_dma_sem(semkey)
                h, prev = ent[0], ent[1]
                same_gen = gen is not None and len(ent) > 2 and ent[2] == gen
                if prev > 0 and not same_gen and myvc.get(id(h), 0) < prev:
                    if needed.get(id(h), 0) < prev:
                        needed[id(h)] = prev
                    myvc[id(h)] = prev
                ent[1] = prev + 16
                if len(ent) > 2:
                    ent[2] = gen
                else:
                    ent.append(gen)
                op.event = (id(h), ent[1])
                op.inc = (h, 16)
            elif op.consumed:
                cnt[op.eng] += 1
                op.event = (id(eng_sem[op.eng]), cnt[op.eng])
                op.inc = (eng_sem[op.eng], 1)
            else:
                op.event = (id(eng_sem[op.eng]), cnt[op.eng])
                op.inc = None
            op.waits = needed
            snap = dict(myvc)
            if op.inc is not None:
                snap[op.event[0]] = op.event[1]
            op.vc = snap

        id2h = {}
        for e in ENGINES:
            id2h[id(eng_sem[e])] = eng_sem[e]
        for ent in dma_sems.values():
            id2h[id(ent[0])] = ent[0]

        tail = [(eng_sem[e], cnt[e]) for e in ENGINES if cnt[e] > 0]
        tail += [(ent[0], ent[1]) for ent in dma_sems.values()]

        with nc.Block() as block:
            def mk(eng_ops, is_sync):
                def body(eng):
                    for op in eng_ops:
                        for s, v in sorted(op.waits.items(), key=lambda kv: kv[1]):
                            eng.wait_ge(id2h[s], v)
                        inst = op.fn(eng)
                        if op.inc is not None:
                            inst.then_inc(op.inc[0], op.inc[1])
                    if is_sync:
                        for h, v in tail:
                            eng.wait_ge(h, v)
                        for h, _ in tail:
                            eng.sem_clear(h)
                return body
            for e in ENGINES:
                if self.by_eng[e] or e == "sync":
                    getattr(block, e)(mk(self.by_eng[e], e == "sync"))
        return eng_sem, dma_sems


class Arena:
    QUANT = 512

    def __init__(self, nc, name, dtype, cols):
        self.name = name
        self.dtype = dtype
        self.cols = cols
        self.t = nc.alloc_sbuf_tensor(name, [128, cols], dtype).ap()
        self.free_list = [(0, cols)]
        self.peak = 0
        self.used = 0

    def alloc(self, cols, tag=""):
        cols = ((cols + self.QUANT - 1) // self.QUANT) * self.QUANT
        for i, (off, size) in enumerate(self.free_list):
            if size >= cols:
                if size == cols:
                    self.free_list.pop(i)
                else:
                    self.free_list[i] = (off + cols, size - cols)
                self.used += cols
                self.peak = max(self.peak, self.used)
                return Tile(self, off, cols)
        raise MemoryError(f"arena {self.name} out of space for {cols} ({tag}); used={self.used}/{self.cols}")

    def free(self, tile):
        self.used -= tile.cols
        self.free_list.append((tile.off, tile.cols))
        self.free_list.sort()
        merged = []
        for off, size in self.free_list:
            if merged and merged[-1][0] + merged[-1][1] == off:
                merged[-1] = (merged[-1][0], merged[-1][1] + size)
            else:
                merged.append((off, size))
        self.free_list = [tuple(x) for x in merged]


class Tile:
    __slots__ = ("arena", "off", "cols")

    def __init__(self, arena, off, cols):
        self.arena = arena
        self.off = off
        self.cols = cols

    def ap(self, c0=0, c1=None, p0=0, p1=128):
        c1 = self.cols if c1 is None else c1
        return self.arena.t[p0:p1, self.off + c0:self.off + c1]

    def keys(self, c0=0, c1=None):
        c1 = self.cols if c1 is None else c1
        q = Arena.QUANT
        lo = (self.off + c0) // q
        hi = (self.off + c1 + q - 1) // q
        return [(self.arena.name, i) for i in range(lo, hi)]

    def free(self):
        self.arena.free(self)


def _ceil(a, b):
    return (a + b - 1) // b


class PsumPool:

    def __init__(self, nc):
        self.f32 = [(f"ps{i}", nc.alloc_psum_tensor(f"ps{i}", [128, 512], F32).ap())
                    for i in range(8)]
        self.bf16 = [(k, ap.bitcast(BF16)) for k, ap in self.f32[6:8]]
        self.i_f = 0
        self.i_b = 0

    def f(self):
        k, ap = self.f32[self.i_f % 6]
        self.i_f += 1
        return k, ap

    def fgroup(self, n):
        assert n <= 6
        return [self.f() for _ in range(n)]

    def r(self):
        k, ap = self.f32[6 + self.i_b % 2]
        self.i_b += 1
        return k, ap

    def b(self):
        k, ap = self.bf16[self.i_b % 2]
        self.i_b += 1
        return k, ap


class WStream:
    def __init__(self, mk, name, w_ap):
        self.mk = mk
        self.name = name
        self.w = w_ap
        self.K = w_ap.shape[0]
        self.F = w_ap.shape[1]

    def stream_cols(self, c0, c1):
        width = c1 - c0
        mk = self.mk
        ics_per_piece = max(1, mk.wring_cols // width)
        n_ic = self.K // 128
        pieces = {}
        emitted = {}

        def ensure(ic0):
            if ic0 in emitted:
                return
            emitted[ic0] = True
            ic1 = min(ic0 + ics_per_piece, n_ic)
            sl = mk.wring_slot % len(mk.wring)
            t = mk.wring[sl]
            mk.wring_slot += 1
            nic = ic1 - ic0
            src = self.w[ic0 * 128:ic1 * 128, c0:c1].rearrange(
                "(ic p) f -> p ic f", p=128)
            dst = t.ap(0, nic * width).rearrange("p (ic f) -> p ic f", f=width)
            mk.plan.emit(
                "gpsimd",
                (lambda e, dst=dst, src=src: e.dma_start(out=dst, in_=src)),
                reads=[], writes=t.keys(0, nic * width),
                dma=("wg", sl),
                tag=f"{self.name}[{ic0}:{ic1},{c0}:{c1}]")
            for ic in range(ic0, ic1):
                pieces[ic] = (t, ic - ic0, width)

        def acc(ic, cc0, cc1):
            ensure((ic // ics_per_piece) * ics_per_piece)
            t, icl, w = pieces[ic]
            a = t.ap(icl * w + cc0, icl * w + cc1)
            return a, t.keys(icl * w + cc0, icl * w + cc1)
        return acc


class MK:
    def __init__(self, part="all", taps=()):
        self.part = part
        self.taps = set(taps)
        nc = self.nc = bass.Bass("TRN2", target_bir_lowering=False)
        self.plan = Plan(nc)
        self.bf = Arena(nc, "abf", BF16, 76 * 1024)
        self.f32a = Arena(nc, "af32", F32, 13 * 1024 + 512)
        self.ps = PsumPool(nc)
        self.wring_cols = 7168
        self.wring = [self.bf.alloc(self.wring_cols, f"wring{i}") for i in range(3)]
        self.wring_slot = 0
        self.tapped = {}

        d = self.d = {}
        ein, eout, internal = {}, {}, {}

        def dram(name, shape, dtype, kind):
            d[name] = nc.dram_tensor(name, shape, dtype, kind=kind).ap()

        s1 = part in ("s1", "all")
        rest = part in ("rest", "all")
        if s1:
            dram("x0", [2048, DMODEL], F32, "ExternalInput")
            dram("w_l0qkv", [4096, 3072], F32, "ExternalInput")
            dram("b_l0qkv", [1, 3072], F32, "ExternalInput")
        if rest:
            dram("w_l0o", [1024, 4096], F32, "ExternalInput")
            dram("b_l0o", [1, 4096], F32, "ExternalInput")
            dram("w_l0base_t", [4096, 4096], F32, "ExternalInput")
            dram("b_l0base", [1, 4096], F32, "ExternalInput")
            dram("a_t", [4096, 64], F32, "ExternalInput")
            dram("b_flat", [1024, 4096], F32, "ExternalInput")
            dram("r_t", [4096, 16], F32, "ExternalInput")
            dram("w_l1qkv_t", [4096, 6144], F32, "ExternalInput")
            dram("w_l1proj_t", [4096, 4096], F32, "ExternalInput")
            dram("gate_w", [4096, 8], F32, "ExternalInput")
            dram("up_w", [4096, 8192], F32, "ExternalInput")
            dram("down_w", [8192, 4096], F32, "ExternalInput")
            dram("fc1_w", [4096, 2048], F32, "ExternalInput")
            dram("fc2_w", [2048, 4096], F32, "ExternalInput")
            dram("expert_w", [2048, 2048], F32, "ExternalInput")
            dram("ln_g", [1, 4096], F32, "ExternalInput")
            dram("ln_b", [1, 4096], F32, "ExternalInput")
            dram("out", [2048, DMODEL], F32, "ExternalOutput")
        if part == "s1":
            dram("kin", [4096, 512], BF16, "ExternalOutput")
            dram("vin", [2048, 1024], BF16, "ExternalOutput")
            dram("qdr", [2048, 1024], BF16, "ExternalOutput")
            dram("x0fm", [DMODEL, 2048], BF16, "ExternalOutput")
        elif part == "rest":
            dram("kout0", [16384, 512], BF16, "ExternalInput")
            dram("kout1", [16384, 512], BF16, "ExternalInput")
            dram("vout0", [8192, 1024], BF16, "ExternalInput")
            dram("vout1", [8192, 1024], BF16, "ExternalInput")
            dram("qdr", [2048, 1024], BF16, "ExternalInput")
            dram("x0fm", [DMODEL, 2048], BF16, "ExternalInput")
        else:
            dram("kin", [4096, 512], BF16, "Internal")
            dram("vin", [2048, 1024], BF16, "Internal")
            dram("kout0", [16384, 512], BF16, "Internal")
            dram("kout1", [16384, 512], BF16, "Internal")
            dram("vout0", [8192, 1024], BF16, "Internal")
            dram("vout1", [8192, 1024], BF16, "Internal")
            dram("qdr", [2048, 1024], BF16, "Internal")
            dram("x0fm", [DMODEL, 2048], BF16, "Internal")
        names = [("x1fm", [DMODEL, 2048]), ("x2fm", [DMODEL, 2048]),
                 ("x3fm", [DMODEL, 2048])]
        for p in range(2):
            names += [(f"prefm{p}", [4096, TP]), (f"whtm{p}", [1024, 1024]),
                      (f"hdr{p}", [8192, TP]), (f"l1q{p}", [6144, TP])]
        for name, shape in names:
            if rest:
                dram(name, shape, BF16, "Internal")
        for t in self.taps:
            dram("tap_" + t, list(d[t].shape), d[t].dtype, "ExternalOutput")

        self._init_consts(s1, rest)

    def E(self, eng, fn, reads=(), writes=(), dma=None, tag=""):
        return self.plan.emit(eng, fn, reads, writes, dma, tag)

    def tap(self, name, keys):
        if name in self.taps:
            src, dst = self.d[name], self.d["tap_" + name]
            self.E("scalar", (lambda e, s=src, t=dst: e.dma_start(out=t, in_=s)),
                   reads=keys, writes=[("tap", name)], dma=("tap", None), tag=f"tap_{name}")

    def _init_consts(self, s1, rest):
        nc, d = self.nc, self.d
        self.id_bf = self.bf.alloc(512, "id_bf")
        self.id_f32 = self.f32a.alloc(512, "id_f32")
        for t, dt in ((self.id_bf, "idb"), (self.id_f32, "idf")):
            ap = t.ap(0, 128)
            self.E("gpsimd", (lambda e, a=ap: e.memset(a, 0.0)), writes=t.keys(0, 128), tag="id0")
            self.E("gpsimd",
                   (lambda e, a=ap: e.affine_select(
                       out=a, in_=a, compare_op=ALU.not_equal, fill=1.0,
                       base=0, pattern=[[-1, 128]], channel_multiplier=1)),
                   reads=t.keys(0, 128), writes=t.keys(0, 128), tag="id1")
        self.ones_row = self.bf.alloc(512, "ones_row")
        self.E("gpsimd", (lambda e, a=self.ones_row.ap(0, 128, 0, 1): e.memset(a, 1.0)),
               writes=self.ones_row.keys(0, 128), tag="ones_r")
        self.ones_col = self.bf.alloc(512, "ones_col")
        self.E("gpsimd", (lambda e, a=self.ones_col.ap(0, 1): e.memset(a, 1.0)),
               writes=self.ones_col.keys(0, 1), tag="ones_c")
        self.ones_col_f = self.f32a.alloc(512, "ones_col_f")
        self.E("gpsimd", (lambda e, a=self.ones_col_f.ap(0, 1): e.memset(a, 1.0)),
               writes=self.ones_col_f.keys(0, 1), tag="ones_cf")
        self.ones_row_f = self.f32a.alloc(512, "ones_row_f")
        self.E("gpsimd", (lambda e, a=self.ones_row_f.ap(0, 128, 0, 1): e.memset(a, 1.0)),
               writes=self.ones_row_f.keys(0, 128), tag="ones_rf")
        self.epsc = self.f32a.alloc(512, "epsc")
        self.E("gpsimd", (lambda e, a=self.epsc.ap(0, 1): e.memset(a, LN_EPS)),
               writes=self.epsc.keys(0, 1), tag="epsc")

        if s1:
            self.bqkv = self.f32a.alloc(512, "bqkv")
            src = d["b_l0qkv"].rearrange("o (c p) -> p (o c)", p=128)
            self.E("gpsimd", (lambda e, a=self.bqkv.ap(0, 24), s=src:
                              e.dma_start(out=a, in_=s, allow_slow_non_contiguous=True)),
                   writes=self.bqkv.keys(0, 24), dma=("cst", None), tag="ld_bqkv")
            self.bq_s = self.f32a.alloc(512, "bq_s")
            self.E("vector", (lambda e: e.tensor_scalar_mul(self.bq_s.ap(0, 8), self.bqkv.ap(0, 8), L0_SCALE)),
                   reads=self.bqkv.keys(0, 24), writes=self.bq_s.keys(0, 8), tag="bq_s")
            vrow = self.f32a.alloc(1024, "vrow")
            self.E("gpsimd", (lambda e, a=vrow.ap(0, 1024, 0, 1), s=d["b_l0qkv"][0:1, 2048:3072]:
                              e.dma_start(out=a, in_=s)),
                   writes=vrow.keys(), dma=("cst", None), tag="ld_vrow")
            vrow_bf = self.bf.alloc(1024, "vrow_bf")
            self.E("vector", (lambda e: e.tensor_copy(vrow_bf.ap(0, 1024, 0, 1), vrow.ap(0, 1024, 0, 1))),
                   reads=vrow.keys(), writes=vrow_bf.keys(), tag="vrow_bf")
            self.vbias = self.bf.alloc(1024, "vbias")
            for h in range(2):
                pk, pap = self.ps.f()
                self.E("tensor",
                       (lambda e, p=pap, r=vrow_bf.ap(h * 512, (h + 1) * 512, 0, 1):
                        e.matmul(p, self.ones_row.ap(0, 128, 0, 1), r, start=True, stop=True)),
                       reads=self.ones_row.keys(0, 128) + vrow_bf.keys(), writes=[pk], tag="vb_mm")
                self.E("scalar", (lambda e, p=pap, o=self.vbias.ap(h * 512, (h + 1) * 512): e.copy(o, p)),
                       reads=[pk], writes=self.vbias.keys(h * 512, (h + 1) * 512), tag="vb_ev")
            vrow.free()
            vrow_bf.free()
        if rest:
            self.bo_t = self.f32a.alloc(512, "bo_t")
            self.bb_t = self.f32a.alloc(512, "bb_t")
            self.gam_t = self.f32a.alloc(512, "gam_t")
            self.bet_t = self.f32a.alloc(512, "bet_t")
            for name, tile in (("b_l0o", self.bo_t), ("b_l0base", self.bb_t),
                               ("ln_g", self.gam_t), ("ln_b", self.bet_t)):
                src = d[name].rearrange("o (c p) -> p (o c)", p=128)
                self.E("gpsimd", (lambda e, a=tile.ap(0, 32), s=src:
                                  e.dma_start(out=a, in_=s, allow_slow_non_contiguous=True)),
                       writes=tile.keys(0, 32), dma=("cst", None), tag=f"ld_{name}")


    _st_ctr = 0

    def st(self):
        MK._st_ctr += 1
        return ("st", MK._st_ctr % 8)

    _wg_ctr = 0

    def wg(self):
        MK._wg_ctr += 1
        return ("wg", MK._wg_ctr % 6)

    def xtm_keys(self, name, p):
        return [(name, p, tt) for tt in range(8)]

    def s1(self, p, ws_qkv):
        d, E, ps = self.d, self.E, self.ps
        xin = [self.bf.alloc(4096, "xin") for _ in range(3)]
        xfm = [self.bf.alloc(1024, f"xfm{cc}") for cc in range(32)]
        for tt in range(8):
            b, q0 = tt // 4, (tt % 4) * 128
            t = xin[tt % 3]
            r0 = b * 1024 + p * 512 + q0
            src = d["x0"][r0:r0 + 128, :]
            E("gpsimd", (lambda e, a=t.ap(), s=src: e.dma_start(out=a, in_=s)),
              writes=t.keys(), dma=("xin", tt % 3), tag=f"xin{p}{tt}")
            for ccg in range(8):
                pkT, papT = ps.b()
                for j in range(4):
                    cc = ccg * 4 + j
                    E("tensor", (lambda e, o=papT[:, j * 128:(j + 1) * 128],
                                 i=t.ap(cc * 128, (cc + 1) * 128),
                                 idn=self.id_bf.ap(0, 128): e.transpose(o, i, idn)),
                      reads=t.keys(cc * 128, (cc + 1) * 128) + self.id_bf.keys(0, 128),
                      writes=[pkT], tag=f"x0t{tt}{ccg}")
                for j in range(4):
                    cc = ccg * 4 + j
                    dst = xfm[cc].ap(tt * 128, (tt + 1) * 128)
                    E("scalar", (lambda e, o=dst, i=papT[:, j * 128:(j + 1) * 128]: e.copy(o, i)),
                      reads=[pkT], writes=xfm[cc].keys(tt * 128, (tt + 1) * 128),
                      tag=f"x0te{tt}{ccg}")
        for cc in range(32):
            dst = d["x0fm"][cc * 128:(cc + 1) * 128, p * 1024:(p + 1) * 1024]
            E("scalar", (lambda e, a=xfm[cc].ap(), o=dst: e.dma_start(out=o, in_=a)),
              reads=xfm[cc].keys(), writes=[("x0fm", p, cc)], dma=self.st(), tag=f"x0st{cc}")
        qkst = [self.bf.alloc(512, "qkst") for _ in range(4)]
        ogs = [(i, min(i + 3, 16)) for i in range(0, 16, 3)]
        for (lo, hi) in ogs:
            accw = ws_qkv.stream_cols(lo * 128, hi * 128)
            banks = ps.fgroup((hi - lo) * 2)
            for cc in range(32):
                wap, wkeys = None, None
                for j, dc in enumerate(range(lo, hi)):
                    lap, lkeys = accw(cc, j * 128, (j + 1) * 128)
                    for th in range(2):
                        pk, pap = banks[j * 2 + th]
                        E("tensor",
                          (lambda e, o=pap, l=lap, r=xfm[cc].ap(th * 512, (th + 1) * 512),
                           s=(cc == 0), t=(cc == 31): e.matmul(o, l, r, start=s, stop=t)),
                          reads=lkeys + xfm[cc].keys(th * 512, (th + 1) * 512),
                          writes=[pk], tag=f"qk{dc}{th}")
            for j, dc in enumerate(range(lo, hi)):
                for th in range(2):
                    pk, pap = banks[j * 2 + th]
                    stg = qkst[(dc * 2 + th) % 4]
                    if dc < 8:
                        scale, bias_ap, bias_keys = L0_SCALE, self.bq_s.ap(dc, dc + 1), self.bq_s.keys(0, 8)
                    else:
                        scale, bias_ap, bias_keys = 1.0, self.bqkv.ap(dc, dc + 1), self.bqkv.keys(0, 24)
                    E("scalar",
                      (lambda e, o=stg.ap(0, 512), i=pap, b=bias_ap, s=scale:
                       e.activation(o, i, AF.Identity, bias=b, scale=s)),
                      reads=[pk] + bias_keys, writes=stg.keys(0, 512), tag=f"qkev{dc}{th}")
                    if dc < 8:
                        r0 = p * 1024 + dc * 128
                        dst = d["qdr"][r0:r0 + 128, th * 512:(th + 1) * 512]
                        wkey = ("qdr", p, dc, th)
                    else:
                        r0 = p * 2048 + th * 1024 + (dc - 8) * 128
                        dst = d["kin"][r0:r0 + 128, :]
                        wkey = ("kin", p, th, dc - 8)
                    E("scalar", (lambda e, a=stg.ap(0, 512), o=dst: e.dma_start(out=o, in_=a)),
                      reads=stg.keys(0, 512), writes=[wkey], dma=self.st(), tag=f"qkst{dc}{th}")
        for dvh in range(2):
            accw = ws_qkv.stream_cols(2048 + dvh * 512, 2048 + (dvh + 1) * 512)
            for tg in ((0, 6), (6, 8)):
                banks = ps.fgroup(tg[1] - tg[0])
                for cc in range(32):
                    rap, rkeys = accw(cc, 0, 512)
                    for j, tt in enumerate(range(*tg)):
                        pk, pap = banks[j]
                        E("tensor",
                          (lambda e, o=pap, l=xfm[cc].ap(tt * 128, (tt + 1) * 128), r=rap,
                           s=(cc == 0), t=(cc == 31): e.matmul(o, l, r, start=s, stop=t)),
                          reads=rkeys + xfm[cc].keys(tt * 128, (tt + 1) * 128),
                          writes=[pk], tag=f"v{tt}{dvh}")
                for j, tt in enumerate(range(*tg)):
                    pk, pap = banks[j]
                    stg = qkst[tt % 4]
                    E("vector",
                      (lambda e, o=stg.ap(0, 512), i=pap, b=self.vbias.ap(dvh * 512, (dvh + 1) * 512):
                       e.tensor_add(o, i, b)),
                      reads=[pk] + self.vbias.keys(), writes=stg.keys(0, 512), tag=f"vev{tt}")
                    r0 = p * 1024 + tt * 128
                    dst = d["vin"][r0:r0 + 128, dvh * 512:(dvh + 1) * 512]
                    E("gpsimd", (lambda e, a=stg.ap(0, 512), o=dst: e.dma_start(out=o, in_=a)),
                      reads=stg.keys(0, 512), writes=[("vin", p, tt, dvh)], dma=self.st(),
                      tag=f"vst{tt}")
        for t in xin + xfm + qkst:
            t.free()

    def allgather(self, p):
        d, E = self.d, self.E
        kin_keys = [("kin", p, b, dk) for b in range(2) for dk in range(8)]
        vin_keys = [("vin", p, tt, dvh) for tt in range(8) for dvh in range(2)]
        kin = d["kin"][p * 2048:(p + 1) * 2048, :]
        vin = d["vin"][p * 1024:(p + 1) * 1024, :]
        kout, vout = d[f"kout{p}"], d[f"vout{p}"]

        def fk(e):
            return e.collective_compute(
                "AllGather", ALU.bypass, replica_groups=[list(range(NCORES))],
                ins=[kin.opt()], outs=[kout.opt()])

        def fv(e):
            return e.collective_compute(
                "AllGather", ALU.bypass, replica_groups=[list(range(NCORES))],
                ins=[vin.opt()], outs=[vout.opt()])
        self.plan.emit("gpsimd", fk, reads=kin_keys, writes=[("kout", p)], tag=f"ag_k{p}")
        self.plan.emit("gpsimd", fv, reads=vin_keys, writes=[("vout", p)], tag=f"ag_v{p}")

    def flash(self, p):
        d, E, ps = self.d, self.E, self.ps
        qf = [self.bf.alloc(1024, f"qf{dc}") for dc in range(8)]
        for dc in range(8):
            r0 = p * 1024 + dc * 128
            E("gpsimd", (lambda e, a=qf[dc].ap(), s=d["qdr"][r0:r0 + 128, :]: e.dma_start(out=a, in_=s)),
              reads=[("qdr", p, dc, th) for th in range(2)], writes=qf[dc].keys(),
              dma=("qf", None), tag=f"qf{dc}")
        attn = [self.bf.alloc(1024, f"attn{dc}") for dc in range(8)]
        kvk = [self.bf.alloc(4096, "kvk") for _ in range(2)]
        kvv = [self.bf.alloc(4096, "kvv") for _ in range(2)]
        pbuf = [self.bf.alloc(512, "pbuf") for _ in range(4)]
        pt4 = [self.bf.alloc(512, "pt4") for _ in range(2)]
        acc = [self.f32a.alloc(1024, f"acc{qt}") for qt in range(4)]
        ml = self.f32a.alloc(512, "ml")
        mlk = ml.keys()

        def mlap(c):
            return ml.ap(c, c + 1)
        for b in range(2):
            for kc in range(16):
                pp, r = kc // 8, kc % 8
                sl = kc % 2
                r0 = r * 2048 + b * 1024
                src = d[f"kout{pp}"][r0:r0 + 1024, :].rearrange("(dc p) k -> p dc k", p=128)
                E("gpsimd", (lambda e, a=kvk[sl].ap(), s=src: e.dma_start(out=a, in_=s)),
                  reads=[("kout", pp)], writes=kvk[sl].keys(), dma=("kvk", sl), tag=f"kld{b}{kc}")
                r0 = r * 1024 + b * 512
                src = d[f"vout{pp}"][r0:r0 + 512, :].rearrange("(s p) v -> p s v", p=128)
                E("gpsimd", (lambda e, a=kvv[sl].ap(), s=src: e.dma_start(out=a, in_=s)),
                  reads=[("vout", pp)], writes=kvv[sl].keys(), dma=("kvv", sl), tag=f"vld{b}{kc}")
                sbanks = []
                for qt in range(4):
                    pkS, papS = ps.f()
                    sbanks.append((pkS, papS))
                    for dc in range(8):
                        E("tensor",
                          (lambda e, o=papS, l=qf[dc].ap(b * 512 + qt * 128, b * 512 + (qt + 1) * 128),
                           r_=kvk[sl].ap(dc * 512, (dc + 1) * 512), s=(dc == 0), t=(dc == 7):
                           e.matmul(o, l, r_, start=s, stop=t)),
                          reads=qf[dc].keys() + kvk[sl].keys(dc * 512, (dc + 1) * 512),
                          writes=[pkS], tag=f"sc{b}{kc}{qt}")
                for qt in range(4):
                    pkS, papS = sbanks[qt]
                    par = kc % 2
                    mq, mq_old = mlap(qt * 2 + par), mlap(qt * 2 + 1 - par)
                    lq, lq_old = mlap(8 + qt * 2 + par), mlap(8 + qt * 2 + 1 - par)
                    nm, rs, al = mlap(16 + qt), mlap(20 + qt), mlap(24 + qt)
                    P = pbuf[qt]
                    if kc == 0:
                        E("vector", (lambda e, o=mq, i=papS: e.reduce_max(o, i, axis=AX.X)),
                          reads=[pkS], writes=mlk, tag="fm0")
                        E("vector", (lambda e, o=nm, i=mq: e.tensor_scalar_mul(o, i, -1.0)),
                          reads=mlk, writes=mlk, tag="fm1")
                        E("scalar", (lambda e, o=P.ap(0, 512), i=papS, bi=nm, ac=lq:
                                     e.activation(o, i, AF.Exp, bias=bi, accum_out=ac)),
                          reads=[pkS] + mlk, writes=P.keys() + mlk, tag="fexp0")
                    else:
                        cm = mlap(28 + qt)
                        E("vector", (lambda e, o=cm, i=papS: e.reduce_max(o, i, axis=AX.X)),
                          reads=[pkS], writes=mlk, tag="fcm")
                        E("vector", (lambda e, o=mq, a=mq_old, b_=cm: e.tensor_max(o, a, b_)),
                          reads=mlk, writes=mlk, tag="fmx")
                        E("vector", (lambda e, o=nm, i=mq: e.tensor_scalar_mul(o, i, -1.0)),
                          reads=mlk, writes=mlk, tag="fnm")
                        E("scalar", (lambda e, o=al, i=mq_old, bi=nm: e.activation(o, i, AF.Exp, bias=bi)),
                          reads=mlk, writes=mlk, tag="fal")
                        E("scalar", (lambda e, o=P.ap(0, 512), i=papS, bi=nm, ac=rs:
                                     e.activation(o, i, AF.Exp, bias=bi, accum_out=ac)),
                          reads=[pkS] + mlk, writes=P.keys() + mlk, tag="fexp")
                        E("vector", (lambda e, o=lq, i=lq_old, s=al, t=rs:
                                     e.scalar_tensor_tensor(o, i, s, t, ALU.mult, ALU.add)),
                          reads=mlk, writes=mlk, tag="fl")
                ptdone = {}

                def emit_pt(qt):
                    P = pbuf[qt]
                    pkT, papT = ps.b()
                    for ks in range(4):
                        E("tensor", (lambda e, o=papT[:, ks * 128:(ks + 1) * 128],
                                     i=P.ap(ks * 128, (ks + 1) * 128),
                                     idn=self.id_bf.ap(0, 128): e.transpose(o, i, idn)),
                          reads=P.keys() + self.id_bf.keys(0, 128), writes=[pkT], tag="fpt")
                    pt = pt4[qt % 2]
                    E("scalar", (lambda e, o=pt.ap(0, 512), i=papT[:, 0:512]: e.copy(o, i)),
                      reads=[pkT], writes=pt.keys(0, 512), tag="fptev")
                    ptdone[qt] = pt

                def emit_av(qt):
                    par = kc % 2
                    al = mlap(24 + qt)
                    pt = ptdone[qt]
                    for dvh in range(2):
                        pkA, papA = ps.f()
                        for ks in range(4):
                            E("tensor",
                              (lambda e, o=papA, l=pt.ap(ks * 128, (ks + 1) * 128),
                               r_=kvv[sl].ap(ks * 1024 + dvh * 512, ks * 1024 + (dvh + 1) * 512),
                               s=(ks == 0), t=(ks == 3): e.matmul(o, l, r_, start=s, stop=t)),
                              reads=pt.keys(ks * 128, (ks + 1) * 128) +
                              kvv[sl].keys(ks * 1024 + dvh * 512, ks * 1024 + (dvh + 1) * 512),
                              writes=[pkA], tag=f"av{b}{kc}{qt}{dvh}")
                        a_ap = acc[qt].ap(dvh * 512, (dvh + 1) * 512)
                        a_keys = acc[qt].keys(dvh * 512, (dvh + 1) * 512)
                        if kc == 0:
                            E("vector", (lambda e, o=a_ap, i=papA: e.tensor_copy(o, i)),
                              reads=[pkA], writes=a_keys, tag="facc0")
                        else:
                            E("vector", (lambda e, o=a_ap, i=papA, s=al:
                                         e.scalar_tensor_tensor(o, o, s, i, ALU.mult, ALU.add)),
                              reads=[pkA] + a_keys + mlk, writes=a_keys, tag="facc")
                emit_pt(0)
                emit_pt(1)
                emit_av(0)
                emit_pt(2)
                emit_av(1)
                emit_pt(3)
                emit_av(2)
                emit_av(3)
            for qt in range(4):
                lq = mlap(8 + qt * 2 + 1)
                rl = mlap(28 + qt)
                E("vector", (lambda e, o=rl, i=lq: e.reciprocal(o, i)), reads=mlk, writes=mlk, tag="frl")
                ao2 = self.bf.alloc(1024, "ao2")
                E("vector", (lambda e, o=ao2.ap(), i=acc[qt].ap(), s=rl:
                             e.tensor_scalar_mul(o, i, s)),
                  reads=acc[qt].keys() + mlk, writes=ao2.keys(), tag="fnorm")
                for half in range(2):
                    pkT, papT = ps.b()
                    for dc2 in range(4):
                        dc = half * 4 + dc2
                        E("tensor", (lambda e, o=papT[:, dc2 * 128:(dc2 + 1) * 128],
                                     i=ao2.ap(dc * 128, (dc + 1) * 128),
                                     idn=self.id_bf.ap(0, 128): e.transpose(o, i, idn)),
                          reads=ao2.keys() + self.id_bf.keys(0, 128), writes=[pkT], tag="faot")
                    for dc2 in range(4):
                        dc = half * 4 + dc2
                        dst = attn[dc].ap(b * 512 + qt * 128, b * 512 + (qt + 1) * 128)
                        E("scalar", (lambda e, o=dst, i=papT[:, dc2 * 128:(dc2 + 1) * 128]: e.copy(o, i)),
                          reads=[pkT], writes=attn[dc].keys(b * 512 + qt * 128, b * 512 + (qt + 1) * 128),
                          tag="faotev")
                ao2.free()
        for t in qf + kvk + kvv + pbuf + pt4 + acc + [ml]:
            t.free()
        return attn

    def _ln_stats(self, t, stat, sk, rstd_col, negmr_col):
        E = self.E
        for s8 in range(8):
            E("vector", (lambda e, o=stat.ap(s8 * 6, s8 * 6 + 6), i=t.ap(s8 * 512, (s8 + 1) * 512):
                         e.bn_stats(o, i)),
              reads=t.keys(s8 * 512, (s8 + 1) * 512), writes=sk, tag="bns")
        E("vector", (lambda e, o=stat.ap(48, 50), i=stat.ap(0, 48): e.bn_aggr(o, i)),
          reads=sk, writes=sk, tag="bna")
        E("scalar", (lambda e, o=stat.ap(rstd_col, rstd_col + 1), i=stat.ap(49, 50),
                     b=self.epsc.ap(0, 1): e.activation(o, i, AF.Ln, bias=b)),
          reads=sk + self.epsc.keys(0, 1), writes=sk, tag="lnv")
        E("scalar", (lambda e, o=stat.ap(rstd_col, rstd_col + 1), i=stat.ap(rstd_col, rstd_col + 1):
                     e.activation(o, i, AF.Exp, scale=-0.5)),
          reads=sk, writes=sk, tag="rstd")
        E("vector", (lambda e, o=stat.ap(negmr_col, negmr_col + 1), i=stat.ap(48, 49),
                     r=stat.ap(rstd_col, rstd_col + 1):
                     e.tensor_scalar(o, i, r, -1.0, ALU.mult, ALU.mult)),
          reads=sk, writes=sk, tag="negmr")

    def ln_begin(self, tag):
        st = {}
        st["srow"] = self.f32a.alloc(2048, f"lns_{tag}")
        st["sq"] = [self.bf.alloc(512, "lnsq") for _ in range(2)]
        st["rbf"] = self.f32a.alloc(1024, f"lnrb_{tag}")
        st["nbf"] = self.f32a.alloc(1024, f"lnnb_{tag}")
        st["rowbf"] = self.bf.alloc(512, "lnrow")
        self.E("vector", (lambda e, a=st["srow"].ap(0, 2048, 0, 1): e.memset(a, 0.0)),
               writes=st["srow"].keys(), tag=f"lnz_{tag}")
        return st

    def ln_chunk(self, st, pr_ap, pr_keys, th):
        E, ps = self.E, self.ps
        sqt = st["sq"][th]
        E("vector", (lambda e, o=sqt.ap(0, 512), a=pr_ap: e.tensor_mul(o, a, a)),
          reads=pr_keys, writes=sqt.keys(), tag="lnsq")
        for rhs_ap, rhs_keys, col0 in ((pr_ap, pr_keys, th * 512),
                                       (sqt.ap(0, 512), sqt.keys(), 1024 + th * 512)):
            pk, pap = ps.r()
            E("tensor", (lambda e, o=pap[0:1, :], l=self.ones_col.ap(0, 1), r=rhs_ap:
                         e.matmul(o, l, r, start=True, stop=True)),
              reads=self.ones_col.keys(0, 1) + rhs_keys, writes=[pk], tag="lnmm")
            E("vector", (lambda e, o=st["srow"].ap(col0, col0 + 512, 0, 1),
                         a=st["srow"].ap(col0, col0 + 512, 0, 1), b=pap[0:1, :]:
                         e.tensor_add(o, a, b)),
              reads=[pk] + st["srow"].keys(), writes=st["srow"].keys(), tag="lnacc")

    def ln_finalize(self, st):
        E, ps = self.E, self.ps
        srow, rowbf = st["srow"], st["rowbf"]
        sk = srow.keys()
        E("vector", (lambda e, o=srow.ap(0, 1024, 0, 1), i=srow.ap(0, 1024, 0, 1):
                     e.tensor_scalar_mul(o, i, 1.0 / 4096.0)),
          reads=sk, writes=sk, tag="lnmean")
        E("vector", (lambda e, o=srow.ap(1024, 2048, 0, 1), i=srow.ap(1024, 2048, 0, 1):
                     e.tensor_scalar_mul(o, i, 1.0 / 4096.0)),
          reads=sk, writes=sk, tag="lne2")
        msq = self.f32a.alloc(1024, "lnmsq")
        E("vector", (lambda e, o=msq.ap(0, 1024, 0, 1), a=srow.ap(0, 1024, 0, 1),
                     b=srow.ap(0, 1024, 0, 1): e.tensor_mul(o, a, b)),
          reads=sk, writes=msq.keys(), tag="lnm2")
        E("vector", (lambda e, o=srow.ap(1024, 2048, 0, 1), a=srow.ap(1024, 2048, 0, 1),
                     b=msq.ap(0, 1024, 0, 1): e.tensor_sub(o, a, b)),
          reads=sk + msq.keys(), writes=sk, tag="lnvar")
        msq.free()
        E("scalar", (lambda e, o=srow.ap(1024, 2048, 0, 1), i=srow.ap(1024, 2048, 0, 1),
                     b=self.epsc.ap(0, 1, 0, 1): e.activation(o, i, AF.Ln, bias=b)),
          reads=sk + self.epsc.keys(0, 1), writes=sk, tag="lnln")
        E("scalar", (lambda e, o=srow.ap(1024, 2048, 0, 1), i=srow.ap(1024, 2048, 0, 1):
                     e.activation(o, i, AF.Exp, scale=-0.5)),
          reads=sk, writes=sk, tag="lnexp")
        E("vector", (lambda e, o=srow.ap(0, 1024, 0, 1), a=srow.ap(0, 1024, 0, 1),
                     b=srow.ap(1024, 2048, 0, 1): e.tensor_mul(o, a, b)),
          reads=sk, writes=sk, tag="lnnm1")
        E("vector", (lambda e, o=srow.ap(0, 1024, 0, 1), i=srow.ap(0, 1024, 0, 1):
                     e.tensor_scalar_mul(o, i, -1.0)),
          reads=sk, writes=sk, tag="lnnm2")
        for srccol, dsttile in ((1024, st["rbf"]), (0, st["nbf"])):
            for th in range(2):
                pk, pap = ps.f()
                E("tensor", (lambda e, o=pap, l=self.ones_row_f.ap(0, 128, 0, 1),
                             r=srow.ap(srccol + th * 512, srccol + (th + 1) * 512, 0, 1):
                             e.matmul(o, l, r, start=True, stop=True)),
                  reads=self.ones_row_f.keys(0, 128) + sk, writes=[pk], tag="lnbc")
                E("scalar", (lambda e, o=dsttile.ap(th * 512, (th + 1) * 512), i=pap: e.copy(o, i)),
                  reads=[pk], writes=dsttile.keys(th * 512, (th + 1) * 512), tag="lnbce")

    def ln_apply(self, st, p, src_name, dst_name, free=True):
        d, E = self.d, self.E
        pre_keys = [("prefm", p, oc, th) for oc in range(32) for th in range(2)]
        xin = [self.bf.alloc(1024, "lnxin") for _ in range(3)]
        zf = [self.f32a.alloc(1024, "lnzf") for _ in range(2)]
        zo = [self.bf.alloc(1024, "lnzo") for _ in range(2)]
        for cc in range(32):
            t = xin[cc % 3]
            E("sync", (lambda e, a=t.ap(), s=d[src_name][cc * 128:(cc + 1) * 128, :]:
                       e.dma_start(out=a, in_=s)),
              reads=pre_keys, writes=t.keys(), dma=("lnld", cc % 3), tag=f"lnld{cc}")
            z = zf[cc % 2]
            E("vector", (lambda e, o=z.ap(), a=t.ap(), b=st["rbf"].ap(0, 1024):
                         e.tensor_mul(o, a, b)),
              reads=t.keys() + st["rbf"].keys(), writes=z.keys(), tag=f"lnap1_{cc}")
            o2 = zo[cc % 2]
            E("vector", (lambda e, o=o2.ap(), a=z.ap(), b=st["nbf"].ap(0, 1024):
                         e.tensor_add(o, a, b)),
              reads=z.keys() + st["nbf"].keys(), writes=o2.keys(), tag=f"lnap2_{cc}")
            dst = d[dst_name][cc * 128:(cc + 1) * 128, p * 1024:(p + 1) * 1024]
            E("gpsimd", (lambda e, a=o2.ap(), o=dst: e.dma_start(out=o, in_=a)),
              reads=o2.keys(), writes=[(dst_name, p, cc)], dma=self.st(), tag=f"lnst{cc}")
        for t in xin + zf + zo:
            t.free()
        if free:
            for k in ("srow", "rbf", "nbf", "rowbf"):
                st[k].free()
            for t in st["sq"]:
                t.free()

    def proj_to_prefm(self, p, ws, attn_tiles, bias_tile, bias_keys, res_name, lnst):
        d, E, ps = self.d, self.E, self.ps
        nic = len(attn_tiles)
        t1r = [self.f32a.alloc(512, "t1r") for _ in range(2)]
        resr = [self.bf.alloc(512, "resr") for _ in range(4)]
        prest = [self.bf.alloc(512, "prest") for _ in range(4)]
        ogs = [(i, min(i + 3, 32)) for i in range(0, 32, 3)]
        for gi, (lo, hi) in enumerate(ogs):
            accw = ws.stream_cols(lo * 128, hi * 128)
            banks = ps.fgroup((hi - lo) * 2)
            for ic in range(nic):
                for j, oc in enumerate(range(lo, hi)):
                    lap, lkeys = accw(ic, j * 128, (j + 1) * 128)
                    for th in range(2):
                        pk, pap = banks[j * 2 + th]
                        E("tensor",
                          (lambda e, o=pap, l=lap, r=attn_tiles[ic].ap(th * 512, (th + 1) * 512),
                           s=(ic == 0), t=(ic == nic - 1): e.matmul(o, l, r, start=s, stop=t)),
                          reads=lkeys + attn_tiles[ic].keys(th * 512, (th + 1) * 512),
                          writes=[pk], tag=f"pj{oc}{th}")
            for j, oc in enumerate(range(lo, hi)):
                for th in range(2):
                    pk, pap = banks[j * 2 + th]
                    res = resr[(oc * 2 + th) % 4]
                    src = d[res_name][oc * 128:(oc + 1) * 128,
                                      p * 1024 + th * 512:p * 1024 + (th + 1) * 512]
                    E("gpsimd", (lambda e, a=res.ap(0, 512), s=src: e.dma_start(out=a, in_=s)),
                      reads=[(res_name, p, oc)], writes=res.keys(0, 512),
                      dma=("res", (oc * 2 + th) % 4), tag=f"res{oc}{th}")
                    pr = prest[(oc * 2 + th) % 4]
                    if bias_tile is not None:
                        t1 = t1r[th]
                        E("scalar", (lambda e, o=t1.ap(0, 512), i=pap, b=bias_tile.ap(oc, oc + 1):
                                     e.activation(o, i, AF.Identity, bias=b)),
                          reads=[pk] + bias_keys, writes=t1.keys(0, 512), tag=f"pjb{oc}{th}")
                        E("vector", (lambda e, o=pr.ap(0, 512), i=t1.ap(0, 512), r=res.ap(0, 512):
                                     e.tensor_add(o, i, r)),
                          reads=t1.keys(0, 512) + res.keys(0, 512), writes=pr.keys(0, 512),
                          tag=f"pjr{oc}{th}")
                    else:
                        E("vector", (lambda e, o=pr.ap(0, 512), i=pap, r=res.ap(0, 512):
                                     e.tensor_add(o, i, r)),
                          reads=[pk] + res.keys(0, 512), writes=pr.keys(0, 512), tag=f"pjr{oc}{th}")
                    self.ln_chunk(lnst, pr.ap(0, 512), pr.keys(0, 512), th)
                    dst = d[f"prefm{p}"][oc * 128:(oc + 1) * 128, th * 512:(th + 1) * 512]
                    E("gpsimd", (lambda e, a=pr.ap(0, 512), o=dst: e.dma_start(out=o, in_=a)),
                      reads=pr.keys(0, 512), writes=[("prefm", p, oc, th)], dma=self.st(),
                      tag=f"pjst{oc}{th}")
        for t in t1r + resr + prest:
            t.free()

    def l0_attn(self, p, ws_o):
        attn = self.flash(p)
        lnst = self.ln_begin(f"l0a{p}")
        self.proj_to_prefm(p, ws_o, attn, self.bo_t, self.bo_t.keys(0, 32), "x0fm", lnst)
        for t in attn:
            t.free()
        self.ln_finalize(lnst)
        self.ln_apply(lnst, p, f"prefm{p}", "x1fm")

    def load_fm(self, name, p, fam):
        d, E = self.d, self.E
        xfm = [self.bf.alloc(1024, f"{name}fm{cc}") for cc in range(32)]
        for cc in range(32):
            src = d[name][cc * 128:(cc + 1) * 128, p * 1024:(p + 1) * 1024]
            E("gpsimd", (lambda e, a=xfm[cc].ap(), s=src: e.dma_start(out=a, in_=s)),
              reads=[(name, p, cc)], writes=xfm[cc].keys(), dma=(fam, None),
              tag=f"{name}fm{cc}")
        return xfm

    def l0_ffn(self, p, ws_base, ws_bflat, rt_tile, at_tile):
        d, E, ps = self.d, self.E, self.ps
        xfm = self.load_fm("x1fm", p, "xfm")
        lf = self.f32a.alloc(1024, "lf")
        for th in range(2):
            pk, pap = ps.f()
            for cc in range(32):
                E("tensor",
                  (lambda e, o=pap[0:16, :], l=rt_tile.ap(cc * 16, (cc + 1) * 16),
                   r=xfm[cc].ap(th * 512, (th + 1) * 512), s=(cc == 0), t=(cc == 31):
                   e.matmul(o, l, r, start=s, stop=t)),
                  reads=rt_tile.keys() + xfm[cc].keys(th * 512, (th + 1) * 512),
                  writes=[pk], tag="rl")
            E("scalar", (lambda e, o=lf.ap(th * 512, (th + 1) * 512, 0, 16), i=pap[0:16, :]:
                         e.copy(o, i)),
              reads=[pk], writes=lf.keys(th * 512, (th + 1) * 512), tag="rlev")
        comb = [self.f32a.alloc(512, f"comb{tt}") for tt in range(8)]
        for tt in range(8):
            pk, pap = ps.f()
            E("tensor", (lambda e, o=pap[:, 0:16], i=lf.ap(tt * 128, (tt + 1) * 128, 0, 16),
                         idn=self.id_f32.ap(0, 16, 0, 16): e.transpose(o, i, idn)),
              reads=lf.keys() + self.id_f32.keys(0, 128), writes=[pk], tag="rltr")
            c = comb[tt]
            ck = c.keys()
            E("scalar", (lambda e, o=c.ap(0, 16), i=pap[:, 0:16]: e.copy(o, i)),
              reads=[pk], writes=ck, tag="rltm")
            E("vector", (lambda e, o=c.ap(48, 49), i=c.ap(0, 16): e.reduce_max(o, i, axis=AX.X)),
              reads=ck, writes=ck, tag="rm1")
            E("vector", (lambda e, o=c.ap(53, 54), i=c.ap(48, 49): e.tensor_scalar_mul(o, i, -1.0)),
              reads=ck, writes=ck, tag="rnm")
            E("scalar", (lambda e, o=c.ap(16, 32), i=c.ap(0, 16), b=c.ap(53, 54), ac=c.ap(49, 50):
                         e.activation(o, i, AF.Exp, bias=b, accum_out=ac)),
              reads=ck, writes=ck, tag="rexp")
            E("vector", (lambda e, o=c.ap(32, 48), i=c.ap(0, 16), s=c.ap(48, 49):
                         e.tensor_scalar(o, i, s, None, ALU.is_equal)),
              reads=ck, writes=ck, tag="req")
            E("vector", (lambda e, o=c.ap(32, 48), i=c.ap(32, 48), t2=c.ap(0, 16):
                         e.scalar_tensor_tensor(o, i, -1e30, t2, ALU.mult, ALU.add)),
              reads=ck, writes=ck, tag="rmsk")
            E("vector", (lambda e, o=c.ap(50, 51), i=c.ap(32, 48): e.reduce_max(o, i, axis=AX.X)),
              reads=ck, writes=ck, tag="rm2")
            E("vector", (lambda e, o=c.ap(32, 48), i=c.ap(0, 16), s=c.ap(50, 51):
                         e.tensor_scalar(o, i, s, None, ALU.is_ge)),
              reads=ck, writes=ck, tag="rind")
            E("vector", (lambda e, o=c.ap(16, 32), i=c.ap(16, 32), t2=c.ap(32, 48):
                         e.tensor_mul(o, i, t2)),
              reads=ck, writes=ck, tag="rcu")
            E("vector", (lambda e, o=c.ap(51, 52), i=c.ap(16, 32): e.reduce_sum(o, i, axis=AX.X)),
              reads=ck, writes=ck, tag="rs")
            E("vector", (lambda e, o=c.ap(52, 53), i=c.ap(49, 50), t2=c.ap(51, 52):
                         e.scalar_tensor_tensor(o, i, 1e-6, t2, ALU.mult, ALU.add)),
              reads=ck, writes=ck, tag="rden")
            E("vector", (lambda e, o=c.ap(52, 53), i=c.ap(52, 53): e.reciprocal(o, i)),
              reads=ck, writes=ck, tag="rrcp")
            E("vector", (lambda e, o=c.ap(32, 48), i=c.ap(16, 32), s=c.ap(52, 53):
                         e.tensor_scalar(o, i, s, 0.5, ALU.mult, ALU.mult)),
              reads=ck, writes=ck, tag="rcomb")
        lf.free()
        rhfm = self.bf.alloc(1024, "rhfm")
        for th in range(2):
            pk, pap = ps.f()
            for cc in range(32):
                E("tensor",
                  (lambda e, o=pap[0:64, :], l=at_tile.ap(cc * 64, (cc + 1) * 64),
                   r=xfm[cc].ap(th * 512, (th + 1) * 512), s=(cc == 0), t=(cc == 31):
                   e.matmul(o, l, r, start=s, stop=t)),
                  reads=at_tile.keys() + xfm[cc].keys(th * 512, (th + 1) * 512),
                  writes=[pk], tag="rh")
            E("scalar", (lambda e, o=rhfm.ap(th * 512, (th + 1) * 512, 0, 64), i=pap[0:64, :]:
                         e.copy(o, i)),
              reads=[pk], writes=rhfm.keys(th * 512, (th + 1) * 512), tag="rhev")
        whst = [self.bf.alloc(1024, "whst") for _ in range(2)]
        rhtm = self.bf.alloc(512, "rhtm")
        for tt in range(8):
            pk, pap = ps.b()
            E("tensor", (lambda e, o=pap[:, 0:64], i=rhfm.ap(tt * 128, (tt + 1) * 128, 0, 64),
                         idn=self.id_bf.ap(0, 64, 0, 64): e.transpose(o, i, idn)),
              reads=rhfm.keys() + self.id_bf.keys(0, 128), writes=[pk], tag="rhtr")
            E("scalar", (lambda e, o=rhtm.ap(0, 64), i=pap[:, 0:64]: e.copy(o, i)),
              reads=[pk], writes=rhtm.keys(0, 64), tag="rhtm")
            w = whst[tt % 2]
            for ei in range(16):
                E("vector", (lambda e, o=w.ap(ei * 64, (ei + 1) * 64), i=rhtm.ap(0, 64),
                             s=comb[tt].ap(32 + ei, 33 + ei):
                             e.tensor_scalar(o, i, s, None, ALU.mult)),
                  reads=rhtm.keys(0, 64) + comb[tt].keys(), writes=w.keys(ei * 64, (ei + 1) * 64),
                  tag=f"wh{tt}{ei}")
            E("sync", (lambda e, a=w.ap(), o=d[f"whtm{p}"][tt * 128:(tt + 1) * 128, :]:
                       e.dma_start(out=o, in_=a)),
              reads=w.keys(), writes=[("whtm", p, tt)], dma=self.st(), tag=f"whst{tt}")
        rhfm.free(); rhtm.free()
        for t in whst:
            t.free()
        whfm = [self.bf.alloc(1024, f"whfm{ec}") for ec in range(8)]
        for ec in range(8):
            src = d[f"whtm{p}"][:, ec * 128:(ec + 1) * 128]
            E("sync", (lambda e, a=whfm[ec].ap(), s=src: e.dma_start_transpose(out=a, in_=s)),
              reads=[("whtm", p, tt) for tt in range(8)], writes=whfm[ec].keys(),
              dma=("whfm", None), tag=f"whfm{ec}")
        lnst = self.ln_begin(f"l0f{p}")
        prest = [self.bf.alloc(512, "prest") for _ in range(4)]
        t1r = [self.f32a.alloc(512, "t1r") for _ in range(2)]
        ogs = [(i, min(i + 3, 32)) for i in range(0, 32, 3)]
        for (lo, hi) in ogs:
            accb = ws_base.stream_cols(lo * 128, hi * 128)
            accd = ws_bflat.stream_cols(lo * 128, hi * 128)
            banks = ps.fgroup((hi - lo) * 2)
            for ic in range(32):
                for j, oc in enumerate(range(lo, hi)):
                    lap, lkeys = accb(ic, j * 128, (j + 1) * 128)
                    for th in range(2):
                        pk, pap = banks[j * 2 + th]
                        E("tensor",
                          (lambda e, o=pap, l=lap, r=xfm[ic].ap(th * 512, (th + 1) * 512),
                           s=(ic == 0): e.matmul(o, l, r, start=s, stop=False)),
                          reads=lkeys + xfm[ic].keys(th * 512, (th + 1) * 512),
                          writes=[pk], tag=f"bs{oc}{th}")
            for ic in range(8):
                for j, oc in enumerate(range(lo, hi)):
                    lap, lkeys = accd(ic, j * 128, (j + 1) * 128)
                    for th in range(2):
                        pk, pap = banks[j * 2 + th]
                        E("tensor",
                          (lambda e, o=pap, l=lap, r=whfm[ic].ap(th * 512, (th + 1) * 512),
                           t=(ic == 7): e.matmul(o, l, r, start=False, stop=t)),
                          reads=lkeys + whfm[ic].keys(th * 512, (th + 1) * 512),
                          writes=[pk], tag=f"dl{oc}{th}")
            for j, oc in enumerate(range(lo, hi)):
                for th in range(2):
                    pk, pap = banks[j * 2 + th]
                    t1 = t1r[th]
                    E("scalar", (lambda e, o=t1.ap(0, 512), i=pap, b=self.bb_t.ap(oc, oc + 1):
                                 e.activation(o, i, AF.Identity, bias=b)),
                      reads=[pk] + self.bb_t.keys(0, 32), writes=t1.keys(0, 512), tag=f"bb{oc}{th}")
                    pr = prest[(oc * 2 + th) % 4]
                    E("vector", (lambda e, o=pr.ap(0, 512), i=t1.ap(0, 512),
                                 r=xfm[oc].ap(th * 512, (th + 1) * 512):
                                 e.tensor_add(o, i, r)),
                      reads=t1.keys(0, 512) + xfm[oc].keys(th * 512, (th + 1) * 512),
                      writes=pr.keys(0, 512), tag=f"br{oc}{th}")
                    self.ln_chunk(lnst, pr.ap(0, 512), pr.keys(0, 512), th)
                    dst = d[f"prefm{p}"][oc * 128:(oc + 1) * 128, th * 512:(th + 1) * 512]
                    E("sync", (lambda e, a=pr.ap(0, 512), o=dst: e.dma_start(out=o, in_=a)),
                      reads=pr.keys(0, 512), writes=[("prefm", p, oc, th)], dma=self.st(),
                      tag=f"bst{oc}{th}")
        for t in xfm + whfm + comb + prest + t1r:
            t.free()
        self.ln_finalize(lnst)
        self.ln_apply(lnst, p, f"prefm{p}", "x2fm")

    def l1_attn(self, p, ws_qkv, ws_proj):
        d, E, ps = self.d, self.E, self.ps
        xfm = self.load_fm("x2fm", p, "xfm")
        qst = [self.bf.alloc(512, "qst") for _ in range(4)]
        ogs = [(i, min(i + 3, 48)) for i in range(0, 48, 3)]
        for (lo, hi) in ogs:
            accw = ws_qkv.stream_cols(lo * 128, hi * 128)
            banks = ps.fgroup((hi - lo) * 2)
            for hc in range(32):
                for j, pc in enumerate(range(lo, hi)):
                    lap, lkeys = accw(hc, j * 128, (j + 1) * 128)
                    for th in range(2):
                        pk, pap = banks[j * 2 + th]
                        E("tensor",
                          (lambda e, o=pap, l=lap, r=xfm[hc].ap(th * 512, (th + 1) * 512),
                           s=(hc == 0), t=(hc == 31): e.matmul(o, l, r, start=s, stop=t)),
                          reads=lkeys + xfm[hc].keys(th * 512, (th + 1) * 512),
                          writes=[pk], tag=f"qkv1_{pc}{th}")
            for j, pc in enumerate(range(lo, hi)):
                for th in range(2):
                    pk, pap = banks[j * 2 + th]
                    stg = qst[(pc * 2 + th) % 4]
                    E("scalar", (lambda e, o=stg.ap(0, 512), i=pap: e.copy(o, i)),
                      reads=[pk], writes=stg.keys(0, 512), tag=f"qkv1e{pc}{th}")
                    dst = d[f"l1q{p}"][pc * 128:(pc + 1) * 128, th * 512:(th + 1) * 512]
                    E("sync", (lambda e, a=stg.ap(0, 512), o=dst: e.dma_start(out=o, in_=a)),
                      reads=stg.keys(0, 512), writes=[("l1q", p, pc, th)], dma=self.st(),
                      tag=f"qkv1s{pc}{th}")
        for t in xfm + qst:
            t.free()
        attn32 = [self.bf.alloc(1024, f"a32_{h}") for h in range(32)]
        qg = [[self.bf.alloc(1024, "qg") for _ in range(6)] for _ in range(2)]
        scr = self.bf.alloc(2048, "l1scr")
        scrf = self.f32a.alloc(2048, "l1scrf")
        srow = self.f32a.alloc(1024, "srow")
        wrow = self.bf.alloc(512, "wrow")
        for g in range(8):
            qs = qg[g % 2]
            for pc6 in range(6):
                pc = g * 6 + pc6
                src_ = d[f"l1q{p}"][pc * 128:(pc + 1) * 128, :]
                E("sync", (lambda e, a=qs[pc6].ap(), s=src_: e.dma_start(out=a, in_=s)),
                  reads=[("l1q", p, pc, th) for th in range(2)], writes=qs[pc6].keys(),
                  dma=("l1ld", g % 3, (p, g)), tag=f"l1ld{pc}")
            k, v = qs[4], qs[5]
            E("vector", (lambda e, o=scrf.ap(0, 512), a=k.ap(512, 1024), b=k.ap(0, 512):
                         e.tensor_sub(o, a, b)),
              reads=k.keys(), writes=scrf.keys(0, 512), tag=f"kd{g}")
            E("vector", (lambda e, o=scr.ap(512, 1024), a=v.ap(512, 1024), b=v.ap(0, 512):
                         e.tensor_sub(o, a, b)),
              reads=v.keys(), writes=scr.keys(512, 1024), tag=f"vd{g}")
            for j in range(4):
                h = g * 4 + j
                at = attn32[h]
                E("scalar", (lambda e, o=at.ap(0, 512), i=v.ap(0, 512): e.copy(o, i)),
                  reads=v.keys(), writes=at.keys(0, 512), tag=f"o0_{h}")
                E("vector", (lambda e, o=scrf.ap(512, 1024), a=qs[j].ap(512, 1024),
                             b=scrf.ap(0, 512): e.tensor_mul(o, a, b)),
                  reads=qs[j].keys() + scrf.keys(0, 512), writes=scrf.keys(512, 1024),
                  tag=f"pr{h}")
                pkD, papD = ps.f()
                E("tensor", (lambda e, o=papD[0:1, :], l=self.ones_col_f.ap(0, 1),
                             r=scrf.ap(512, 1024): e.matmul(o, l, r, start=True, stop=True)),
                  reads=self.ones_col_f.keys(0, 1) + scrf.keys(512, 1024), writes=[pkD],
                  tag=f"dot{h}")
                E("scalar", (lambda e, o=srow.ap(0, 512, 0, 1), i=papD[0:1, :]:
                             e.activation(o, i, AF.Exp, scale=-L1_SCALE)),
                  reads=[pkD], writes=srow.keys(0, 512), tag=f"se{h}")
                E("vector", (lambda e, o=srow.ap(512, 1024, 0, 1), i=srow.ap(0, 512, 0, 1):
                             e.tensor_scalar_add(o, i, 1.0)),
                  reads=srow.keys(0, 512), writes=srow.keys(512, 1024), tag=f"sd{h}")
                E("vector", (lambda e, o=srow.ap(512, 1024, 0, 1), i=srow.ap(512, 1024, 0, 1):
                             e.reciprocal(o, i)),
                  reads=srow.keys(512, 1024), writes=srow.keys(512, 1024), tag=f"sr{h}")
                E("vector", (lambda e, o=wrow.ap(0, 512, 0, 1), i=srow.ap(512, 1024, 0, 1):
                             e.tensor_copy(o, i)),
                  reads=srow.keys(512, 1024), writes=wrow.keys(), tag=f"swb{h}")
                pkB, papB = ps.f()
                E("tensor", (lambda e, o=papB, l=self.ones_row.ap(0, 128, 0, 1),
                             r=wrow.ap(0, 512, 0, 1): e.matmul(o, l, r, start=True, stop=True)),
                  reads=self.ones_row.keys(0, 128) + wrow.keys(), writes=[pkB], tag=f"wb{h}")
                E("vector", (lambda e, o=scrf.ap(1024, 1536), a=scr.ap(512, 1024), b=papB:
                             e.tensor_mul(o, a, b)),
                  reads=scr.keys(512, 1024) + [pkB], writes=scrf.keys(1024, 1536), tag=f"wm{h}")
                E("vector", (lambda e, o=at.ap(512, 1024), a=scrf.ap(1024, 1536), b=v.ap(0, 512):
                             e.tensor_add(o, a, b)),
                  reads=scrf.keys(1024, 1536) + v.keys(), writes=at.keys(512, 1024), tag=f"o1_{h}")
        for s in qg:
            for t in s:
                t.free()
        scr.free(); scrf.free(); srow.free(); wrow.free()
        lnst = self.ln_begin(f"l1a{p}")
        self.proj_to_prefm(p, ws_proj, attn32, None, [], "x2fm", lnst)
        for t in attn32:
            t.free()
        self.ln_finalize(lnst)
        self.ln_apply(lnst, p, f"prefm{p}", "x3fm")

    def l1_ffn(self, p, ws_up, ws_down, ws_fc1, ws_fc2, ws_exp, gw_tile):
        d, E, ps = self.d, self.E, self.ps
        xfm = self.load_fm("x3fm", p, "xfm")
        hst = [self.bf.alloc(512, "hst") for _ in range(4)]
        ogs = [(i, min(i + 3, 64)) for i in range(0, 64, 3)]
        for (lo, hi) in ogs:
            accw = ws_up.stream_cols(lo * 128, hi * 128)
            banks = ps.fgroup((hi - lo) * 2)
            for cc in range(32):
                for j, io in enumerate(range(lo, hi)):
                    lap, lkeys = accw(cc, j * 128, (j + 1) * 128)
                    for th in range(2):
                        pk, pap = banks[j * 2 + th]
                        E("tensor",
                          (lambda e, o=pap, l=lap, r=xfm[cc].ap(th * 512, (th + 1) * 512),
                           s=(cc == 0), t=(cc == 31): e.matmul(o, l, r, start=s, stop=t)),
                          reads=lkeys + xfm[cc].keys(th * 512, (th + 1) * 512),
                          writes=[pk], tag=f"up{io}{th}")
            for j, io in enumerate(range(lo, hi)):
                for th in range(2):
                    pk, pap = banks[j * 2 + th]
                    hs = hst[(io * 2 + th) % 4]
                    E("scalar", (lambda e, o=hs.ap(0, 512), i=pap: e.activation(o, i, AF.Relu)),
                      reads=[pk], writes=hs.keys(0, 512), tag=f"uprl{io}{th}")
                    E("vector", (lambda e, o=hs.ap(0, 512), i=hs.ap(0, 512): e.tensor_mul(o, i, i)),
                      reads=hs.keys(0, 512), writes=hs.keys(0, 512), tag=f"upsq{io}{th}")
                    dst = d[f"hdr{p}"][io * 128:(io + 1) * 128, th * 512:(th + 1) * 512]
                    E("sync", (lambda e, a=hs.ap(0, 512), o=dst: e.dma_start(out=o, in_=a)),
                      reads=hs.keys(0, 512), writes=[("hdr", p, io, th)], dma=self.st(),
                      tag=f"upst{io}{th}")
        for t in hst:
            t.free()

        rwrow = self.f32a.alloc(1024, "rwrow")
        rwb = [self.f32a.alloc(512, f"rwb{th}") for th in range(2)]
        gl = self.f32a.alloc(512, "gl")
        glk = gl.keys()
        for tt in range(8):
            pk, pap = ps.f()
            for cc in range(32):
                E("tensor",
                  (lambda e, o=pap[:, 0:8], l=xfm[cc].ap(tt * 128, (tt + 1) * 128),
                   r=gw_tile.ap(cc * 8, (cc + 1) * 8), s=(cc == 0), t=(cc == 31):
                   e.matmul(o, l, r, start=s, stop=t)),
                  reads=gw_tile.keys() + xfm[cc].keys(tt * 128, (tt + 1) * 128),
                  writes=[pk], tag=f"gt{tt}")
            E("scalar", (lambda e, o=gl.ap(0, 8), i=pap[:, 0:8]: e.copy(o, i)),
              reads=[pk], writes=glk, tag=f"gtev{tt}")
            E("vector", (lambda e, o=gl.ap(16, 17), i=gl.ap(0, 8): e.reduce_max(o, i, axis=AX.X)),
              reads=glk, writes=glk, tag="gm1")
            E("vector", (lambda e, o=gl.ap(8, 16), i=gl.ap(0, 8), s=gl.ap(16, 17):
                         e.tensor_scalar(o, i, s, None, ALU.is_equal)),
              reads=glk, writes=glk, tag="geq")
            E("vector", (lambda e, o=gl.ap(8, 16), i=gl.ap(8, 16), t2=gl.ap(0, 8):
                         e.scalar_tensor_tensor(o, i, -1e30, t2, ALU.mult, ALU.add)),
              reads=glk, writes=glk, tag="gmsk")
            E("vector", (lambda e, o=gl.ap(17, 18), i=gl.ap(8, 16): e.reduce_max(o, i, axis=AX.X)),
              reads=glk, writes=glk, tag="gm2")
            E("vector", (lambda e, o=gl.ap(18, 19), a=gl.ap(16, 17), b=gl.ap(17, 18):
                         e.tensor_add(o, a, b)),
              reads=glk, writes=glk, tag="grw")
            pkT, papT = ps.f()
            E("tensor", (lambda e, o=papT[0:1, 0:128], i=gl.ap(18, 19),
                         idn=self.id_f32.ap(0, 128): e.transpose(o, i, idn)),
              reads=glk + self.id_f32.keys(0, 128), writes=[pkT], tag=f"gtr{tt}")
            E("scalar", (lambda e, o=rwrow.ap(tt * 128, (tt + 1) * 128, 0, 1), i=papT[0:1, 0:128]:
                         e.copy(o, i)),
              reads=[pkT], writes=rwrow.keys(), tag=f"grow{tt}")
        gl.free()
        rwrow_bf = self.bf.alloc(1024, "rwrow_bf")
        E("vector", (lambda e, o=rwrow_bf.ap(0, 1024, 0, 1), i=rwrow.ap(0, 1024, 0, 1):
                     e.tensor_copy(o, i)),
          reads=rwrow.keys(), writes=rwrow_bf.keys(), tag="growbf")
        for th in range(2):
            pkB, papB = ps.f()
            E("tensor", (lambda e, o=papB, l=self.ones_row.ap(0, 128, 0, 1),
                         r=rwrow_bf.ap(th * 512, (th + 1) * 512, 0, 1):
                         e.matmul(o, l, r, start=True, stop=True)),
              reads=self.ones_row.keys(0, 128) + rwrow_bf.keys(), writes=[pkB], tag=f"rwb{th}")
            E("scalar", (lambda e, o=rwb[th].ap(0, 512), i=papB: e.copy(o, i)),
              reads=[pkB], writes=rwb[th].keys(), tag=f"rwbe{th}")
        xlat = [self.bf.alloc(1024, f"xlat{lc}") for lc in range(16)]
        for (lo, hi) in [(i, min(i + 3, 16)) for i in range(0, 16, 3)]:
            accw = ws_fc1.stream_cols(lo * 128, hi * 128)
            banks = ps.fgroup((hi - lo) * 2)
            for cc in range(32):
                for j, lc in enumerate(range(lo, hi)):
                    lap, lkeys = accw(cc, j * 128, (j + 1) * 128)
                    for th in range(2):
                        pk, pap = banks[j * 2 + th]
                        E("tensor",
                          (lambda e, o=pap, l=lap, r=xfm[cc].ap(th * 512, (th + 1) * 512),
                           s=(cc == 0), t=(cc == 31): e.matmul(o, l, r, start=s, stop=t)),
                          reads=lkeys + xfm[cc].keys(th * 512, (th + 1) * 512),
                          writes=[pk], tag=f"f1{lc}{th}")
            for j, lc in enumerate(range(lo, hi)):
                for th in range(2):
                    pk, pap = banks[j * 2 + th]
                    E("scalar", (lambda e, o=xlat[lc].ap(th * 512, (th + 1) * 512), i=pap:
                                 e.copy(o, i)),
                      reads=[pk], writes=xlat[lc].keys(th * 512, (th + 1) * 512), tag=f"f1e{lc}{th}")
        for t in xfm:
            t.free()
        moe = [self.bf.alloc(1024, f"moe{mc}") for mc in range(16)]
        for (lo, hi) in [(i, min(i + 3, 16)) for i in range(0, 16, 3)]:
            accw = ws_exp.stream_cols(lo * 128, hi * 128)
            banks = ps.fgroup((hi - lo) * 2)
            for lc in range(16):
                for j, mc in enumerate(range(lo, hi)):
                    lap, lkeys = accw(lc, j * 128, (j + 1) * 128)
                    for th in range(2):
                        pk, pap = banks[j * 2 + th]
                        E("tensor",
                          (lambda e, o=pap, l=lap, r=xlat[lc].ap(th * 512, (th + 1) * 512),
                           s=(lc == 0), t=(lc == 15): e.matmul(o, l, r, start=s, stop=t)),
                          reads=lkeys + xlat[lc].keys(th * 512, (th + 1) * 512),
                          writes=[pk], tag=f"ex{mc}{th}")
            for j, mc in enumerate(range(lo, hi)):
                for th in range(2):
                    pk, pap = banks[j * 2 + th]
                    E("vector", (lambda e, o=moe[mc].ap(th * 512, (th + 1) * 512), i=pap,
                                 r=rwb[th].ap(0, 512): e.tensor_mul(o, i, r)),
                      reads=[pk] + rwb[th].keys(), writes=moe[mc].keys(th * 512, (th + 1) * 512),
                      tag=f"exe{mc}{th}")
        for t in xlat:
            t.free()
        routed = [self.bf.alloc(1024, f"rt{hc}") for hc in range(32)]
        for (lo, hi) in [(i, min(i + 3, 32)) for i in range(0, 32, 3)]:
            accw = ws_fc2.stream_cols(lo * 128, hi * 128)
            banks = ps.fgroup((hi - lo) * 2)
            for lc in range(16):
                for j, hc in enumerate(range(lo, hi)):
                    lap, lkeys = accw(lc, j * 128, (j + 1) * 128)
                    for th in range(2):
                        pk, pap = banks[j * 2 + th]
                        E("tensor",
                          (lambda e, o=pap, l=lap, r=moe[lc].ap(th * 512, (th + 1) * 512),
                           s=(lc == 0), t=(lc == 15): e.matmul(o, l, r, start=s, stop=t)),
                          reads=lkeys + moe[lc].keys(th * 512, (th + 1) * 512),
                          writes=[pk], tag=f"f2{hc}{th}")
            for j, hc in enumerate(range(lo, hi)):
                for th in range(2):
                    pk, pap = banks[j * 2 + th]
                    E("scalar", (lambda e, o=routed[hc].ap(th * 512, (th + 1) * 512), i=pap:
                                 e.copy(o, i)),
                      reads=[pk], writes=routed[hc].keys(th * 512, (th + 1) * 512),
                      tag=f"f2e{hc}{th}")
        for t in moe:
            t.free()
        lnst = self.ln_begin(f"l1f{p}")
        hrd = [self.bf.alloc(1024, "hrd") for _ in range(3)]
        prest = [self.bf.alloc(512, "prest") for _ in range(4)]
        for og, (lo, hi) in enumerate([(i, min(i + 3, 32)) for i in range(0, 32, 3)]):
            accw = ws_down.stream_cols(lo * 128, hi * 128)
            banks = ps.fgroup((hi - lo) * 2)
            for ic in range(64):
                hb = hrd[ic % 3]
                src_ = d[f"hdr{p}"][ic * 128:(ic + 1) * 128, :]
                E("sync", (lambda e, a=hb.ap(0, 1024), s=src_: e.dma_start(out=a, in_=s)),
                  reads=[("hdr", p, ic, 0), ("hdr", p, ic, 1)], writes=hb.keys(0, 1024),
                  dma=("hrd", (og * 64 + ic) % 3), tag=f"hld{og}{ic}")
                for j, oc in enumerate(range(lo, hi)):
                    lap, lkeys = accw(ic, j * 128, (j + 1) * 128)
                    for th in range(2):
                        pk, pap = banks[j * 2 + th]
                        E("tensor",
                          (lambda e, o=pap, l=lap, r=hb.ap(th * 512, (th + 1) * 512),
                           s=(ic == 0), t=(ic == 63): e.matmul(o, l, r, start=s, stop=t)),
                          reads=lkeys + hb.keys(th * 512, (th + 1) * 512), writes=[pk],
                          tag=f"dn{oc}{th}")
            for j, oc in enumerate(range(lo, hi)):
                for th in range(2):
                    pk, pap = banks[j * 2 + th]
                    pr = prest[(oc * 2 + th) % 4]
                    E("vector", (lambda e, o=pr.ap(0, 512), i=pap,
                                 r=routed[oc].ap(th * 512, (th + 1) * 512): e.tensor_add(o, i, r)),
                      reads=[pk] + routed[oc].keys(th * 512, (th + 1) * 512),
                      writes=pr.keys(0, 512), tag=f"dne{oc}{th}")
                    self.ln_chunk(lnst, pr.ap(0, 512), pr.keys(0, 512), th)
                    dst = d[f"prefm{p}"][oc * 128:(oc + 1) * 128, th * 512:(th + 1) * 512]
                    E("sync", (lambda e, a=pr.ap(0, 512), o=dst: e.dma_start(out=o, in_=a)),
                      reads=pr.keys(0, 512), writes=[("prefm", p, oc, th)], dma=self.st(),
                      tag=f"dnst{oc}{th}")
        for t in hrd + prest + rwb + [rwrow, rwrow_bf]:
            t.free()

        self.ln_finalize(lnst)
        pre_keys = [("prefm", p, oc, th) for oc in range(32) for th in range(2)]
        xin2 = [self.bf.alloc(1024, "epx") for _ in range(3)]
        x3r = [self.bf.alloc(1024, "epx3") for _ in range(3)]
        za = [self.f32a.alloc(1024, "epza") for _ in range(2)]
        zbf = [self.bf.alloc(1024, "epzb") for _ in range(2)]
        for cc in range(32):
            t = xin2[cc % 3]
            E("sync", (lambda e, a=t.ap(), s=d[f"prefm{p}"][cc * 128:(cc + 1) * 128, :]:
                       e.dma_start(out=a, in_=s)),
              reads=pre_keys, writes=t.keys(), dma=("lnld", cc % 3), tag=f"epld{cc}")
            x3 = x3r[cc % 3]
            src_ = d["x3fm"][cc * 128:(cc + 1) * 128, p * 1024:(p + 1) * 1024]
            E("sync", (lambda e, a=x3.ap(), s=src_: e.dma_start(out=a, in_=s)),
              reads=[("x3fm", p, cc)], writes=x3.keys(), dma=("res", cc % 3), tag=f"epx3{cc}")
            z = za[cc % 2]
            E("vector", (lambda e, o=z.ap(), a=t.ap(), b=lnst["rbf"].ap(0, 1024):
                         e.tensor_mul(o, a, b)),
              reads=t.keys() + lnst["rbf"].keys(), writes=z.keys(), tag=f"ep1_{cc}")
            E("vector", (lambda e, o=z.ap(), a=z.ap(), b=lnst["nbf"].ap(0, 1024):
                         e.tensor_add(o, a, b)),
              reads=z.keys() + lnst["nbf"].keys(), writes=z.keys(), tag=f"ep2_{cc}")
            zb = zbf[cc % 2]
            E("vector", (lambda e, o=zb.ap(), i=z.ap(), g=self.gam_t.ap(cc, cc + 1),
                         b=self.bet_t.ap(cc, cc + 1):
                         e.tensor_scalar(o, i, g, b, ALU.mult, ALU.add)),
              reads=z.keys() + self.gam_t.keys(0, 32) + self.bet_t.keys(0, 32),
              writes=zb.keys(), tag=f"ep3_{cc}")
            pre2 = routed[cc]
            E("vector", (lambda e, o=pre2.ap(), a=zb.ap(), b=x3.ap(): e.tensor_add(o, a, b)),
              reads=zb.keys() + x3.keys(), writes=pre2.keys(), tag=f"ep4_{cc}")
        for t in xin2 + x3r + za + zbf:
            t.free()
        for k in ("srow", "rbf", "nbf", "rowbf"):
            lnst[k].free()
        for t in lnst["sq"]:
            t.free()
        pretm = [self.bf.alloc(4096, "pretm") for _ in range(2)]
        ostg = [self.f32a.alloc(1024, "ostg") for _ in range(2)]
        stat = self.f32a.alloc(512, "stat2")
        sk = stat.keys()
        for tt in range(8):
            t = pretm[tt % 2]
            for ccg in range(8):
                pkT, papT = ps.b()
                for j in range(4):
                    cc = ccg * 4 + j
                    E("tensor", (lambda e, o=papT[:, j * 128:(j + 1) * 128],
                                 i=routed[cc].ap(tt * 128, (tt + 1) * 128),
                                 idn=self.id_bf.ap(0, 128): e.transpose(o, i, idn)),
                      reads=routed[cc].keys(tt * 128, (tt + 1) * 128) + self.id_bf.keys(0, 128),
                      writes=[pkT], tag=f"eptr{tt}{ccg}")
                E("scalar", (lambda e, o=t.ap(ccg * 512, (ccg + 1) * 512), i=papT[:, 0:512]:
                             e.copy(o, i)),
                  reads=[pkT], writes=t.keys(ccg * 512, (ccg + 1) * 512), tag=f"eptre{tt}{ccg}")
            self._ln_stats(t, stat, sk, 50, 51)
            b_, q0 = tt // 4, (tt % 4) * 128
            orow = b_ * 1024 + p * 512 + q0
            for ch in range(4):
                o = ostg[ch % 2]
                E("vector", (lambda e, o_=o.ap(), i=t.ap(ch * 1024, (ch + 1) * 1024),
                             r=stat.ap(50, 51), m=stat.ap(51, 52):
                             e.tensor_scalar(o_, i, r, m, ALU.mult, ALU.add)),
                  reads=t.keys() + sk, writes=o.keys(), tag=f"eout{tt}{ch}")
                dst = d["out"][orow:orow + 128, ch * 1024:(ch + 1) * 1024]
                E("sync", (lambda e, a=o.ap(), o_=dst: e.dma_start(out=o_, in_=a)),
                  reads=o.keys(), writes=[("out", p, tt, ch)], dma=self.st(), tag=f"est{tt}{ch}")
        for t in pretm + ostg + [stat]:
            t.free()
        for t in routed:
            t.free()


def build_program(part, taps=()):
    mk = MK(part, taps)
    d = mk.d
    if part in ("s1", "all"):
        ws_qkv = WStream(mk, "wqkv", d["w_l0qkv"])
        for p in range(2):
            mk.s1(p, ws_qkv)
            if part == "all":
                mk.allgather(p)
    if part == "all":
        mk.bqkv.free(); mk.bq_s.free(); mk.vbias.free()
    if part in ("rest", "all"):
        rt = mk.bf.alloc(512, "rt")
        at = mk.bf.alloc(2048, "at")
        gw = mk.bf.alloc(512, "gw")
        for tile, name, width in ((rt, "r_t", 16), (at, "a_t", 64), (gw, "gate_w", 8)):
            src = d[name].rearrange("(cc p) e -> p cc e", p=128)
            dst = tile.ap(0, 32 * width).rearrange("p (cc e) -> p cc e", e=width)
            mk.E("gpsimd", (lambda e, a=dst, s=src: e.dma_start(out=a, in_=s)),
                 writes=tile.keys(0, 32 * width), dma=("cst", None), tag=f"ld_{name}")
        ws_o = WStream(mk, "wo", d["w_l0o"])
        ws_base = WStream(mk, "wbase", d["w_l0base_t"])
        ws_bflat = WStream(mk, "wbflat", d["b_flat"])
        ws_l1qkv = WStream(mk, "wl1qkv", d["w_l1qkv_t"])
        ws_l1proj = WStream(mk, "wl1proj", d["w_l1proj_t"])
        ws_up = WStream(mk, "wup", d["up_w"])
        ws_down = WStream(mk, "wdown", d["down_w"])
        ws_fc1 = WStream(mk, "wfc1", d["fc1_w"])
        ws_fc2 = WStream(mk, "wfc2", d["fc2_w"])
        ws_exp = WStream(mk, "wexp", d["expert_w"])
        for p in range(2):
            mk.l0_attn(p, ws_o)
            mk.tap("x1fm", [("x1fm", p, cc) for cc in range(32)])
        for p in range(2):
            mk.l0_ffn(p, ws_base, ws_bflat, rt, at)
            mk.tap("x2fm", [("x2fm", p, cc) for cc in range(32)])
        for p in range(2):
            mk.l1_attn(p, ws_l1qkv, ws_l1proj)
            mk.tap("x3fm", [("x3fm", p, cc) for cc in range(32)])
        for p in range(2):
            mk.l1_ffn(p, ws_up, ws_down, ws_fc1, ws_fc2, ws_exp, gw)
    mk.plan.build()
    return mk


_progs = {}


def _get_prog(part, taps=()):
    key = (part, tuple(sorted(taps)))
    if key not in _progs:
        _progs[key] = build_program(part, taps)
    return _progs[key]


REST_W = ("w_l0o", "b_l0o", "w_l0base_t", "b_l0base", "a_t", "b_flat", "r_t",
          "w_l1qkv_t", "w_l1proj_t", "gate_w", "up_w", "down_w", "fc1_w",
          "fc2_w", "expert_w", "ln_g", "ln_b")


def host_layout(inputs):
    C = np.ascontiguousarray
    w = {}
    w["w_l0qkv"] = C(inputs["L0_attn_Wqkv"])
    w["b_l0qkv"] = C(inputs["L0_attn_bqkv"]).reshape(1, 3072)
    w["w_l0o"] = C(inputs["L0_attn_Wo"])
    w["b_l0o"] = C(inputs["L0_attn_bo"]).reshape(1, 4096)
    w["w_l0base_t"] = C(inputs["L0_ffn_base_w"].T)
    w["b_l0base"] = C(inputs["L0_ffn_base_b"]).reshape(1, 4096)
    w["a_t"] = C(inputs["L0_ffn_A"].T)
    w["b_flat"] = C(np.transpose(inputs["L0_ffn_B"], (0, 2, 1)).reshape(1024, 4096))
    w["r_t"] = C(inputs["L0_ffn_router_w"].T)
    w["w_l1qkv_t"] = C(inputs["L1_attn_weight_qkv"].T)
    w["w_l1proj_t"] = C(inputs["L1_attn_weight_proj"].T)
    w["gate_w"] = C(inputs["L1_ffn_gate_w"])
    w["up_w"] = C(inputs["L1_ffn_up_w"])
    w["down_w"] = C(inputs["L1_ffn_down_w"])
    w["fc1_w"] = C(inputs["L1_ffn_fc1_w"])
    w["fc2_w"] = C(inputs["L1_ffn_fc2_w"])
    w["expert_w"] = C(inputs["L1_ffn_expert_w"])
    w["ln_g"] = C(inputs["L1_ffn_ln_gamma"]).reshape(1, 4096)
    w["ln_b"] = C(inputs["L1_ffn_ln_beta"]).reshape(1, 4096)
    return w


def shard_x(x):
    return [np.ascontiguousarray(x[:, r * 1024:(r + 1) * 1024, :].reshape(2048, DMODEL))
            for r in range(NCORES)]


FUSED = True


def kernel(**inputs):
    w = host_layout(inputs)
    xs = shard_x(np.asarray(inputs["L0_attn_x"]))
    if FUSED:
        mk = _get_prog("all")
        ins = [{**{n: w[n] for n in REST_W}, "x0": xs[r],
                "w_l0qkv": w["w_l0qkv"], "b_l0qkv": w["b_l0qkv"]}
               for r in range(NCORES)]
        res = run_bass_kernel_spmd(mk.nc, ins, core_ids=list(range(NCORES)))
        out = np.empty((2, SEQ, DMODEL), np.float32)
        for r in range(NCORES):
            out[:, r * 1024:(r + 1) * 1024, :] = res.results[r]["out"].reshape(2, 1024, DMODEL)
        return out
    mk1 = _get_prog("s1")
    in1 = [{"x0": xs[r], "w_l0qkv": w["w_l0qkv"], "b_l0qkv": w["b_l0qkv"]}
           for r in range(NCORES)]
    r1 = run_bass_kernel_spmd(mk1.nc, in1, core_ids=list(range(NCORES)))
    ksp = {f"kout{p}": np.concatenate([r1.results[r]["kin"][p * 2048:(p + 1) * 2048]
                                        for r in range(NCORES)], axis=0) for p in range(2)}
    vsp = {f"vout{p}": np.concatenate([r1.results[r]["vin"][p * 1024:(p + 1) * 1024]
                                       for r in range(NCORES)], axis=0) for p in range(2)}
    mk2 = _get_prog("rest")
    in2 = [{**{n: w[n] for n in REST_W}, **ksp, **vsp,
            "qdr": r1.results[r]["qdr"], "x0fm": r1.results[r]["x0fm"]}
           for r in range(NCORES)]
    r2 = run_bass_kernel_spmd(mk2.nc, in2, core_ids=list(range(NCORES)))
    out = np.empty((2, SEQ, DMODEL), np.float32)
    for r in range(NCORES):
        out[:, r * 1024:(r + 1) * 1024, :] = r2.results[r]["out"].reshape(2, 1024, DMODEL)
    return out


# version 10
# speedup vs baseline: 1.0368x; 1.0083x over previous
import numpy as np
import ml_dtypes
import concourse.bass as bass
import concourse.mybir as mybir
from concourse.bass_utils import run_bass_kernel_spmd

F32 = mybir.dt.float32
BF16 = mybir.dt.bfloat16
AF = mybir.ActivationFunctionType
ALU = mybir.AluOpType
AX = mybir.AxisListType
BFNP = ml_dtypes.bfloat16

NCORES = 8
DMODEL = 4096
CC = 32
TP = 1024
NP2 = 2
SEQ = 8192
POS_PER_CORE = 1024
LN_EPS = 1e-5
L0_SCALE = 1.0 / 32.0
L1_SCALE = 1.0 / (128.0 ** 0.5)

ENGINES = ("tensor", "scalar", "vector", "gpsimd", "sync")


class Op:
    __slots__ = ("eng", "fn", "deps", "is_dma", "dma_key", "consumed",
                 "event", "vc", "waits", "inc", "idx", "tag")

    def __init__(self):
        self.deps = ()
        self.consumed = False
        self.event = None
        self.vc = None
        self.waits = None
        self.inc = None


class Plan:
    def __init__(self, nc):
        self.nc = nc
        self.ops = []
        self.by_eng = {e: [] for e in ENGINES}
        self.res = {}
        self.n_sems = 0

    def emit(self, eng, fn, reads=(), writes=(), dma=None, tag=""):
        deps = set()
        for k in reads:
            e = self.res.get(k)
            if e is not None and e[0] is not None:
                deps.add(e[0])
        for k in writes:
            e = self.res.get(k)
            if e is not None:
                if e[0] is not None:
                    deps.add(e[0])
                deps.update(e[1])
        op = Op()
        op.eng = eng
        op.fn = fn
        op.is_dma = dma is not None
        op.dma_key = dma
        op.tag = tag
        op.idx = len(self.ops)
        kept = []
        for d in deps:
            if eng == "tensor" and d.eng == "tensor" and not d.is_dma and not op.is_dma:
                continue
            kept.append(d)
        op.deps = tuple(kept)
        self.ops.append(op)
        self.by_eng[eng].append(op)
        for k in reads:
            ent = self.res.get(k)
            if ent is None:
                self.res[k] = [None, [op]]
            else:
                ent[1].append(op)
        for k in writes:
            self.res[k] = [op, []]
        return op

    def build(self):
        nc = self.nc
        for op in self.ops:
            for d in op.deps:
                d.consumed = True
        eng_sem = {e: nc.alloc_semaphore(name=f"sq_{e}") for e in ENGINES}
        self.n_sems = len(ENGINES)
        dma_sems = {}
        fam_counts = {}
        fam_gen_pending = {}

        def get_dma_sem(semkey):
            ent = dma_sems.get(semkey)
            if ent is None:
                h = nc.alloc_semaphore(name=f"dm{len(dma_sems)}")
                self.n_sems += 1
                ent = dma_sems[semkey] = [h, 0]
            return ent

        cnt = {e: 0 for e in ENGINES}
        vc = {e: {} for e in ENGINES}

        for op in self.ops:
            myvc = vc[op.eng]
            needed = {}
            for d in op.deps:
                assert d.event is not None, f"dep without event: {d.tag}"
                s, v = d.event
                if d.is_dma and d.dma_key[1] is None:
                    semkey = ("F", d.dma_key[0])
                    v = dma_sems[semkey][1]
                if myvc.get(s, 0) < v and needed.get(s, 0) < v:
                    needed[s] = v
            for d in op.deps:
                for s2, v2 in d.vc.items():
                    if myvc.get(s2, 0) < v2:
                        myvc[s2] = v2
            for s, v in needed.items():
                if myvc.get(s, 0) < v:
                    myvc[s] = v
            if op.is_dma:
                fam, slot = op.dma_key[0], op.dma_key[1]
                gen = op.dma_key[2] if len(op.dma_key) > 2 else None
                if slot is None:
                    semkey = ("F", fam)
                else:
                    semkey = ("S", fam, slot)
                ent = get_dma_sem(semkey)
                h, prev = ent[0], ent[1]
                same_gen = gen is not None and len(ent) > 2 and ent[2] == gen
                if prev > 0 and not same_gen and myvc.get(id(h), 0) < prev:
                    if needed.get(id(h), 0) < prev:
                        needed[id(h)] = prev
                    myvc[id(h)] = prev
                ent[1] = prev + 16
                if len(ent) > 2:
                    ent[2] = gen
                else:
                    ent.append(gen)
                op.event = (id(h), ent[1])
                op.inc = (h, 16)
            elif op.consumed:
                cnt[op.eng] += 1
                op.event = (id(eng_sem[op.eng]), cnt[op.eng])
                op.inc = (eng_sem[op.eng], 1)
            else:
                op.event = (id(eng_sem[op.eng]), cnt[op.eng])
                op.inc = None
            op.waits = needed
            snap = dict(myvc)
            if op.inc is not None:
                snap[op.event[0]] = op.event[1]
            op.vc = snap

        id2h = {}
        for e in ENGINES:
            id2h[id(eng_sem[e])] = eng_sem[e]
        for ent in dma_sems.values():
            id2h[id(ent[0])] = ent[0]

        tail = [(eng_sem[e], cnt[e]) for e in ENGINES if cnt[e] > 0]
        tail += [(ent[0], ent[1]) for ent in dma_sems.values()]

        with nc.Block() as block:
            def mk(eng_ops, is_sync):
                def body(eng):
                    for op in eng_ops:
                        for s, v in sorted(op.waits.items(), key=lambda kv: kv[1]):
                            eng.wait_ge(id2h[s], v)
                        inst = op.fn(eng)
                        if op.inc is not None:
                            inst.then_inc(op.inc[0], op.inc[1])
                    if is_sync:
                        for h, v in tail:
                            eng.wait_ge(h, v)
                        for h, _ in tail:
                            eng.sem_clear(h)
                return body
            for e in ENGINES:
                if self.by_eng[e] or e == "sync":
                    getattr(block, e)(mk(self.by_eng[e], e == "sync"))
        return eng_sem, dma_sems


class Arena:
    QUANT = 512

    def __init__(self, nc, name, dtype, cols):
        self.name = name
        self.dtype = dtype
        self.cols = cols
        self.t = nc.alloc_sbuf_tensor(name, [128, cols], dtype).ap()
        self.free_list = [(0, cols)]
        self.peak = 0
        self.used = 0
        self.live = {}

    def alloc(self, cols, tag=""):
        cols = ((cols + self.QUANT - 1) // self.QUANT) * self.QUANT
        for i, (off, size) in enumerate(self.free_list):
            if size >= cols:
                if size == cols:
                    self.free_list.pop(i)
                else:
                    self.free_list[i] = (off + cols, size - cols)
                self.used += cols
                self.peak = max(self.peak, self.used)
                t = Tile(self, off, cols)
                self.live[t] = tag
                return t
        raise MemoryError(f"arena {self.name} out of space for {cols} ({tag}); "
                          f"used={self.used}/{self.cols} live={sorted(self.live.values())}")

    def free(self, tile):
        self.live.pop(tile, None)
        self.used -= tile.cols
        self.free_list.append((tile.off, tile.cols))
        self.free_list.sort()
        merged = []
        for off, size in self.free_list:
            if merged and merged[-1][0] + merged[-1][1] == off:
                merged[-1] = (merged[-1][0], merged[-1][1] + size)
            else:
                merged.append((off, size))
        self.free_list = [tuple(x) for x in merged]


class Tile:
    __slots__ = ("arena", "off", "cols")

    def __init__(self, arena, off, cols):
        self.arena = arena
        self.off = off
        self.cols = cols

    def ap(self, c0=0, c1=None, p0=0, p1=128):
        c1 = self.cols if c1 is None else c1
        return self.arena.t[p0:p1, self.off + c0:self.off + c1]

    def keys(self, c0=0, c1=None):
        c1 = self.cols if c1 is None else c1
        q = Arena.QUANT
        lo = (self.off + c0) // q
        hi = (self.off + c1 + q - 1) // q
        return [(self.arena.name, i) for i in range(lo, hi)]

    def free(self):
        self.arena.free(self)


def _ceil(a, b):
    return (a + b - 1) // b


class PsumPool:

    def __init__(self, nc):
        self.f32 = [(f"ps{i}", nc.alloc_psum_tensor(f"ps{i}", [128, 512], F32).ap())
                    for i in range(8)]
        self.bf16 = [(k, ap.bitcast(BF16)) for k, ap in self.f32[6:8]]
        self.i_f = 0
        self.i_b = 0

    def f(self):
        k, ap = self.f32[self.i_f % 6]
        self.i_f += 1
        return k, ap

    def fgroup(self, n):
        assert n <= 6
        return [self.f() for _ in range(n)]

    def r(self):
        k, ap = self.f32[6 + self.i_b % 2]
        self.i_b += 1
        return k, ap

    def b(self):
        k, ap = self.bf16[self.i_b % 2]
        self.i_b += 1
        return k, ap


class WStream:
    def __init__(self, mk, name, w_ap):
        self.mk = mk
        self.name = name
        self.w = w_ap
        self.K = w_ap.shape[0]
        self.F = w_ap.shape[1]

    def stream_cols(self, c0, c1):
        width = c1 - c0
        mk = self.mk
        ics_per_piece = max(1, mk.wring_cols // width)
        n_ic = self.K // 128
        pieces = {}
        emitted = {}

        def ensure(ic0):
            if ic0 in emitted:
                return
            emitted[ic0] = True
            ic1 = min(ic0 + ics_per_piece, n_ic)
            sl = mk.wring_slot % len(mk.wring)
            t = mk.wring[sl]
            mk.wring_slot += 1
            nic = ic1 - ic0
            src = self.w[ic0 * 128:ic1 * 128, c0:c1].rearrange(
                "(ic p) f -> p ic f", p=128)
            dst = t.ap(0, nic * width).rearrange("p (ic f) -> p ic f", f=width)
            mk.plan.emit(
                "gpsimd",
                (lambda e, dst=dst, src=src: e.dma_start(out=dst, in_=src)),
                reads=[], writes=t.keys(0, nic * width),
                dma=("wg", sl),
                tag=f"{self.name}[{ic0}:{ic1},{c0}:{c1}]")
            for ic in range(ic0, ic1):
                pieces[ic] = (t, ic - ic0, width)

        def acc(ic, cc0, cc1):
            ensure((ic // ics_per_piece) * ics_per_piece)
            t, icl, w = pieces[ic]
            a = t.ap(icl * w + cc0, icl * w + cc1)
            return a, t.keys(icl * w + cc0, icl * w + cc1)
        return acc


class MK:
    def __init__(self, part="all", taps=()):
        self.part = part
        self.taps = set(taps)
        nc = self.nc = bass.Bass("TRN2", target_bir_lowering=False)
        self.plan = Plan(nc)
        self.bf = Arena(nc, "abf", BF16, 74 * 1024 + 512)
        self.f32a = Arena(nc, "af32", F32, 14 * 1024)
        self.ps = PsumPool(nc)
        self.wring_cols = 7168
        self.wring = [self.bf.alloc(self.wring_cols, f"wring{i}") for i in range(3)]
        self.wring_slot = 0
        self.tapped = {}

        d = self.d = {}
        ein, eout, internal = {}, {}, {}

        def dram(name, shape, dtype, kind):
            d[name] = nc.dram_tensor(name, shape, dtype, kind=kind).ap()

        s1 = part in ("s1", "all")
        rest = part in ("rest", "all")
        if s1:
            dram("x0", [2048, DMODEL], F32, "ExternalInput")
            dram("w_l0qkv", [4096, 3072], F32, "ExternalInput")
            dram("b_l0qkv", [1, 3072], F32, "ExternalInput")
        if rest:
            dram("w_l0o", [1024, 4096], F32, "ExternalInput")
            dram("b_l0o", [1, 4096], F32, "ExternalInput")
            dram("w_l0base_t", [4096, 4096], F32, "ExternalInput")
            dram("b_l0base", [1, 4096], F32, "ExternalInput")
            dram("a_t", [4096, 64], F32, "ExternalInput")
            dram("b_flat", [1024, 4096], F32, "ExternalInput")
            dram("r_t", [4096, 16], F32, "ExternalInput")
            dram("w_l1qkv_t", [4096, 6144], F32, "ExternalInput")
            dram("w_l1proj_t", [4096, 4096], F32, "ExternalInput")
            dram("gate_w", [4096, 8], F32, "ExternalInput")
            dram("up_w", [4096, 8192], F32, "ExternalInput")
            dram("down_w", [8192, 4096], F32, "ExternalInput")
            dram("fc1_w", [4096, 2048], F32, "ExternalInput")
            dram("fc2_w", [2048, 4096], F32, "ExternalInput")
            dram("expert_w", [2048, 2048], F32, "ExternalInput")
            dram("ln_g", [1, 4096], F32, "ExternalInput")
            dram("ln_b", [1, 4096], F32, "ExternalInput")
            dram("out", [2048, DMODEL], F32, "ExternalOutput")
        if part == "s1":
            dram("kin", [4096, 512], BF16, "ExternalOutput")
            dram("vin", [2048, 1024], BF16, "ExternalOutput")
            dram("qdr", [2048, 1024], BF16, "ExternalOutput")
            dram("x0fm", [DMODEL, 2048], BF16, "ExternalOutput")
        elif part == "rest":
            dram("kout0", [16384, 512], BF16, "ExternalInput")
            dram("kout1", [16384, 512], BF16, "ExternalInput")
            dram("vout0", [8192, 1024], BF16, "ExternalInput")
            dram("vout1", [8192, 1024], BF16, "ExternalInput")
            dram("qdr", [2048, 1024], BF16, "ExternalInput")
            dram("x0fm", [DMODEL, 2048], BF16, "ExternalInput")
        else:
            dram("kin", [4096, 512], BF16, "Internal")
            dram("vin", [2048, 1024], BF16, "Internal")
            dram("kout0", [16384, 512], BF16, "Internal")
            dram("kout1", [16384, 512], BF16, "Internal")
            dram("vout0", [8192, 1024], BF16, "Internal")
            dram("vout1", [8192, 1024], BF16, "Internal")
            dram("qdr", [2048, 1024], BF16, "Internal")
            dram("x0fm", [DMODEL, 2048], BF16, "Internal")
        names = [("x1fm", [DMODEL, 2048]), ("x2fm", [DMODEL, 2048]),
                 ("x3fm", [DMODEL, 2048])]
        for p in range(2):
            names += [(f"prefm{p}", [4096, TP]), (f"whtm{p}", [1024, 1024]),
                      (f"hdr{p}", [8192, TP]), (f"l1q{p}", [6144, TP]),
                      (f"rtd{p}", [4096, TP])]
        for name, shape in names:
            if rest:
                dram(name, shape, BF16, "Internal")
        for t in self.taps:
            dram("tap_" + t, list(d[t].shape), d[t].dtype, "ExternalOutput")

        self._init_consts(s1, rest)

    def E(self, eng, fn, reads=(), writes=(), dma=None, tag=""):
        return self.plan.emit(eng, fn, reads, writes, dma, tag)

    def tap(self, name, keys):
        if name in self.taps:
            src, dst = self.d[name], self.d["tap_" + name]
            self.E("scalar", (lambda e, s=src, t=dst: e.dma_start(out=t, in_=s)),
                   reads=keys, writes=[("tap", name)], dma=("tap", None), tag=f"tap_{name}")

    def _init_consts(self, s1, rest):
        nc, d = self.nc, self.d
        self.id_bf = self.bf.alloc(512, "id_bf")
        self.id_f32 = self.f32a.alloc(512, "id_f32")
        for t, dt in ((self.id_bf, "idb"), (self.id_f32, "idf")):
            ap = t.ap(0, 128)
            self.E("gpsimd", (lambda e, a=ap: e.memset(a, 0.0)), writes=t.keys(0, 128), tag="id0")
            self.E("gpsimd",
                   (lambda e, a=ap: e.affine_select(
                       out=a, in_=a, compare_op=ALU.not_equal, fill=1.0,
                       base=0, pattern=[[-1, 128]], channel_multiplier=1)),
                   reads=t.keys(0, 128), writes=t.keys(0, 128), tag="id1")
        self.ones_row = self.bf.alloc(512, "ones_row")
        self.E("gpsimd", (lambda e, a=self.ones_row.ap(0, 128, 0, 1): e.memset(a, 1.0)),
               writes=self.ones_row.keys(0, 128), tag="ones_r")
        self.ones_col = self.bf.alloc(512, "ones_col")
        self.E("gpsimd", (lambda e, a=self.ones_col.ap(0, 1): e.memset(a, 1.0)),
               writes=self.ones_col.keys(0, 1), tag="ones_c")
        self.ones_col_f = self.f32a.alloc(512, "ones_col_f")
        self.E("gpsimd", (lambda e, a=self.ones_col_f.ap(0, 1): e.memset(a, 1.0)),
               writes=self.ones_col_f.keys(0, 1), tag="ones_cf")
        self.ones_row_f = self.f32a.alloc(512, "ones_row_f")
        self.E("gpsimd", (lambda e, a=self.ones_row_f.ap(0, 128, 0, 1): e.memset(a, 1.0)),
               writes=self.ones_row_f.keys(0, 128), tag="ones_rf")
        self.epsc = self.f32a.alloc(512, "epsc")
        self.E("gpsimd", (lambda e, a=self.epsc.ap(0, 1): e.memset(a, LN_EPS)),
               writes=self.epsc.keys(0, 1), tag="epsc")

        if s1:
            self.bqkv = self.f32a.alloc(512, "bqkv")
            src = d["b_l0qkv"].rearrange("o (c p) -> p (o c)", p=128)
            self.E("gpsimd", (lambda e, a=self.bqkv.ap(0, 24), s=src:
                              e.dma_start(out=a, in_=s, allow_slow_non_contiguous=True)),
                   writes=self.bqkv.keys(0, 24), dma=("cst", None), tag="ld_bqkv")
            self.bq_s = self.f32a.alloc(512, "bq_s")
            self.E("vector", (lambda e: e.tensor_scalar_mul(self.bq_s.ap(0, 8), self.bqkv.ap(0, 8), L0_SCALE)),
                   reads=self.bqkv.keys(0, 24), writes=self.bq_s.keys(0, 8), tag="bq_s")
            vrow = self.f32a.alloc(1024, "vrow")
            self.E("gpsimd", (lambda e, a=vrow.ap(0, 1024, 0, 1), s=d["b_l0qkv"][0:1, 2048:3072]:
                              e.dma_start(out=a, in_=s)),
                   writes=vrow.keys(), dma=("cst", None), tag="ld_vrow")
            vrow_bf = self.bf.alloc(1024, "vrow_bf")
            self.E("vector", (lambda e: e.tensor_copy(vrow_bf.ap(0, 1024, 0, 1), vrow.ap(0, 1024, 0, 1))),
                   reads=vrow.keys(), writes=vrow_bf.keys(), tag="vrow_bf")
            self.vbias = self.bf.alloc(1024, "vbias")
            for h in range(2):
                pk, pap = self.ps.f()
                self.E("tensor",
                       (lambda e, p=pap, r=vrow_bf.ap(h * 512, (h + 1) * 512, 0, 1):
                        e.matmul(p, self.ones_row.ap(0, 128, 0, 1), r, start=True, stop=True)),
                       reads=self.ones_row.keys(0, 128) + vrow_bf.keys(), writes=[pk], tag="vb_mm")
                self.E("scalar", (lambda e, p=pap, o=self.vbias.ap(h * 512, (h + 1) * 512): e.copy(o, p)),
                       reads=[pk], writes=self.vbias.keys(h * 512, (h + 1) * 512), tag="vb_ev")
            vrow.free()
            vrow_bf.free()
        if rest:
            self.bo_t = self.f32a.alloc(512, "bo_t")
            self.bb_t = self.f32a.alloc(512, "bb_t")
            self.gam_t = self.f32a.alloc(512, "gam_t")
            self.bet_t = self.f32a.alloc(512, "bet_t")
            for name, tile in (("b_l0o", self.bo_t), ("b_l0base", self.bb_t),
                               ("ln_g", self.gam_t), ("ln_b", self.bet_t)):
                src = d[name].rearrange("o (c p) -> p (o c)", p=128)
                self.E("gpsimd", (lambda e, a=tile.ap(0, 32), s=src:
                                  e.dma_start(out=a, in_=s, allow_slow_non_contiguous=True)),
                       writes=tile.keys(0, 32), dma=("cst", None), tag=f"ld_{name}")


    _st_ctr = 0

    def st(self):
        MK._st_ctr += 1
        return ("st", MK._st_ctr % 8)

    _wg_ctr = 0

    def wg(self):
        MK._wg_ctr += 1
        return ("wg", MK._wg_ctr % 6)

    def xtm_keys(self, name, p):
        return [(name, p, tt) for tt in range(8)]

    def s1(self, p, ws_qkv):
        d, E, ps = self.d, self.E, self.ps
        xin = [self.bf.alloc(4096, "xin") for _ in range(3)]
        xfm = [self.bf.alloc(1024, f"xfm{cc}") for cc in range(32)]
        for tt in range(8):
            b, q0 = tt // 4, (tt % 4) * 128
            t = xin[tt % 3]
            r0 = b * 1024 + p * 512 + q0
            src = d["x0"][r0:r0 + 128, :]
            E("gpsimd", (lambda e, a=t.ap(), s=src: e.dma_start(out=a, in_=s)),
              writes=t.keys(), dma=("xin", tt % 3), tag=f"xin{p}{tt}")
            for ccg in range(8):
                pkT, papT = ps.b()
                for j in range(4):
                    cc = ccg * 4 + j
                    E("tensor", (lambda e, o=papT[:, j * 128:(j + 1) * 128],
                                 i=t.ap(cc * 128, (cc + 1) * 128),
                                 idn=self.id_bf.ap(0, 128): e.transpose(o, i, idn)),
                      reads=t.keys(cc * 128, (cc + 1) * 128) + self.id_bf.keys(0, 128),
                      writes=[pkT], tag=f"x0t{tt}{ccg}")
                for j in range(4):
                    cc = ccg * 4 + j
                    dst = xfm[cc].ap(tt * 128, (tt + 1) * 128)
                    E("scalar", (lambda e, o=dst, i=papT[:, j * 128:(j + 1) * 128]: e.copy(o, i)),
                      reads=[pkT], writes=xfm[cc].keys(tt * 128, (tt + 1) * 128),
                      tag=f"x0te{tt}{ccg}")
        for cc in range(32):
            dst = d["x0fm"][cc * 128:(cc + 1) * 128, p * 1024:(p + 1) * 1024]
            E("scalar", (lambda e, a=xfm[cc].ap(), o=dst: e.dma_start(out=o, in_=a)),
              reads=xfm[cc].keys(), writes=[("x0fm", p, cc)], dma=self.st(), tag=f"x0st{cc}")
        qkst = [self.bf.alloc(512, "qkst") for _ in range(4)]
        ogs = [(i, min(i + 3, 16)) for i in range(0, 16, 3)]
        for (lo, hi) in ogs:
            accw = ws_qkv.stream_cols(lo * 128, hi * 128)
            banks = ps.fgroup((hi - lo) * 2)
            for cc in range(32):
                wap, wkeys = None, None
                for j, dc in enumerate(range(lo, hi)):
                    lap, lkeys = accw(cc, j * 128, (j + 1) * 128)
                    for th in range(2):
                        pk, pap = banks[j * 2 + th]
                        E("tensor",
                          (lambda e, o=pap, l=lap, r=xfm[cc].ap(th * 512, (th + 1) * 512),
                           s=(cc == 0), t=(cc == 31): e.matmul(o, l, r, start=s, stop=t)),
                          reads=lkeys + xfm[cc].keys(th * 512, (th + 1) * 512),
                          writes=[pk], tag=f"qk{dc}{th}")
            for j, dc in enumerate(range(lo, hi)):
                for th in range(2):
                    pk, pap = banks[j * 2 + th]
                    stg = qkst[(dc * 2 + th) % 4]
                    if dc < 8:
                        scale, bias_ap, bias_keys = L0_SCALE, self.bq_s.ap(dc, dc + 1), self.bq_s.keys(0, 8)
                    else:
                        scale, bias_ap, bias_keys = 1.0, self.bqkv.ap(dc, dc + 1), self.bqkv.keys(0, 24)
                    E("scalar",
                      (lambda e, o=stg.ap(0, 512), i=pap, b=bias_ap, s=scale:
                       e.activation(o, i, AF.Identity, bias=b, scale=s)),
                      reads=[pk] + bias_keys, writes=stg.keys(0, 512), tag=f"qkev{dc}{th}")
                    if dc < 8:
                        r0 = p * 1024 + dc * 128
                        dst = d["qdr"][r0:r0 + 128, th * 512:(th + 1) * 512]
                        wkey = ("qdr", p, dc, th)
                    else:
                        r0 = p * 2048 + th * 1024 + (dc - 8) * 128
                        dst = d["kin"][r0:r0 + 128, :]
                        wkey = ("kin", p, th, dc - 8)
                    E("scalar", (lambda e, a=stg.ap(0, 512), o=dst: e.dma_start(out=o, in_=a)),
                      reads=stg.keys(0, 512), writes=[wkey], dma=self.st(), tag=f"qkst{dc}{th}")
        for dvh in range(2):
            accw = ws_qkv.stream_cols(2048 + dvh * 512, 2048 + (dvh + 1) * 512)
            for tg in ((0, 6), (6, 8)):
                banks = ps.fgroup(tg[1] - tg[0])
                for cc in range(32):
                    rap, rkeys = accw(cc, 0, 512)
                    for j, tt in enumerate(range(*tg)):
                        pk, pap = banks[j]
                        E("tensor",
                          (lambda e, o=pap, l=xfm[cc].ap(tt * 128, (tt + 1) * 128), r=rap,
                           s=(cc == 0), t=(cc == 31): e.matmul(o, l, r, start=s, stop=t)),
                          reads=rkeys + xfm[cc].keys(tt * 128, (tt + 1) * 128),
                          writes=[pk], tag=f"v{tt}{dvh}")
                for j, tt in enumerate(range(*tg)):
                    pk, pap = banks[j]
                    stg = qkst[tt % 4]
                    E("vector",
                      (lambda e, o=stg.ap(0, 512), i=pap, b=self.vbias.ap(dvh * 512, (dvh + 1) * 512):
                       e.tensor_add(o, i, b)),
                      reads=[pk] + self.vbias.keys(), writes=stg.keys(0, 512), tag=f"vev{tt}")
                    r0 = p * 1024 + tt * 128
                    dst = d["vin"][r0:r0 + 128, dvh * 512:(dvh + 1) * 512]
                    E("gpsimd", (lambda e, a=stg.ap(0, 512), o=dst: e.dma_start(out=o, in_=a)),
                      reads=stg.keys(0, 512), writes=[("vin", p, tt, dvh)], dma=self.st(),
                      tag=f"vst{tt}")
        for t in xin + xfm + qkst:
            t.free()

    def allgather(self, p):
        d, E = self.d, self.E
        kin_keys = [("kin", p, b, dk) for b in range(2) for dk in range(8)]
        vin_keys = [("vin", p, tt, dvh) for tt in range(8) for dvh in range(2)]
        kin = d["kin"][p * 2048:(p + 1) * 2048, :]
        vin = d["vin"][p * 1024:(p + 1) * 1024, :]
        kout, vout = d[f"kout{p}"], d[f"vout{p}"]

        def fk(e):
            return e.collective_compute(
                "AllGather", ALU.bypass, replica_groups=[list(range(NCORES))],
                ins=[kin.opt()], outs=[kout.opt()])

        def fv(e):
            return e.collective_compute(
                "AllGather", ALU.bypass, replica_groups=[list(range(NCORES))],
                ins=[vin.opt()], outs=[vout.opt()])
        self.plan.emit("gpsimd", fk, reads=kin_keys, writes=[("kout", p)], tag=f"ag_k{p}")
        self.plan.emit("gpsimd", fv, reads=vin_keys, writes=[("vout", p)], tag=f"ag_v{p}")

    def flash(self, p):
        d, E, ps = self.d, self.E, self.ps
        qf = [self.bf.alloc(1024, f"qf{dc}") for dc in range(8)]
        for dc in range(8):
            r0 = p * 1024 + dc * 128
            E("gpsimd", (lambda e, a=qf[dc].ap(), s=d["qdr"][r0:r0 + 128, :]: e.dma_start(out=a, in_=s)),
              reads=[("qdr", p, dc, th) for th in range(2)], writes=qf[dc].keys(),
              dma=("qf", None), tag=f"qf{dc}")
        attn = [self.bf.alloc(1024, f"attn{dc}") for dc in range(8)]
        kvk = [self.bf.alloc(4096, "kvk") for _ in range(2)]
        kvv = [self.bf.alloc(4096, "kvv") for _ in range(2)]
        pbuf = [self.bf.alloc(512, "pbuf") for _ in range(4)]
        pt4 = [self.bf.alloc(512, "pt4") for _ in range(2)]
        acc = [self.f32a.alloc(1024, f"acc{qt}") for qt in range(4)]
        ml = self.f32a.alloc(512, "ml")
        mlk = ml.keys()

        def mlap(c):
            return ml.ap(c, c + 1)
        for b in range(2):
            for kc in range(16):
                pp, r = kc // 8, kc % 8
                sl = kc % 2
                r0 = r * 2048 + b * 1024
                src = d[f"kout{pp}"][r0:r0 + 1024, :].rearrange("(dc p) k -> p dc k", p=128)
                E("gpsimd", (lambda e, a=kvk[sl].ap(), s=src: e.dma_start(out=a, in_=s)),
                  reads=[("kout", pp)], writes=kvk[sl].keys(), dma=("kvk", sl), tag=f"kld{b}{kc}")
                r0 = r * 1024 + b * 512
                src = d[f"vout{pp}"][r0:r0 + 512, :].rearrange("(s p) v -> p s v", p=128)
                E("gpsimd", (lambda e, a=kvv[sl].ap(), s=src: e.dma_start(out=a, in_=s)),
                  reads=[("vout", pp)], writes=kvv[sl].keys(), dma=("kvv", sl), tag=f"vld{b}{kc}")
                sbanks = []
                for qt in range(4):
                    pkS, papS = ps.f()
                    sbanks.append((pkS, papS))
                    for dc in range(8):
                        E("tensor",
                          (lambda e, o=papS, l=qf[dc].ap(b * 512 + qt * 128, b * 512 + (qt + 1) * 128),
                           r_=kvk[sl].ap(dc * 512, (dc + 1) * 512), s=(dc == 0), t=(dc == 7):
                           e.matmul(o, l, r_, start=s, stop=t)),
                          reads=qf[dc].keys() + kvk[sl].keys(dc * 512, (dc + 1) * 512),
                          writes=[pkS], tag=f"sc{b}{kc}{qt}")
                for qt in range(4):
                    pkS, papS = sbanks[qt]
                    par = kc % 2
                    mq, mq_old = mlap(qt * 2 + par), mlap(qt * 2 + 1 - par)
                    lq, lq_old = mlap(8 + qt * 2 + par), mlap(8 + qt * 2 + 1 - par)
                    nm, rs, al = mlap(16 + qt), mlap(20 + qt), mlap(24 + qt)
                    P = pbuf[qt]
                    if kc == 0:
                        E("vector", (lambda e, o=mq, i=papS: e.reduce_max(o, i, axis=AX.X)),
                          reads=[pkS], writes=mlk, tag="fm0")
                        E("vector", (lambda e, o=nm, i=mq: e.tensor_scalar_mul(o, i, -1.0)),
                          reads=mlk, writes=mlk, tag="fm1")
                        E("scalar", (lambda e, o=P.ap(0, 512), i=papS, bi=nm:
                                     e.activation(o, i, AF.Exp, bias=bi)),
                          reads=[pkS] + mlk, writes=P.keys(), tag="fexp0")
                        E("vector", (lambda e, o=lq, i=P.ap(0, 512): e.reduce_sum(o, i, axis=AX.X)),
                          reads=P.keys(), writes=mlk, tag="fls0")
                    else:
                        cm = mlap(28 + qt)
                        E("vector", (lambda e, o=cm, i=papS: e.reduce_max(o, i, axis=AX.X)),
                          reads=[pkS], writes=mlk, tag="fcm")
                        E("vector", (lambda e, o=mq, a=mq_old, b_=cm: e.tensor_max(o, a, b_)),
                          reads=mlk, writes=mlk, tag="fmx")
                        E("vector", (lambda e, o=nm, i=mq: e.tensor_scalar_mul(o, i, -1.0)),
                          reads=mlk, writes=mlk, tag="fnm")
                        E("scalar", (lambda e, o=al, i=mq_old, bi=nm: e.activation(o, i, AF.Exp, bias=bi)),
                          reads=mlk, writes=mlk, tag="fal")
                        E("scalar", (lambda e, o=P.ap(0, 512), i=papS, bi=nm:
                                     e.activation(o, i, AF.Exp, bias=bi)),
                          reads=[pkS] + mlk, writes=P.keys(), tag="fexp")
                        E("vector", (lambda e, o=rs, i=P.ap(0, 512): e.reduce_sum(o, i, axis=AX.X)),
                          reads=P.keys(), writes=mlk, tag="frs")
                        E("vector", (lambda e, o=lq, i=lq_old, s=al, t=rs:
                                     e.scalar_tensor_tensor(o, i, s, t, ALU.mult, ALU.add)),
                          reads=mlk, writes=mlk, tag="fl")
                ptdone = {}

                def emit_pt(qt):
                    P = pbuf[qt]
                    pkT, papT = ps.b()
                    for ks in range(4):
                        E("tensor", (lambda e, o=papT[:, ks * 128:(ks + 1) * 128],
                                     i=P.ap(ks * 128, (ks + 1) * 128),
                                     idn=self.id_bf.ap(0, 128): e.transpose(o, i, idn)),
                          reads=P.keys() + self.id_bf.keys(0, 128), writes=[pkT], tag="fpt")
                    pt = pt4[qt % 2]
                    E("scalar", (lambda e, o=pt.ap(0, 512), i=papT[:, 0:512]: e.copy(o, i)),
                      reads=[pkT], writes=pt.keys(0, 512), tag="fptev")
                    ptdone[qt] = pt

                def emit_av(qt):
                    par = kc % 2
                    al = mlap(24 + qt)
                    pt = ptdone[qt]
                    for dvh in range(2):
                        pkA, papA = ps.f()
                        for ks in range(4):
                            E("tensor",
                              (lambda e, o=papA, l=pt.ap(ks * 128, (ks + 1) * 128),
                               r_=kvv[sl].ap(ks * 1024 + dvh * 512, ks * 1024 + (dvh + 1) * 512),
                               s=(ks == 0), t=(ks == 3): e.matmul(o, l, r_, start=s, stop=t)),
                              reads=pt.keys(ks * 128, (ks + 1) * 128) +
                              kvv[sl].keys(ks * 1024 + dvh * 512, ks * 1024 + (dvh + 1) * 512),
                              writes=[pkA], tag=f"av{b}{kc}{qt}{dvh}")
                        a_ap = acc[qt].ap(dvh * 512, (dvh + 1) * 512)
                        a_keys = acc[qt].keys(dvh * 512, (dvh + 1) * 512)
                        if kc == 0:
                            E("vector", (lambda e, o=a_ap, i=papA: e.tensor_copy(o, i)),
                              reads=[pkA], writes=a_keys, tag="facc0")
                        else:
                            E("vector", (lambda e, o=a_ap, i=papA, s=al:
                                         e.scalar_tensor_tensor(o, o, s, i, ALU.mult, ALU.add)),
                              reads=[pkA] + a_keys + mlk, writes=a_keys, tag="facc")
                emit_pt(0)
                emit_pt(1)
                emit_av(0)
                emit_pt(2)
                emit_av(1)
                emit_pt(3)
                emit_av(2)
                emit_av(3)
            for qt in range(4):
                lq = mlap(8 + qt * 2 + 1)
                rl = mlap(28 + qt)
                E("vector", (lambda e, o=rl, i=lq: e.reciprocal(o, i)), reads=mlk, writes=mlk, tag="frl")
                ao2 = self.bf.alloc(1024, "ao2")
                E("vector", (lambda e, o=ao2.ap(), i=acc[qt].ap(), s=rl:
                             e.tensor_scalar_mul(o, i, s)),
                  reads=acc[qt].keys() + mlk, writes=ao2.keys(), tag="fnorm")
                for half in range(2):
                    pkT, papT = ps.b()
                    for dc2 in range(4):
                        dc = half * 4 + dc2
                        E("tensor", (lambda e, o=papT[:, dc2 * 128:(dc2 + 1) * 128],
                                     i=ao2.ap(dc * 128, (dc + 1) * 128),
                                     idn=self.id_bf.ap(0, 128): e.transpose(o, i, idn)),
                          reads=ao2.keys() + self.id_bf.keys(0, 128), writes=[pkT], tag="faot")
                    for dc2 in range(4):
                        dc = half * 4 + dc2
                        dst = attn[dc].ap(b * 512 + qt * 128, b * 512 + (qt + 1) * 128)
                        E("scalar", (lambda e, o=dst, i=papT[:, dc2 * 128:(dc2 + 1) * 128]: e.copy(o, i)),
                          reads=[pkT], writes=attn[dc].keys(b * 512 + qt * 128, b * 512 + (qt + 1) * 128),
                          tag="faotev")
                ao2.free()
        for t in qf + kvk + kvv + pbuf + pt4 + acc + [ml]:
            t.free()
        return attn

    def _ln_stats(self, t, stat, sk, rstd_col, negmr_col):
        E = self.E
        for s8 in range(8):
            E("vector", (lambda e, o=stat.ap(s8 * 6, s8 * 6 + 6), i=t.ap(s8 * 512, (s8 + 1) * 512):
                         e.bn_stats(o, i)),
              reads=t.keys(s8 * 512, (s8 + 1) * 512), writes=sk, tag="bns")
        E("vector", (lambda e, o=stat.ap(48, 50), i=stat.ap(0, 48): e.bn_aggr(o, i)),
          reads=sk, writes=sk, tag="bna")
        E("scalar", (lambda e, o=stat.ap(rstd_col, rstd_col + 1), i=stat.ap(49, 50),
                     b=self.epsc.ap(0, 1): e.activation(o, i, AF.Ln, bias=b)),
          reads=sk + self.epsc.keys(0, 1), writes=sk, tag="lnv")
        E("scalar", (lambda e, o=stat.ap(rstd_col, rstd_col + 1), i=stat.ap(rstd_col, rstd_col + 1):
                     e.activation(o, i, AF.Exp, scale=-0.5)),
          reads=sk, writes=sk, tag="rstd")
        E("vector", (lambda e, o=stat.ap(negmr_col, negmr_col + 1), i=stat.ap(48, 49),
                     r=stat.ap(rstd_col, rstd_col + 1):
                     e.tensor_scalar(o, i, r, -1.0, ALU.mult, ALU.mult)),
          reads=sk, writes=sk, tag="negmr")

    def ln_begin(self, tag):
        st = {}
        st["srow"] = self.f32a.alloc(2048, f"lns_{tag}")
        st["sq"] = [self.bf.alloc(512, "lnsq") for _ in range(2)]
        st["rbf"] = self.f32a.alloc(1024, f"lnrb_{tag}")
        st["nbf"] = self.f32a.alloc(1024, f"lnnb_{tag}")
        st["rowbf"] = self.bf.alloc(512, "lnrow")
        self.E("vector", (lambda e, a=st["srow"].ap(0, 2048, 0, 1): e.memset(a, 0.0)),
               writes=st["srow"].keys(), tag=f"lnz_{tag}")
        return st

    def ln_chunk(self, st, pr_ap, pr_keys, th):
        E, ps = self.E, self.ps
        sqt = st["sq"][th]
        E("vector", (lambda e, o=sqt.ap(0, 512), a=pr_ap: e.tensor_mul(o, a, a)),
          reads=pr_keys, writes=sqt.keys(), tag="lnsq")
        for rhs_ap, rhs_keys, col0 in ((pr_ap, pr_keys, th * 512),
                                       (sqt.ap(0, 512), sqt.keys(), 1024 + th * 512)):
            pk, pap = ps.r()
            E("tensor", (lambda e, o=pap[0:1, :], l=self.ones_col.ap(0, 1), r=rhs_ap:
                         e.matmul(o, l, r, start=True, stop=True)),
              reads=self.ones_col.keys(0, 1) + rhs_keys, writes=[pk], tag="lnmm")
            E("vector", (lambda e, o=st["srow"].ap(col0, col0 + 512, 0, 1),
                         a=st["srow"].ap(col0, col0 + 512, 0, 1), b=pap[0:1, :]:
                         e.tensor_add(o, a, b)),
              reads=[pk] + st["srow"].keys(), writes=st["srow"].keys(), tag="lnacc")

    def ln_finalize(self, st):
        E, ps = self.E, self.ps
        srow, rowbf = st["srow"], st["rowbf"]
        sk = srow.keys()
        E("vector", (lambda e, o=srow.ap(0, 1024, 0, 1), i=srow.ap(0, 1024, 0, 1):
                     e.tensor_scalar_mul(o, i, 1.0 / 4096.0)),
          reads=sk, writes=sk, tag="lnmean")
        E("vector", (lambda e, o=srow.ap(1024, 2048, 0, 1), i=srow.ap(1024, 2048, 0, 1):
                     e.tensor_scalar_mul(o, i, 1.0 / 4096.0)),
          reads=sk, writes=sk, tag="lne2")
        msq = self.f32a.alloc(1024, "lnmsq")
        E("vector", (lambda e, o=msq.ap(0, 1024, 0, 1), a=srow.ap(0, 1024, 0, 1),
                     b=srow.ap(0, 1024, 0, 1): e.tensor_mul(o, a, b)),
          reads=sk, writes=msq.keys(), tag="lnm2")
        E("vector", (lambda e, o=srow.ap(1024, 2048, 0, 1), a=srow.ap(1024, 2048, 0, 1),
                     b=msq.ap(0, 1024, 0, 1): e.tensor_sub(o, a, b)),
          reads=sk + msq.keys(), writes=sk, tag="lnvar")
        msq.free()
        E("scalar", (lambda e, o=srow.ap(1024, 2048, 0, 1), i=srow.ap(1024, 2048, 0, 1),
                     b=self.epsc.ap(0, 1, 0, 1): e.activation(o, i, AF.Ln, bias=b)),
          reads=sk + self.epsc.keys(0, 1), writes=sk, tag="lnln")
        E("scalar", (lambda e, o=srow.ap(1024, 2048, 0, 1), i=srow.ap(1024, 2048, 0, 1):
                     e.activation(o, i, AF.Exp, scale=-0.5)),
          reads=sk, writes=sk, tag="lnexp")
        E("vector", (lambda e, o=srow.ap(0, 1024, 0, 1), a=srow.ap(0, 1024, 0, 1),
                     b=srow.ap(1024, 2048, 0, 1): e.tensor_mul(o, a, b)),
          reads=sk, writes=sk, tag="lnnm1")
        E("vector", (lambda e, o=srow.ap(0, 1024, 0, 1), i=srow.ap(0, 1024, 0, 1):
                     e.tensor_scalar_mul(o, i, -1.0)),
          reads=sk, writes=sk, tag="lnnm2")
        for srccol, dsttile in ((1024, st["rbf"]), (0, st["nbf"])):
            for th in range(2):
                pk, pap = ps.f()
                E("tensor", (lambda e, o=pap, l=self.ones_row_f.ap(0, 128, 0, 1),
                             r=srow.ap(srccol + th * 512, srccol + (th + 1) * 512, 0, 1):
                             e.matmul(o, l, r, start=True, stop=True)),
                  reads=self.ones_row_f.keys(0, 128) + sk, writes=[pk], tag="lnbc")
                E("scalar", (lambda e, o=dsttile.ap(th * 512, (th + 1) * 512), i=pap: e.copy(o, i)),
                  reads=[pk], writes=dsttile.keys(th * 512, (th + 1) * 512), tag="lnbce")
        st["srow"].free()
        st["rowbf"].free()
        for t in st["sq"]:
            t.free()

    def ln_apply(self, st, p, src_name, dst_name):
        d, E = self.d, self.E
        pre_keys = [("prefm", p, oc, th) for oc in range(32) for th in range(2)]
        xfm = [self.bf.alloc(1024, f"xf{cc}") for cc in range(32)]
        xin = [self.bf.alloc(1024, "lnxin") for _ in range(3)]
        zf = [self.f32a.alloc(1024, "lnzf") for _ in range(2)]
        for cc in range(32):
            t = xin[cc % 3]
            E("sync", (lambda e, a=t.ap(), s=d[src_name][cc * 128:(cc + 1) * 128, :]:
                       e.dma_start(out=a, in_=s)),
              reads=pre_keys, writes=t.keys(), dma=("lnld", cc % 3), tag=f"lnld{cc}")
            z = zf[cc % 2]
            E("vector", (lambda e, o=z.ap(), a=t.ap(), b=st["rbf"].ap(0, 1024):
                         e.tensor_mul(o, a, b)),
              reads=t.keys() + st["rbf"].keys(), writes=z.keys(), tag=f"lnap1_{cc}")
            o2 = xfm[cc]
            E("vector", (lambda e, o=o2.ap(), a=z.ap(), b=st["nbf"].ap(0, 1024):
                         e.tensor_add(o, a, b)),
              reads=z.keys() + st["nbf"].keys(), writes=o2.keys(), tag=f"lnap2_{cc}")
            dst = d[dst_name][cc * 128:(cc + 1) * 128, p * 1024:(p + 1) * 1024]
            E("gpsimd", (lambda e, a=o2.ap(), o=dst: e.dma_start(out=o, in_=a)),
              reads=o2.keys(), writes=[(dst_name, p, cc)], dma=self.st(), tag=f"lnst{cc}")
        for t in xin + zf:
            t.free()
        st["rbf"].free()
        st["nbf"].free()
        return xfm

    def proj_to_prefm(self, p, ws, attn_tiles, bias_tile, bias_keys, res_name, lnst):
        d, E, ps = self.d, self.E, self.ps
        nic = len(attn_tiles)
        t1r = [self.f32a.alloc(512, "t1r") for _ in range(2)]
        resr = [self.bf.alloc(512, "resr") for _ in range(4)]
        prest = [self.bf.alloc(512, "prest") for _ in range(4)]
        ogs = [(i, min(i + 3, 32)) for i in range(0, 32, 3)]
        for gi, (lo, hi) in enumerate(ogs):
            accw = ws.stream_cols(lo * 128, hi * 128)
            banks = ps.fgroup((hi - lo) * 2)
            for ic in range(nic):
                for j, oc in enumerate(range(lo, hi)):
                    lap, lkeys = accw(ic, j * 128, (j + 1) * 128)
                    for th in range(2):
                        pk, pap = banks[j * 2 + th]
                        E("tensor",
                          (lambda e, o=pap, l=lap, r=attn_tiles[ic].ap(th * 512, (th + 1) * 512),
                           s=(ic == 0), t=(ic == nic - 1): e.matmul(o, l, r, start=s, stop=t)),
                          reads=lkeys + attn_tiles[ic].keys(th * 512, (th + 1) * 512),
                          writes=[pk], tag=f"pj{oc}{th}")
            for j, oc in enumerate(range(lo, hi)):
                for th in range(2):
                    pk, pap = banks[j * 2 + th]
                    res = resr[(oc * 2 + th) % 4]
                    src = d[res_name][oc * 128:(oc + 1) * 128,
                                      p * 1024 + th * 512:p * 1024 + (th + 1) * 512]
                    E("gpsimd", (lambda e, a=res.ap(0, 512), s=src: e.dma_start(out=a, in_=s)),
                      reads=[(res_name, p, oc)], writes=res.keys(0, 512),
                      dma=("res", (oc * 2 + th) % 4), tag=f"res{oc}{th}")
                    pr = prest[(oc * 2 + th) % 4]
                    if bias_tile is not None:
                        t1 = t1r[th]
                        E("scalar", (lambda e, o=t1.ap(0, 512), i=pap, b=bias_tile.ap(oc, oc + 1):
                                     e.activation(o, i, AF.Identity, bias=b)),
                          reads=[pk] + bias_keys, writes=t1.keys(0, 512), tag=f"pjb{oc}{th}")
                        E("vector", (lambda e, o=pr.ap(0, 512), i=t1.ap(0, 512), r=res.ap(0, 512):
                                     e.tensor_add(o, i, r)),
                          reads=t1.keys(0, 512) + res.keys(0, 512), writes=pr.keys(0, 512),
                          tag=f"pjr{oc}{th}")
                    else:
                        E("vector", (lambda e, o=pr.ap(0, 512), i=pap, r=res.ap(0, 512):
                                     e.tensor_add(o, i, r)),
                          reads=[pk] + res.keys(0, 512), writes=pr.keys(0, 512), tag=f"pjr{oc}{th}")
                    self.ln_chunk(lnst, pr.ap(0, 512), pr.keys(0, 512), th)
                    dst = d[f"prefm{p}"][oc * 128:(oc + 1) * 128, th * 512:(th + 1) * 512]
                    E("gpsimd", (lambda e, a=pr.ap(0, 512), o=dst: e.dma_start(out=o, in_=a)),
                      reads=pr.keys(0, 512), writes=[("prefm", p, oc, th)], dma=self.st(),
                      tag=f"pjst{oc}{th}")
        for t in t1r + resr + prest:
            t.free()

    def l0_attn(self, p, ws_o):
        attn = self.flash(p)
        lnst = self.ln_begin(f"l0a{p}")
        self.proj_to_prefm(p, ws_o, attn, self.bo_t, self.bo_t.keys(0, 32), "x0fm", lnst)
        for t in attn:
            t.free()
        self.ln_finalize(lnst)
        return lnst

    def load_fm(self, name, p, fam):
        d, E = self.d, self.E
        xfm = [self.bf.alloc(1024, f"{name}fm{cc}") for cc in range(32)]
        for cc in range(32):
            src = d[name][cc * 128:(cc + 1) * 128, p * 1024:(p + 1) * 1024]
            E("gpsimd", (lambda e, a=xfm[cc].ap(), s=src: e.dma_start(out=a, in_=s)),
              reads=[(name, p, cc)], writes=xfm[cc].keys(), dma=(fam, None),
              tag=f"{name}fm{cc}")
        return xfm

    def l0_ffn(self, p, ws_base, ws_bflat, rt_tile, at_tile, lnst_in):
        d, E, ps = self.d, self.E, self.ps
        xfm = self.ln_apply(lnst_in, p, f"prefm{p}", "x1fm")
        lf = self.f32a.alloc(1024, "lf")
        for th in range(2):
            pk, pap = ps.f()
            for cc in range(32):
                E("tensor",
                  (lambda e, o=pap[0:16, :], l=rt_tile.ap(cc * 16, (cc + 1) * 16),
                   r=xfm[cc].ap(th * 512, (th + 1) * 512), s=(cc == 0), t=(cc == 31):
                   e.matmul(o, l, r, start=s, stop=t)),
                  reads=rt_tile.keys() + xfm[cc].keys(th * 512, (th + 1) * 512),
                  writes=[pk], tag="rl")
            E("scalar", (lambda e, o=lf.ap(th * 512, (th + 1) * 512, 0, 16), i=pap[0:16, :]:
                         e.copy(o, i)),
              reads=[pk], writes=lf.keys(th * 512, (th + 1) * 512), tag="rlev")
        combT = self.f32a.alloc(512, "comb")
        for tt in range(8):
            pk, pap = ps.f()
            E("tensor", (lambda e, o=pap[:, 0:16], i=lf.ap(tt * 128, (tt + 1) * 128, 0, 16),
                         idn=self.id_f32.ap(0, 16, 0, 16): e.transpose(o, i, idn)),
              reads=lf.keys() + self.id_f32.keys(0, 128), writes=[pk], tag="rltr")
            class _C:
                def __init__(self, t, base):
                    self.t, self.base = t, base

                def ap(self, a, b):
                    return self.t.ap(self.base + a, self.base + b)
            c = _C(combT, tt * 64)
            ck = combT.keys()
            E("scalar", (lambda e, o=c.ap(0, 16), i=pap[:, 0:16]: e.copy(o, i)),
              reads=[pk], writes=ck, tag="rltm")
            E("vector", (lambda e, o=c.ap(48, 49), i=c.ap(0, 16): e.reduce_max(o, i, axis=AX.X)),
              reads=ck, writes=ck, tag="rm1")
            E("vector", (lambda e, o=c.ap(53, 54), i=c.ap(48, 49): e.tensor_scalar_mul(o, i, -1.0)),
              reads=ck, writes=ck, tag="rnm")
            E("scalar", (lambda e, o=c.ap(16, 32), i=c.ap(0, 16), b=c.ap(53, 54):
                         e.activation(o, i, AF.Exp, bias=b)),
              reads=ck, writes=ck, tag="rexp")
            E("vector", (lambda e, o=c.ap(49, 50), i=c.ap(16, 32): e.reduce_sum(o, i, axis=AX.X)),
              reads=ck, writes=ck, tag="rz")
            E("vector", (lambda e, o=c.ap(32, 48), i=c.ap(0, 16), s=c.ap(48, 49):
                         e.tensor_scalar(o, i, s, None, ALU.is_equal)),
              reads=ck, writes=ck, tag="req")
            E("vector", (lambda e, o=c.ap(32, 48), i=c.ap(32, 48), t2=c.ap(0, 16):
                         e.scalar_tensor_tensor(o, i, -1e30, t2, ALU.mult, ALU.add)),
              reads=ck, writes=ck, tag="rmsk")
            E("vector", (lambda e, o=c.ap(50, 51), i=c.ap(32, 48): e.reduce_max(o, i, axis=AX.X)),
              reads=ck, writes=ck, tag="rm2")
            E("vector", (lambda e, o=c.ap(32, 48), i=c.ap(0, 16), s=c.ap(50, 51):
                         e.tensor_scalar(o, i, s, None, ALU.is_ge)),
              reads=ck, writes=ck, tag="rind")
            E("vector", (lambda e, o=c.ap(16, 32), i=c.ap(16, 32), t2=c.ap(32, 48):
                         e.tensor_mul(o, i, t2)),
              reads=ck, writes=ck, tag="rcu")
            E("vector", (lambda e, o=c.ap(51, 52), i=c.ap(16, 32): e.reduce_sum(o, i, axis=AX.X)),
              reads=ck, writes=ck, tag="rs")
            E("vector", (lambda e, o=c.ap(52, 53), i=c.ap(49, 50), t2=c.ap(51, 52):
                         e.scalar_tensor_tensor(o, i, 1e-6, t2, ALU.mult, ALU.add)),
              reads=ck, writes=ck, tag="rden")
            E("vector", (lambda e, o=c.ap(52, 53), i=c.ap(52, 53): e.reciprocal(o, i)),
              reads=ck, writes=ck, tag="rrcp")
            E("vector", (lambda e, o=c.ap(32, 48), i=c.ap(16, 32), s=c.ap(52, 53):
                         e.tensor_scalar(o, i, s, 0.5, ALU.mult, ALU.mult)),
              reads=ck, writes=ck, tag="rcomb")
        lf.free()
        rhfm = self.bf.alloc(1024, "rhfm")
        for th in range(2):
            pk, pap = ps.f()
            for cc in range(32):
                E("tensor",
                  (lambda e, o=pap[0:64, :], l=at_tile.ap(cc * 64, (cc + 1) * 64),
                   r=xfm[cc].ap(th * 512, (th + 1) * 512), s=(cc == 0), t=(cc == 31):
                   e.matmul(o, l, r, start=s, stop=t)),
                  reads=at_tile.keys() + xfm[cc].keys(th * 512, (th + 1) * 512),
                  writes=[pk], tag="rh")
            E("scalar", (lambda e, o=rhfm.ap(th * 512, (th + 1) * 512, 0, 64), i=pap[0:64, :]:
                         e.copy(o, i)),
              reads=[pk], writes=rhfm.keys(th * 512, (th + 1) * 512), tag="rhev")
        whst = [self.bf.alloc(1024, "whst") for _ in range(2)]
        rhtms = [self.bf.alloc(512, "rhtm") for _ in range(8)]
        for tt in range(8):
            pk, pap = ps.b()
            rhtm = rhtms[tt]
            E("tensor", (lambda e, o=pap[:, 0:64], i=rhfm.ap(tt * 128, (tt + 1) * 128, 0, 64),
                         idn=self.id_bf.ap(0, 64, 0, 64): e.transpose(o, i, idn)),
              reads=rhfm.keys() + self.id_bf.keys(0, 128), writes=[pk], tag="rhtr")
            E("scalar", (lambda e, o=rhtm.ap(0, 64), i=pap[:, 0:64]: e.copy(o, i)),
              reads=[pk], writes=rhtm.keys(0, 64), tag="rhtm")
            w = whst[tt % 2]
            for ei in range(16):
                E("vector", (lambda e, o=w.ap(ei * 64, (ei + 1) * 64), i=rhtm.ap(0, 64),
                             s=combT.ap(tt * 64 + 32 + ei, tt * 64 + 33 + ei):
                             e.tensor_scalar(o, i, s, None, ALU.mult)),
                  reads=combT.keys() + rhtm.keys(0, 64), writes=w.keys(ei * 64, (ei + 1) * 64),
                  tag=f"wh{tt}{ei}")
            E("sync", (lambda e, a=w.ap(), o=d[f"whtm{p}"][tt * 128:(tt + 1) * 128, :]:
                       e.dma_start(out=o, in_=a)),
              reads=w.keys(), writes=[("whtm", p, tt)], dma=self.st(), tag=f"whst{tt}")
        rhfm.free()
        for t in rhtms:
            t.free()
        for t in whst:
            t.free()
        whfm = [self.bf.alloc(1024, f"whfm{ec}") for ec in range(8)]
        for ec in range(8):
            src = d[f"whtm{p}"][:, ec * 128:(ec + 1) * 128]
            E("sync", (lambda e, a=whfm[ec].ap(), s=src: e.dma_start_transpose(out=a, in_=s)),
              reads=[("whtm", p, tt) for tt in range(8)], writes=whfm[ec].keys(),
              dma=("whfm", None), tag=f"whfm{ec}")
        lnst = self.ln_begin(f"l0f{p}")
        prest = [self.bf.alloc(512, "prest") for _ in range(4)]
        t1r = [self.f32a.alloc(512, "t1r") for _ in range(2)]
        ogs = [(i, min(i + 3, 32)) for i in range(0, 32, 3)]
        for (lo, hi) in ogs:
            accb = ws_base.stream_cols(lo * 128, hi * 128)
            accd = ws_bflat.stream_cols(lo * 128, hi * 128)
            banks = ps.fgroup((hi - lo) * 2)
            for ic in range(32):
                for j, oc in enumerate(range(lo, hi)):
                    lap, lkeys = accb(ic, j * 128, (j + 1) * 128)
                    for th in range(2):
                        pk, pap = banks[j * 2 + th]
                        E("tensor",
                          (lambda e, o=pap, l=lap, r=xfm[ic].ap(th * 512, (th + 1) * 512),
                           s=(ic == 0): e.matmul(o, l, r, start=s, stop=False)),
                          reads=lkeys + xfm[ic].keys(th * 512, (th + 1) * 512),
                          writes=[pk], tag=f"bs{oc}{th}")
            for ic in range(8):
                for j, oc in enumerate(range(lo, hi)):
                    lap, lkeys = accd(ic, j * 128, (j + 1) * 128)
                    for th in range(2):
                        pk, pap = banks[j * 2 + th]
                        E("tensor",
                          (lambda e, o=pap, l=lap, r=whfm[ic].ap(th * 512, (th + 1) * 512),
                           t=(ic == 7): e.matmul(o, l, r, start=False, stop=t)),
                          reads=lkeys + whfm[ic].keys(th * 512, (th + 1) * 512),
                          writes=[pk], tag=f"dl{oc}{th}")
            for j, oc in enumerate(range(lo, hi)):
                for th in range(2):
                    pk, pap = banks[j * 2 + th]
                    t1 = t1r[th]
                    E("scalar", (lambda e, o=t1.ap(0, 512), i=pap, b=self.bb_t.ap(oc, oc + 1):
                                 e.activation(o, i, AF.Identity, bias=b)),
                      reads=[pk] + self.bb_t.keys(0, 32), writes=t1.keys(0, 512), tag=f"bb{oc}{th}")
                    pr = prest[(oc * 2 + th) % 4]
                    E("vector", (lambda e, o=pr.ap(0, 512), i=t1.ap(0, 512),
                                 r=xfm[oc].ap(th * 512, (th + 1) * 512):
                                 e.tensor_add(o, i, r)),
                      reads=t1.keys(0, 512) + xfm[oc].keys(th * 512, (th + 1) * 512),
                      writes=pr.keys(0, 512), tag=f"br{oc}{th}")
                    self.ln_chunk(lnst, pr.ap(0, 512), pr.keys(0, 512), th)
                    dst = d[f"prefm{p}"][oc * 128:(oc + 1) * 128, th * 512:(th + 1) * 512]
                    E("sync", (lambda e, a=pr.ap(0, 512), o=dst: e.dma_start(out=o, in_=a)),
                      reads=pr.keys(0, 512), writes=[("prefm", p, oc, th)], dma=self.st(),
                      tag=f"bst{oc}{th}")
        for t in xfm + whfm + [combT] + prest + t1r:
            t.free()
        self.ln_finalize(lnst)
        return lnst

    def l1_attn(self, p, ws_qkv, ws_proj, lnst_in):
        d, E, ps = self.d, self.E, self.ps
        xfm = self.ln_apply(lnst_in, p, f"prefm{p}", "x2fm")
        qst = [self.bf.alloc(512, "qst") for _ in range(4)]
        ogs = [(i, min(i + 3, 48)) for i in range(0, 48, 3)]
        for (lo, hi) in ogs:
            accw = ws_qkv.stream_cols(lo * 128, hi * 128)
            banks = ps.fgroup((hi - lo) * 2)
            for hc in range(32):
                for j, pc in enumerate(range(lo, hi)):
                    lap, lkeys = accw(hc, j * 128, (j + 1) * 128)
                    for th in range(2):
                        pk, pap = banks[j * 2 + th]
                        E("tensor",
                          (lambda e, o=pap, l=lap, r=xfm[hc].ap(th * 512, (th + 1) * 512),
                           s=(hc == 0), t=(hc == 31): e.matmul(o, l, r, start=s, stop=t)),
                          reads=lkeys + xfm[hc].keys(th * 512, (th + 1) * 512),
                          writes=[pk], tag=f"qkv1_{pc}{th}")
            for j, pc in enumerate(range(lo, hi)):
                for th in range(2):
                    pk, pap = banks[j * 2 + th]
                    stg = qst[(pc * 2 + th) % 4]
                    E("scalar", (lambda e, o=stg.ap(0, 512), i=pap: e.copy(o, i)),
                      reads=[pk], writes=stg.keys(0, 512), tag=f"qkv1e{pc}{th}")
                    dst = d[f"l1q{p}"][pc * 128:(pc + 1) * 128, th * 512:(th + 1) * 512]
                    E("sync", (lambda e, a=stg.ap(0, 512), o=dst: e.dma_start(out=o, in_=a)),
                      reads=stg.keys(0, 512), writes=[("l1q", p, pc, th)], dma=self.st(),
                      tag=f"qkv1s{pc}{th}")
        for t in xfm + qst:
            t.free()
        attn32 = [self.bf.alloc(1024, f"a32_{h}") for h in range(32)]
        qg = [[self.bf.alloc(1024, "qg") for _ in range(6)]]
        scr = self.bf.alloc(2048, "l1scr")
        scrf = self.f32a.alloc(2048, "l1scrf")
        srow = self.f32a.alloc(1024, "srow")
        wrow = self.bf.alloc(512, "wrow")
        for g in range(8):
            qs = qg[0]
            for pc6 in range(6):
                pc = g * 6 + pc6
                src_ = d[f"l1q{p}"][pc * 128:(pc + 1) * 128, :]
                E("sync", (lambda e, a=qs[pc6].ap(), s=src_: e.dma_start(out=a, in_=s)),
                  reads=[("l1q", p, pc, th) for th in range(2)], writes=qs[pc6].keys(),
                  dma=("l1ld", None), tag=f"l1ld{pc}")
            k, v = qs[4], qs[5]
            E("vector", (lambda e, o=scrf.ap(0, 512), a=k.ap(512, 1024), b=k.ap(0, 512):
                         e.tensor_sub(o, a, b)),
              reads=k.keys(), writes=scrf.keys(0, 512), tag=f"kd{g}")
            E("vector", (lambda e, o=scr.ap(512, 1024), a=v.ap(512, 1024), b=v.ap(0, 512):
                         e.tensor_sub(o, a, b)),
              reads=v.keys(), writes=scr.keys(512, 1024), tag=f"vd{g}")
            for j in range(4):
                h = g * 4 + j
                at = attn32[h]
                E("scalar", (lambda e, o=at.ap(0, 512), i=v.ap(0, 512): e.copy(o, i)),
                  reads=v.keys(), writes=at.keys(0, 512), tag=f"o0_{h}")
                E("vector", (lambda e, o=scrf.ap(512, 1024), a=qs[j].ap(512, 1024),
                             b=scrf.ap(0, 512): e.tensor_mul(o, a, b)),
                  reads=qs[j].keys() + scrf.keys(0, 512), writes=scrf.keys(512, 1024),
                  tag=f"pr{h}")
                pkD, papD = ps.f()
                E("tensor", (lambda e, o=papD[0:1, :], l=self.ones_col_f.ap(0, 1),
                             r=scrf.ap(512, 1024): e.matmul(o, l, r, start=True, stop=True)),
                  reads=self.ones_col_f.keys(0, 1) + scrf.keys(512, 1024), writes=[pkD],
                  tag=f"dot{h}")
                E("scalar", (lambda e, o=srow.ap(0, 512, 0, 1), i=papD[0:1, :]:
                             e.activation(o, i, AF.Exp, scale=-L1_SCALE)),
                  reads=[pkD], writes=srow.keys(0, 512), tag=f"se{h}")
                E("vector", (lambda e, o=srow.ap(512, 1024, 0, 1), i=srow.ap(0, 512, 0, 1):
                             e.tensor_scalar_add(o, i, 1.0)),
                  reads=srow.keys(0, 512), writes=srow.keys(512, 1024), tag=f"sd{h}")
                E("vector", (lambda e, o=srow.ap(512, 1024, 0, 1), i=srow.ap(512, 1024, 0, 1):
                             e.reciprocal(o, i)),
                  reads=srow.keys(512, 1024), writes=srow.keys(512, 1024), tag=f"sr{h}")
                E("vector", (lambda e, o=wrow.ap(0, 512, 0, 1), i=srow.ap(512, 1024, 0, 1):
                             e.tensor_copy(o, i)),
                  reads=srow.keys(512, 1024), writes=wrow.keys(), tag=f"swb{h}")
                pkB, papB = ps.f()
                E("tensor", (lambda e, o=papB, l=self.ones_row.ap(0, 128, 0, 1),
                             r=wrow.ap(0, 512, 0, 1): e.matmul(o, l, r, start=True, stop=True)),
                  reads=self.ones_row.keys(0, 128) + wrow.keys(), writes=[pkB], tag=f"wb{h}")
                E("vector", (lambda e, o=scrf.ap(1024, 1536), a=scr.ap(512, 1024), b=papB:
                             e.tensor_mul(o, a, b)),
                  reads=scr.keys(512, 1024) + [pkB], writes=scrf.keys(1024, 1536), tag=f"wm{h}")
                E("vector", (lambda e, o=at.ap(512, 1024), a=scrf.ap(1024, 1536), b=v.ap(0, 512):
                             e.tensor_add(o, a, b)),
                  reads=scrf.keys(1024, 1536) + v.keys(), writes=at.keys(512, 1024), tag=f"o1_{h}")
        for s in qg:
            for t in s:
                t.free()
        scr.free(); scrf.free(); srow.free(); wrow.free()
        lnst = self.ln_begin(f"l1a{p}")
        self.proj_to_prefm(p, ws_proj, attn32, None, [], "x2fm", lnst)
        for t in attn32:
            t.free()
        self.ln_finalize(lnst)
        return lnst

    def l1_ffn(self, p, ws_up, ws_down, ws_fc1, ws_fc2, ws_exp, gw_tile, lnst_in):
        d, E, ps = self.d, self.E, self.ps
        xfm = self.ln_apply(lnst_in, p, f"prefm{p}", "x3fm")
        hst = [self.bf.alloc(512, "hst") for _ in range(4)]
        ogs = [(i, min(i + 3, 64)) for i in range(0, 64, 3)]
        for (lo, hi) in ogs:
            accw = ws_up.stream_cols(lo * 128, hi * 128)
            banks = ps.fgroup((hi - lo) * 2)
            for cc in range(32):
                for j, io in enumerate(range(lo, hi)):
                    lap, lkeys = accw(cc, j * 128, (j + 1) * 128)
                    for th in range(2):
                        pk, pap = banks[j * 2 + th]
                        E("tensor",
                          (lambda e, o=pap, l=lap, r=xfm[cc].ap(th * 512, (th + 1) * 512),
                           s=(cc == 0), t=(cc == 31): e.matmul(o, l, r, start=s, stop=t)),
                          reads=lkeys + xfm[cc].keys(th * 512, (th + 1) * 512),
                          writes=[pk], tag=f"up{io}{th}")
            for j, io in enumerate(range(lo, hi)):
                for th in range(2):
                    pk, pap = banks[j * 2 + th]
                    hs = hst[(io * 2 + th) % 4]
                    E("scalar", (lambda e, o=hs.ap(0, 512), i=pap: e.activation(o, i, AF.Relu)),
                      reads=[pk], writes=hs.keys(0, 512), tag=f"uprl{io}{th}")
                    E("vector", (lambda e, o=hs.ap(0, 512), i=hs.ap(0, 512): e.tensor_mul(o, i, i)),
                      reads=hs.keys(0, 512), writes=hs.keys(0, 512), tag=f"upsq{io}{th}")
                    dst = d[f"hdr{p}"][io * 128:(io + 1) * 128, th * 512:(th + 1) * 512]
                    E("sync", (lambda e, a=hs.ap(0, 512), o=dst: e.dma_start(out=o, in_=a)),
                      reads=hs.keys(0, 512), writes=[("hdr", p, io, th)], dma=self.st(),
                      tag=f"upst{io}{th}")
        for t in hst:
            t.free()

        rwrow = self.f32a.alloc(1024, "rwrow")
        rwb = [self.f32a.alloc(512, f"rwb{th}") for th in range(2)]
        gl = self.f32a.alloc(512, "gl")
        glk = gl.keys()
        for tt in range(8):
            pk, pap = ps.f()
            for cc in range(32):
                E("tensor",
                  (lambda e, o=pap[:, 0:8], l=xfm[cc].ap(tt * 128, (tt + 1) * 128),
                   r=gw_tile.ap(cc * 8, (cc + 1) * 8), s=(cc == 0), t=(cc == 31):
                   e.matmul(o, l, r, start=s, stop=t)),
                  reads=gw_tile.keys() + xfm[cc].keys(tt * 128, (tt + 1) * 128),
                  writes=[pk], tag=f"gt{tt}")
            E("scalar", (lambda e, o=gl.ap(0, 8), i=pap[:, 0:8]: e.copy(o, i)),
              reads=[pk], writes=glk, tag=f"gtev{tt}")
            E("vector", (lambda e, o=gl.ap(16, 17), i=gl.ap(0, 8): e.reduce_max(o, i, axis=AX.X)),
              reads=glk, writes=glk, tag="gm1")
            E("vector", (lambda e, o=gl.ap(8, 16), i=gl.ap(0, 8), s=gl.ap(16, 17):
                         e.tensor_scalar(o, i, s, None, ALU.is_equal)),
              reads=glk, writes=glk, tag="geq")
            E("vector", (lambda e, o=gl.ap(8, 16), i=gl.ap(8, 16), t2=gl.ap(0, 8):
                         e.scalar_tensor_tensor(o, i, -1e30, t2, ALU.mult, ALU.add)),
              reads=glk, writes=glk, tag="gmsk")
            E("vector", (lambda e, o=gl.ap(17, 18), i=gl.ap(8, 16): e.reduce_max(o, i, axis=AX.X)),
              reads=glk, writes=glk, tag="gm2")
            E("vector", (lambda e, o=gl.ap(18, 19), a=gl.ap(16, 17), b=gl.ap(17, 18):
                         e.tensor_add(o, a, b)),
              reads=glk, writes=glk, tag="grw")
            pkT, papT = ps.f()
            E("tensor", (lambda e, o=papT[0:1, 0:128], i=gl.ap(18, 19),
                         idn=self.id_f32.ap(0, 128): e.transpose(o, i, idn)),
              reads=glk + self.id_f32.keys(0, 128), writes=[pkT], tag=f"gtr{tt}")
            E("scalar", (lambda e, o=rwrow.ap(tt * 128, (tt + 1) * 128, 0, 1), i=papT[0:1, 0:128]:
                         e.copy(o, i)),
              reads=[pkT], writes=rwrow.keys(), tag=f"grow{tt}")
        gl.free()
        rwrow_bf = self.bf.alloc(1024, "rwrow_bf")
        E("vector", (lambda e, o=rwrow_bf.ap(0, 1024, 0, 1), i=rwrow.ap(0, 1024, 0, 1):
                     e.tensor_copy(o, i)),
          reads=rwrow.keys(), writes=rwrow_bf.keys(), tag="growbf")
        for th in range(2):
            pkB, papB = ps.f()
            E("tensor", (lambda e, o=papB, l=self.ones_row.ap(0, 128, 0, 1),
                         r=rwrow_bf.ap(th * 512, (th + 1) * 512, 0, 1):
                         e.matmul(o, l, r, start=True, stop=True)),
              reads=self.ones_row.keys(0, 128) + rwrow_bf.keys(), writes=[pkB], tag=f"rwb{th}")
            E("scalar", (lambda e, o=rwb[th].ap(0, 512), i=papB: e.copy(o, i)),
              reads=[pkB], writes=rwb[th].keys(), tag=f"rwbe{th}")
        xlat = [self.bf.alloc(1024, f"xlat{lc}") for lc in range(16)]
        for (lo, hi) in [(i, min(i + 3, 16)) for i in range(0, 16, 3)]:
            accw = ws_fc1.stream_cols(lo * 128, hi * 128)
            banks = ps.fgroup((hi - lo) * 2)
            for cc in range(32):
                for j, lc in enumerate(range(lo, hi)):
                    lap, lkeys = accw(cc, j * 128, (j + 1) * 128)
                    for th in range(2):
                        pk, pap = banks[j * 2 + th]
                        E("tensor",
                          (lambda e, o=pap, l=lap, r=xfm[cc].ap(th * 512, (th + 1) * 512),
                           s=(cc == 0), t=(cc == 31): e.matmul(o, l, r, start=s, stop=t)),
                          reads=lkeys + xfm[cc].keys(th * 512, (th + 1) * 512),
                          writes=[pk], tag=f"f1{lc}{th}")
            for j, lc in enumerate(range(lo, hi)):
                for th in range(2):
                    pk, pap = banks[j * 2 + th]
                    E("scalar", (lambda e, o=xlat[lc].ap(th * 512, (th + 1) * 512), i=pap:
                                 e.copy(o, i)),
                      reads=[pk], writes=xlat[lc].keys(th * 512, (th + 1) * 512), tag=f"f1e{lc}{th}")
        for t in xfm:
            t.free()
        moe = [self.bf.alloc(1024, f"moe{mc}") for mc in range(16)]
        for (lo, hi) in [(i, min(i + 3, 16)) for i in range(0, 16, 3)]:
            accw = ws_exp.stream_cols(lo * 128, hi * 128)
            banks = ps.fgroup((hi - lo) * 2)
            for lc in range(16):
                for j, mc in enumerate(range(lo, hi)):
                    lap, lkeys = accw(lc, j * 128, (j + 1) * 128)
                    for th in range(2):
                        pk, pap = banks[j * 2 + th]
                        E("tensor",
                          (lambda e, o=pap, l=lap, r=xlat[lc].ap(th * 512, (th + 1) * 512),
                           s=(lc == 0), t=(lc == 15): e.matmul(o, l, r, start=s, stop=t)),
                          reads=lkeys + xlat[lc].keys(th * 512, (th + 1) * 512),
                          writes=[pk], tag=f"ex{mc}{th}")
            for j, mc in enumerate(range(lo, hi)):
                for th in range(2):
                    pk, pap = banks[j * 2 + th]
                    E("vector", (lambda e, o=moe[mc].ap(th * 512, (th + 1) * 512), i=pap,
                                 r=rwb[th].ap(0, 512): e.tensor_mul(o, i, r)),
                      reads=[pk] + rwb[th].keys(), writes=moe[mc].keys(th * 512, (th + 1) * 512),
                      tag=f"exe{mc}{th}")
        for t in xlat:
            t.free()
        rst = [self.bf.alloc(512, "rst") for _ in range(4)]
        for (lo, hi) in [(i, min(i + 3, 32)) for i in range(0, 32, 3)]:
            accw = ws_fc2.stream_cols(lo * 128, hi * 128)
            banks = ps.fgroup((hi - lo) * 2)
            for lc in range(16):
                for j, hc in enumerate(range(lo, hi)):
                    lap, lkeys = accw(lc, j * 128, (j + 1) * 128)
                    for th in range(2):
                        pk, pap = banks[j * 2 + th]
                        E("tensor",
                          (lambda e, o=pap, l=lap, r=moe[lc].ap(th * 512, (th + 1) * 512),
                           s=(lc == 0), t=(lc == 15): e.matmul(o, l, r, start=s, stop=t)),
                          reads=lkeys + moe[lc].keys(th * 512, (th + 1) * 512),
                          writes=[pk], tag=f"f2{hc}{th}")
            for j, hc in enumerate(range(lo, hi)):
                for th in range(2):
                    pk, pap = banks[j * 2 + th]
                    r = rst[(hc * 2 + th) % 4]
                    E("scalar", (lambda e, o=r.ap(0, 512), i=pap: e.copy(o, i)),
                      reads=[pk], writes=r.keys(0, 512), tag=f"f2e{hc}{th}")
                    dst = d[f"rtd{p}"][hc * 128:(hc + 1) * 128, th * 512:(th + 1) * 512]
                    E("scalar", (lambda e, a=r.ap(0, 512), o=dst: e.dma_start(out=o, in_=a)),
                      reads=r.keys(0, 512), writes=[("rtd", p, hc, th)], dma=self.st(),
                      tag=f"f2st{hc}{th}")
        for t in moe + rst:
            t.free()
        lnst = self.ln_begin(f"l1f{p}")
        hrd = [self.bf.alloc(1024, "hrd") for _ in range(3)]
        rrd = [self.bf.alloc(512, "rrd") for _ in range(4)]
        prest = [self.bf.alloc(512, "prest") for _ in range(4)]
        for og, (lo, hi) in enumerate([(i, min(i + 3, 32)) for i in range(0, 32, 3)]):
            accw = ws_down.stream_cols(lo * 128, hi * 128)
            banks = ps.fgroup((hi - lo) * 2)
            for ic in range(64):
                hb = hrd[ic % 3]
                src_ = d[f"hdr{p}"][ic * 128:(ic + 1) * 128, :]
                E("sync", (lambda e, a=hb.ap(0, 1024), s=src_: e.dma_start(out=a, in_=s)),
                  reads=[("hdr", p, ic, 0), ("hdr", p, ic, 1)], writes=hb.keys(0, 1024),
                  dma=("hrd", (og * 64 + ic) % 3), tag=f"hld{og}{ic}")
                for j, oc in enumerate(range(lo, hi)):
                    lap, lkeys = accw(ic, j * 128, (j + 1) * 128)
                    for th in range(2):
                        pk, pap = banks[j * 2 + th]
                        E("tensor",
                          (lambda e, o=pap, l=lap, r=hb.ap(th * 512, (th + 1) * 512),
                           s=(ic == 0), t=(ic == 63): e.matmul(o, l, r, start=s, stop=t)),
                          reads=lkeys + hb.keys(th * 512, (th + 1) * 512), writes=[pk],
                          tag=f"dn{oc}{th}")
            for j, oc in enumerate(range(lo, hi)):
                for th in range(2):
                    pk, pap = banks[j * 2 + th]
                    rr = rrd[(oc * 2 + th) % 4]
                    src_ = d[f"rtd{p}"][oc * 128:(oc + 1) * 128, th * 512:(th + 1) * 512]
                    E("sync", (lambda e, a=rr.ap(0, 512), s=src_: e.dma_start(out=a, in_=s)),
                      reads=[("rtd", p, oc, th)], writes=rr.keys(0, 512),
                      dma=("res", (oc * 2 + th) % 4), tag=f"rrd{oc}{th}")
                    pr = prest[(oc * 2 + th) % 4]
                    E("vector", (lambda e, o=pr.ap(0, 512), i=pap,
                                 r=rr.ap(0, 512): e.tensor_add(o, i, r)),
                      reads=[pk] + rr.keys(0, 512),
                      writes=pr.keys(0, 512), tag=f"dne{oc}{th}")
                    self.ln_chunk(lnst, pr.ap(0, 512), pr.keys(0, 512), th)
                    dst = d[f"prefm{p}"][oc * 128:(oc + 1) * 128, th * 512:(th + 1) * 512]
                    E("sync", (lambda e, a=pr.ap(0, 512), o=dst: e.dma_start(out=o, in_=a)),
                      reads=pr.keys(0, 512), writes=[("prefm", p, oc, th)], dma=self.st(),
                      tag=f"dnst{oc}{th}")
        for t in hrd + rrd + prest + rwb + [rwrow, rwrow_bf]:
            t.free()

        self.ln_finalize(lnst)
        pre_keys = [("prefm", p, oc, th) for oc in range(32) for th in range(2)]
        pre2t = [self.bf.alloc(1024, f"pre2_{cc}") for cc in range(32)]
        xin2 = [self.bf.alloc(1024, "epx") for _ in range(3)]
        x3r = [self.bf.alloc(1024, "epx3") for _ in range(3)]
        za = [self.f32a.alloc(1024, "epza") for _ in range(2)]
        zbf = [self.bf.alloc(1024, "epzb") for _ in range(2)]
        for cc in range(32):
            t = xin2[cc % 3]
            E("sync", (lambda e, a=t.ap(), s=d[f"prefm{p}"][cc * 128:(cc + 1) * 128, :]:
                       e.dma_start(out=a, in_=s)),
              reads=pre_keys, writes=t.keys(), dma=("lnld", cc % 3), tag=f"epld{cc}")
            x3 = x3r[cc % 3]
            src_ = d["x3fm"][cc * 128:(cc + 1) * 128, p * 1024:(p + 1) * 1024]
            E("sync", (lambda e, a=x3.ap(), s=src_: e.dma_start(out=a, in_=s)),
              reads=[("x3fm", p, cc)], writes=x3.keys(), dma=("res", cc % 3), tag=f"epx3{cc}")
            z = za[cc % 2]
            E("vector", (lambda e, o=z.ap(), a=t.ap(), b=lnst["rbf"].ap(0, 1024):
                         e.tensor_mul(o, a, b)),
              reads=t.keys() + lnst["rbf"].keys(), writes=z.keys(), tag=f"ep1_{cc}")
            E("vector", (lambda e, o=z.ap(), a=z.ap(), b=lnst["nbf"].ap(0, 1024):
                         e.tensor_add(o, a, b)),
              reads=z.keys() + lnst["nbf"].keys(), writes=z.keys(), tag=f"ep2_{cc}")
            zb = zbf[cc % 2]
            E("vector", (lambda e, o=zb.ap(), i=z.ap(), g=self.gam_t.ap(cc, cc + 1),
                         b=self.bet_t.ap(cc, cc + 1):
                         e.tensor_scalar(o, i, g, b, ALU.mult, ALU.add)),
              reads=z.keys() + self.gam_t.keys(0, 32) + self.bet_t.keys(0, 32),
              writes=zb.keys(), tag=f"ep3_{cc}")
            pre2 = pre2t[cc]
            E("vector", (lambda e, o=pre2.ap(), a=zb.ap(), b=x3.ap(): e.tensor_add(o, a, b)),
              reads=zb.keys() + x3.keys(), writes=pre2.keys(), tag=f"ep4_{cc}")
        for t in xin2 + x3r + za + zbf:
            t.free()
        lnst["rbf"].free()
        lnst["nbf"].free()
        pretm = [self.bf.alloc(4096, "pretm") for _ in range(2)]
        ostg = [self.f32a.alloc(1024, "ostg") for _ in range(2)]
        stat = self.f32a.alloc(512, "stat2")
        sk = stat.keys()
        for tt in range(8):
            t = pretm[tt % 2]
            for ccg in range(8):
                pkT, papT = ps.b()
                for j in range(4):
                    cc = ccg * 4 + j
                    E("tensor", (lambda e, o=papT[:, j * 128:(j + 1) * 128],
                                 i=pre2t[cc].ap(tt * 128, (tt + 1) * 128),
                                 idn=self.id_bf.ap(0, 128): e.transpose(o, i, idn)),
                      reads=pre2t[cc].keys(tt * 128, (tt + 1) * 128) + self.id_bf.keys(0, 128),
                      writes=[pkT], tag=f"eptr{tt}{ccg}")
                E("scalar", (lambda e, o=t.ap(ccg * 512, (ccg + 1) * 512), i=papT[:, 0:512]:
                             e.copy(o, i)),
                  reads=[pkT], writes=t.keys(ccg * 512, (ccg + 1) * 512), tag=f"eptre{tt}{ccg}")
            self._ln_stats(t, stat, sk, 50, 51)
            b_, q0 = tt // 4, (tt % 4) * 128
            orow = b_ * 1024 + p * 512 + q0
            for ch in range(4):
                o = ostg[ch % 2]
                E("vector", (lambda e, o_=o.ap(), i=t.ap(ch * 1024, (ch + 1) * 1024),
                             r=stat.ap(50, 51), m=stat.ap(51, 52):
                             e.tensor_scalar(o_, i, r, m, ALU.mult, ALU.add)),
                  reads=t.keys() + sk, writes=o.keys(), tag=f"eout{tt}{ch}")
                dst = d["out"][orow:orow + 128, ch * 1024:(ch + 1) * 1024]
                E("sync", (lambda e, a=o.ap(), o_=dst: e.dma_start(out=o_, in_=a)),
                  reads=o.keys(), writes=[("out", p, tt, ch)], dma=self.st(), tag=f"est{tt}{ch}")
        for t in pretm + ostg + [stat] + pre2t:
            t.free()


def build_program(part, taps=()):
    mk = MK(part, taps)
    d = mk.d
    if part in ("s1", "all"):
        ws_qkv = WStream(mk, "wqkv", d["w_l0qkv"])
        for p in range(2):
            mk.s1(p, ws_qkv)
            if part == "all":
                mk.allgather(p)
    if part == "all":
        mk.bqkv.free(); mk.bq_s.free(); mk.vbias.free()
    if part in ("rest", "all"):
        rt = mk.bf.alloc(512, "rt")
        at = mk.bf.alloc(2048, "at")
        gw = mk.bf.alloc(512, "gw")
        for tile, name, width in ((rt, "r_t", 16), (at, "a_t", 64), (gw, "gate_w", 8)):
            src = d[name].rearrange("(cc p) e -> p cc e", p=128)
            dst = tile.ap(0, 32 * width).rearrange("p (cc e) -> p cc e", e=width)
            mk.E("gpsimd", (lambda e, a=dst, s=src: e.dma_start(out=a, in_=s)),
                 writes=tile.keys(0, 32 * width), dma=("cst", None), tag=f"ld_{name}")
        ws_o = WStream(mk, "wo", d["w_l0o"])
        ws_base = WStream(mk, "wbase", d["w_l0base_t"])
        ws_bflat = WStream(mk, "wbflat", d["b_flat"])
        ws_l1qkv = WStream(mk, "wl1qkv", d["w_l1qkv_t"])
        ws_l1proj = WStream(mk, "wl1proj", d["w_l1proj_t"])
        ws_up = WStream(mk, "wup", d["up_w"])
        ws_down = WStream(mk, "wdown", d["down_w"])
        ws_fc1 = WStream(mk, "wfc1", d["fc1_w"])
        ws_fc2 = WStream(mk, "wfc2", d["fc2_w"])
        ws_exp = WStream(mk, "wexp", d["expert_w"])
        ln1 = [mk.l0_attn(p, ws_o) for p in range(2)]
        ln2 = [mk.l0_ffn(p, ws_base, ws_bflat, rt, at, ln1[p]) for p in range(2)]
        for p in range(2):
            mk.tap("x1fm", [("x1fm", p, cc) for cc in range(32)])
        mk.tap("whtm0", [("whtm", 0, tt) for tt in range(8)])
        mk.tap("prefm0", [("prefm", 0, oc, th) for oc in range(32) for th in range(2)])
        ln3 = [mk.l1_attn(p, ws_l1qkv, ws_l1proj, ln2[p]) for p in range(2)]
        for p in range(2):
            mk.tap("x2fm", [("x2fm", p, cc) for cc in range(32)])
        for p in range(2):
            mk.l1_ffn(p, ws_up, ws_down, ws_fc1, ws_fc2, ws_exp, gw, ln3[p])
            mk.tap("x3fm", [("x3fm", p, cc) for cc in range(32)])
    mk.plan.build()
    return mk


_progs = {}


def _get_prog(part, taps=()):
    key = (part, tuple(sorted(taps)))
    if key not in _progs:
        _progs[key] = build_program(part, taps)
    return _progs[key]


REST_W = ("w_l0o", "b_l0o", "w_l0base_t", "b_l0base", "a_t", "b_flat", "r_t",
          "w_l1qkv_t", "w_l1proj_t", "gate_w", "up_w", "down_w", "fc1_w",
          "fc2_w", "expert_w", "ln_g", "ln_b")


def host_layout(inputs):
    C = np.ascontiguousarray
    w = {}
    w["w_l0qkv"] = C(inputs["L0_attn_Wqkv"])
    w["b_l0qkv"] = C(inputs["L0_attn_bqkv"]).reshape(1, 3072)
    w["w_l0o"] = C(inputs["L0_attn_Wo"])
    w["b_l0o"] = C(inputs["L0_attn_bo"]).reshape(1, 4096)
    w["w_l0base_t"] = C(inputs["L0_ffn_base_w"].T)
    w["b_l0base"] = C(inputs["L0_ffn_base_b"]).reshape(1, 4096)
    w["a_t"] = C(inputs["L0_ffn_A"].T)
    w["b_flat"] = C(np.transpose(inputs["L0_ffn_B"], (0, 2, 1)).reshape(1024, 4096))
    w["r_t"] = C(inputs["L0_ffn_router_w"].T)
    w["w_l1qkv_t"] = C(inputs["L1_attn_weight_qkv"].T)
    w["w_l1proj_t"] = C(inputs["L1_attn_weight_proj"].T)
    w["gate_w"] = C(inputs["L1_ffn_gate_w"])
    w["up_w"] = C(inputs["L1_ffn_up_w"])
    w["down_w"] = C(inputs["L1_ffn_down_w"])
    w["fc1_w"] = C(inputs["L1_ffn_fc1_w"])
    w["fc2_w"] = C(inputs["L1_ffn_fc2_w"])
    w["expert_w"] = C(inputs["L1_ffn_expert_w"])
    w["ln_g"] = C(inputs["L1_ffn_ln_gamma"]).reshape(1, 4096)
    w["ln_b"] = C(inputs["L1_ffn_ln_beta"]).reshape(1, 4096)
    return w


def shard_x(x):
    return [np.ascontiguousarray(x[:, r * 1024:(r + 1) * 1024, :].reshape(2048, DMODEL))
            for r in range(NCORES)]


FUSED = True


def kernel(**inputs):
    w = host_layout(inputs)
    xs = shard_x(np.asarray(inputs["L0_attn_x"]))
    if FUSED:
        mk = _get_prog("all")
        ins = [{**{n: w[n] for n in REST_W}, "x0": xs[r],
                "w_l0qkv": w["w_l0qkv"], "b_l0qkv": w["b_l0qkv"]}
               for r in range(NCORES)]
        res = run_bass_kernel_spmd(mk.nc, ins, core_ids=list(range(NCORES)))
        out = np.empty((2, SEQ, DMODEL), np.float32)
        for r in range(NCORES):
            out[:, r * 1024:(r + 1) * 1024, :] = res.results[r]["out"].reshape(2, 1024, DMODEL)
        return out
    mk1 = _get_prog("s1")
    in1 = [{"x0": xs[r], "w_l0qkv": w["w_l0qkv"], "b_l0qkv": w["b_l0qkv"]}
           for r in range(NCORES)]
    r1 = run_bass_kernel_spmd(mk1.nc, in1, core_ids=list(range(NCORES)))
    ksp = {f"kout{p}": np.concatenate([r1.results[r]["kin"][p * 2048:(p + 1) * 2048]
                                        for r in range(NCORES)], axis=0) for p in range(2)}
    vsp = {f"vout{p}": np.concatenate([r1.results[r]["vin"][p * 1024:(p + 1) * 1024]
                                       for r in range(NCORES)], axis=0) for p in range(2)}
    mk2 = _get_prog("rest")
    in2 = [{**{n: w[n] for n in REST_W}, **ksp, **vsp,
            "qdr": r1.results[r]["qdr"], "x0fm": r1.results[r]["x0fm"]}
           for r in range(NCORES)]
    r2 = run_bass_kernel_spmd(mk2.nc, in2, core_ids=list(range(NCORES)))
    out = np.empty((2, SEQ, DMODEL), np.float32)
    for r in range(NCORES):
        out[:, r * 1024:(r + 1) * 1024, :] = r2.results[r]["out"].reshape(2, 1024, DMODEL)
    return out
